# version 11
# baseline (speedup 1.0000x reference)
import math
import contextlib
import numpy as np
import ml_dtypes
import concourse.bass as bass
import concourse.mybir as mybir
from concourse.bass_utils import run_bass_kernel_spmd

F32 = mybir.dt.float32
BF16 = mybir.dt.bfloat16
I32 = mybir.dt.int32
AF = mybir.ActivationFunctionType
ALU = mybir.AluOpType
AX = mybir.AxisListType

B, S, E, H, D = 4, 2048, 2048, 16, 128
T = 1024
NCH = 16
LN2C = float(np.log(2.0))
INV_LN2 = float(1.0 / np.log(2.0))
SC_L0 = float(1.0 / np.sqrt(128.0))
SC_L1 = float(1.0 / np.sqrt(2048.0))
RDIV = float(1.0 / 2048.0)


def c0_of(j, k):
    if k < 8 * j:
        return 0
    return 128 * max(0, int(math.ceil((k - 8 * j - 1) / 2.0)))


def kcol(k):
    return (k % 2) * 1024 + (k // 2) * 128


def vtslot(k):
    return (k % 2) * 8 + (k // 2)


class Ch:
    def __init__(self, stack, nc, name, step=1):
        self.h = stack.enter_context(nc.semaphore(name))
        self.n = 0
        self.step = step


class KB:
    def __init__(self, nc):
        self.q = {k: [] for k in "tvsgy"}

    def on(self, ek, fn):
        self.q[ek].append(fn)

    def wait(self, ek, ch, n=None):
        n = ch.n if n is None else n
        if n > 0:
            self.on(ek, lambda e, h=ch.h, m=n: e.wait_ge(h, m))

    def tick(self, ek, ch, fn):
        ch.n += ch.step
        self.on(ek, lambda e, f=fn, h=ch.h, s=ch.step: f(e).then_inc(h, s))


def build(stage="full"):
    nc = bass.Bass("TRN2", target_bir_lowering=False, debug=False)

    def din(name, shape, dtype=F32):
        return nc.dram_tensor(name, list(shape), dtype, kind="ExternalInput").ap()

    x0 = din("x0", (E, T))
    wq0 = din("wq0", (H, 128, E)); wk0 = din("wk0", (H, 128, E))
    wv0 = din("wv0", (H, 128, E)); wo0 = din("wo0", (NCH, 128, E))
    bq_c = din("bq_c", (128, NCH)); bk_c = din("bk_c", (128, NCH))
    bv_c = din("bv_c", (128, NCH)); bo_c = din("bo_c", (128, NCH))
    b1_c = din("b1_c", (128, 64)); b2_c = din("b2_c", (128, NCH))
    rw = din("rw", (128, NCH * 8))
    w1 = din("w1", (64, 128, E))
    w2 = din("w2", (NCH, 128, 8192))
    wq1 = din("wq1", (H, 128, E)); wk1 = din("wk1", (H, 128, E))
    wv1 = din("wv1", (H, 128, E)); wo1 = din("wo1", (NCH, 128, E))
    wf1 = din("wf1", (NCH, 128, E))
    ropec = din("ropec", (128, T), BF16); ropes = din("ropes", (128, T), BF16)
    p128 = din("p128", (128, 128), BF16)
    identb = din("identb", (128, 128), BF16); identf = din("identf", (128, 128))
    ones128 = din("ones128", (128, 128), BF16)
    onesrow = din("onesrow", (1, 128), BF16); onesrow_f = din("onesrow_f", (1, 128))
    m0t = din("m0t", (128, 16 * 128), BF16); m1t = din("m1t", (128, 16 * 128), BF16)
    u128 = din("u128", (128, 128), BF16)
    mbig = din("mbig", (128, 16), BF16)
    c1024 = din("c1024", (1, 128), BF16)

    y = nc.dram_tensor("y", [E, T], F32, kind="ExternalOutput").ap()

    cc0K_in = nc.dram_tensor("cc0K_in", [E, T], BF16, kind="Internal").ap()
    cc0V_in = nc.dram_tensor("cc0V_in", [E, T], BF16, kind="Internal").ap()
    cc0K_out = nc.dram_tensor("cc0K_out", [2 * E, T], BF16, kind="Internal").ap()
    cc0V_out = nc.dram_tensor("cc0V_out", [2 * E, T], BF16, kind="Internal").ap()
    cnt_in = nc.dram_tensor("cnt_in", [1, 128], F32, kind="Internal").ap()
    cnt_out = nc.dram_tensor("cnt_out", [8, 128], F32, kind="Internal").ap()
    cc1K_in = nc.dram_tensor("cc1K_in", [E, T], BF16, kind="Internal").ap()
    cc1V_in = nc.dram_tensor("cc1V_in", [E, T], BF16, kind="Internal").ap()
    cc1K_out = nc.dram_tensor("cc1K_out", [2 * E, T], BF16, kind="Internal").ap()
    cc1V_out = nc.dram_tensor("cc1V_out", [2 * E, T], BF16, kind="Internal").ap()
    wdq_d = nc.dram_tensor("wdq_d", [NCH, 128, E], BF16, kind="Internal").ap()

    stack = contextlib.ExitStack()
    with stack:
        sb = lambda name, shape, dtype: stack.enter_context(
            nc.sbuf_tensor(name, list(shape), dtype))

        xA = sb("xA", (128, NCH * 1024), BF16)
        xB = sb("xB", (128, NCH * 1024), BF16)
        arenaA = sb("arenaA", (128, 32768), BF16)
        arenaB = sb("arenaB", (128, 8192), BF16)
        arenaA_f = arenaA.bitcast(F32)
        arenaB_f = arenaB.bitcast(F32)
        arenaB_i = arenaB.bitcast(I32)

        kbuf = [sb(f"kbuf{i}", (128, 2048), BF16) for i in range(2)]
        qh = sb("qh", (128, 1024), BF16)
        WR = 3
        wb = [sb(f"wb{i}", (128, 2048), BF16) for i in range(WR)]
        wb2 = wb
        wfb = sb("wfb", (128, 1024), F32)
        gb_sb = kbuf[0].bitcast(F32)
        grow = kbuf[1].bitcast(F32)
        ropec_sb = sb("ropec_sb", (128, T), BF16)
        ropes_sb = sb("ropes_sb", (128, T), BF16)
        p128_sb = sb("p128_sb", (128, 128), BF16)
        identb_sb = sb("identb_sb", (128, 128), BF16)
        identf_sb = sb("identf_sb", (128, 128), F32)
        ones128_sb = sb("ones128_sb", (128, 128), BF16)
        onesrow_sb = sb("onesrow_sb", (1, 128), BF16)
        onesrow_f_sb = sb("onesrow_f_sb", (1, 128), F32)
        allones = sb("allones", (128, 512), BF16)
        m_sb = sb("m_sb", (128, 16 * 128), BF16)
        u128_sb = sb("u128_sb", (128, 128), BF16)
        mbig_sb = sb("mbig_sb", (128, 16), BF16)
        c1024_sb = sb("c1024_sb", (1, 128), BF16)
        bq_sb = sb("bq_sb", (128, NCH), F32)
        bk_sb = sb("bk_sb", (128, NCH), F32)
        bv_sb = sb("bv_sb", (128, NCH), F32)
        bo_sb = sb("bo_sb", (128, NCH), F32)
        b1_sb = sb("b1_sb", (128, 64), F32)
        b2_sb = sb("b2_sb", (128, NCH), F32)
        rw_sb = sb("rw_sb", (128, NCH * 8), BF16)
        wbf = [sb(f"wbf{i}", (128, 1024), BF16) for i in range(5)]
        wk = [sb(f"wkt{i}", (128, 512), F32) for i in range(4)]
        wk.append(wbf[0].bitcast(F32))
        wk.append(sb("wkt5", (128, 512), F32))
        epsT = sb("epsT", (128, 1), F32)
        moe_g = sb("moe_g", (128, 8), F32)
        moe_m1 = sb("moe_m1", (128, 8), F32)
        moe_m2 = sb("moe_m2", (128, 8), F32)
        moe_oh1 = sb("moe_oh1", (128, 8 * 8), BF16)
        moe_oh2 = sb("moe_oh2", (128, 8 * 8), BF16)
        mt8 = [sb(f"mt8_{i}", (128, 8), F32) for i in range(4)]
        cntrow = sb("cntrow", (1, 128), F32)
        cnt_sb1 = sb("cnt_sb1", (128, 8), BF16)
        cnt_sb2 = sb("cnt_sb2", (128, 8), BF16)
        offrow = sb("offrow", (1, 128), F32)
        moe_s1 = sb("moe_s1", (16, 128), F32)
        q_am = sb("q_am", (128, 16), F32)
        q_rc448 = sb("q_rc448", (128, 16), F32)
        q_s = sb("q_s", (128, 16), F32)
        wq_cm = sb("wq_cm", (128, 16), F32)
        wq_rows = sb("wq_rows", (1, 512), F32)
        wq_amrow = wq_rows[0:1, 0:256]
        wq_row2 = wq_rows[0:1, 256:512]
        _wbf1f = wbf[1].bitcast(F32)
        wq_rb448 = _wbf1f[:, 0:256]
        wq_s = _wbf1f[:, 256:512]
        wfb_i = wfb.bitcast(I32)

        pb = [stack.enter_context(nc.psum_tensor(f"pb{i}", [128, 512], F32))
              for i in range(8)]

        assert nc.sbuf_bytes_remaining >= 0, nc.sbuf_bytes_remaining
        assert nc.psum_banks_remaining >= 0

        kb = KB(nc)
        cC = Ch(stack, nc, "cC", 16)
        cX = Ch(stack, nc, "cX", 16)
        cW = [Ch(stack, nc, f"cW{i}", 16) for i in range(4)]
        cSg = [Ch(stack, nc, f"cSg{i}", 16) for i in range(2)]
        cK = [Ch(stack, nc, f"cK{i}", 16) for i in range(2)]
        cVT = Ch(stack, nc, "cVT", 16)
        cM = Ch(stack, nc, "cM", 16)
        cO = Ch(stack, nc, "cO", 16)
        cA = Ch(stack, nc, "cA", 16)
        cT = Ch(stack, nc, "cT", 1)
        cV = Ch(stack, nc, "cV", 1)
        cS = Ch(stack, nc, "cS", 1)
        cCC = Ch(stack, nc, "cCC", 1)

        def ydma(ch, out, in_, **kw):
            kb.tick("y", ch, lambda e, o=out, i=in_, k=kw: e.dma_start(out=o, in_=i, **k))

        def gdma(ch, out, in_, **kw):
            kb.tick("g", ch, lambda e, o=out, i=in_, k=kw: e.dma_start(out=o, in_=i, **k))

        def mm(out, lhsT, rhs, start, stop, tick=False):
            fn = lambda e, o=out, l=lhsT, r=rhs, s1=start, s2=stop: \
                e.matmul(o, l, r, start=s1, stop=s2, skip_group_check=True)
            if tick:
                kb.tick("t", cT, fn)
            else:
                kb.on("t", fn)

        def vop(fn, tick=True):
            kb.wait("v", cV, cV.n)
            kb.tick("v", cV, fn)

        def sop(fn, tick=True):
            kb.wait("s", cS, cS.n)
            kb.tick("s", cS, fn)

        for dst, src in [
            (ropec_sb, ropec), (ropes_sb, ropes), (identb_sb, identb),
            (identf_sb, identf), (ones128_sb, ones128), (onesrow_sb, onesrow),
            (onesrow_f_sb, onesrow_f), (p128_sb, p128),
            (u128_sb, u128), (mbig_sb, mbig),
            (c1024_sb, c1024), (bq_sb, bq_c), (bk_sb, bk_c), (bv_sb, bv_c),
            (bo_sb, bo_c), (b1_sb, b1_c), (b2_sb, b2_c),
        ]:
            ydma(cC, dst[:, :], src[:, :])
        n_const = cC.n
        gdma(cX, rw_sb[:, :], rw[:, :])
        for c in range(NCH):
            gdma(cX, xA[:, 1024 * c:1024 * (c + 1)], x0[128 * c:128 * (c + 1), :])
        vop(lambda e: e.memset(epsT[:, :], 1e-5))
        vop(lambda e: e.memset(allones[:, :], 1.0), tick=True)
        kb.wait("t", cC, n_const)
        kb.wait("v", cC, n_const)
        kb.wait("s", cC, n_const)
        kb.wait("t", cX)

        cur = lambda c, t=None: xA[:, 1024 * c + (0 if t is None else 512 * t):
                                   1024 * c + (1024 if t is None else 512 * (t + 1))]
        aox = lambda c, t=None: xB[:, 1024 * c + (0 if t is None else 512 * t):
                                   1024 * c + (1024 if t is None else 512 * (t + 1))]
        rr = lambda c, t=None: arenaA[:, 1024 * c + (0 if t is None else 512 * t):
                                      1024 * c + (1024 if t is None else 512 * (t + 1))]

        def proj_mms(bank, wslot, tt):
            for ec in range(NCH):
                mm(bank[:, :], wslot[:, 128 * ec:128 * (ec + 1)],
                   cur(ec, tt), start=(ec == 0), stop=(ec == NCH - 1),
                   tick=(ec == NCH - 1))

        def kv_phase(layer, do_vt=True, do_cc=True):
            wkw = wk0 if layer == 0 else wk1
            wvw = wv0 if layer == 0 else wv1
            ccKi = cc0K_in if layer == 0 else cc1K_in
            ccVi = cc0V_in if layer == 0 else cc1V_in
            ccK = cc0K_out if layer == 0 else cc1K_out
            ccV = cc0V_out if layer == 0 else cc1V_out
            steps = ([("k", h) for h in range(H)] + [("v", h) for h in range(H)])
            cons = {}
            dmad = {}
            entry_t = cT.n
            bank_free = {0: (cV, cV.n), 1: (cV, cV.n)}
            for i, (kind, h) in enumerate(steps):
                kb.wait("g", cT, cons[i - WR] if i >= WR else entry_t)
                gdma(cW[i % WR], wb[i % WR][:, :], (wkw if kind == "k" else wvw)[h][:, :])
                ldn = cW[i % WR].n
                stg = wbf[3 + (i % 2)]
                if i >= 2:
                    kb.wait("s", cSg[i % 2], dmad[i - 2])
                    kb.wait("v", cSg[i % 2], dmad[i - 2])
                for tt in range(2):
                    bank = pb[tt]
                    kb.wait("t", cW[i % WR], ldn)
                    ch_, n_ = bank_free[tt]
                    kb.wait("t", ch_, n_)
                    proj_mms(bank, wb[i % WR], tt)
                    mmn = cT.n
                    if layer == 0 and kind == "k":
                        bcol = bk_sb[:, h:h + 1]
                        kb.wait("s", cT, mmn)
                        kb.wait("s", cV, cV.n)
                        sop(lambda e, b=bank, t=tt, bc=bcol: e.activation(
                            wbf[0][:, 512 * t:512 * (t + 1)], b[:, :], AF.Identity,
                            bias=bc), tick=True)
                        kb.wait("v", cS, cS.n)
                        vop(lambda e, t=tt: e.tensor_mul(
                            wbf[1][:, 512 * t:512 * (t + 1)],
                            wbf[0][:, 512 * t:512 * (t + 1)],
                            ropec_sb[:, 512 * t:512 * (t + 1)]))
                        vop(lambda e, t=tt: e.tensor_mul(
                            wbf[2][:, 512 * t:512 * (t + 1)],
                            wbf[0][:, 512 * t:512 * (t + 1)],
                            ropes_sb[:, 512 * t:512 * (t + 1)]), tick=True)
                        kb.wait("t", cV, cV.n)
                        mm(pb[7][:, :], p128_sb[:, :],
                           wbf[2][:, 512 * tt:512 * (tt + 1)], True, True,
                           tick=True)
                        kb.wait("v", cT, cT.n)
                        vop(lambda e, t=tt, o=stg: e.tensor_add(
                            o[:, 512 * t:512 * (t + 1)],
                            wbf[1][:, 512 * t:512 * (t + 1)], pb[7][:, :]),
                            tick=True)
                        bank_free[tt] = (cS, cS.n)
                    else:
                        kb.wait("s", cT, mmn)
                        if layer == 0:
                            bcol = bv_sb[:, h:h + 1]
                            sop(lambda e, b=bank, o=stg, t=tt, bc=bcol:
                                e.activation(o[:, 512 * t:512 * (t + 1)], b[:, :],
                                             AF.Identity, bias=bc), tick=True)
                        else:
                            sop(lambda e, b=bank, o=stg, t=tt: e.copy(
                                o[:, 512 * t:512 * (t + 1)], b[:, :]), tick=True)
                        bank_free[tt] = (cS, cS.n)
                cons[i] = cT.n
                if layer == 0 and kind == "k":
                    kb.wait("y", cV, cV.n)
                else:
                    kb.wait("y", cS, cS.n)
                dst = ccKi if kind == "k" else ccVi
                ydma(cSg[i % 2], dst[128 * h:128 * (h + 1), :], stg[:, :])
                dmad[i] = cSg[i % 2].n
                if i == H - 1 and do_cc:
                    kb.wait("g", cSg[0])
                    kb.wait("g", cSg[1])
                    kb.tick("g", cCC, lambda e, ci=ccKi, co=ccK:
                            e.collective_compute(
                                "AllGather", ALU.bypass,
                                replica_groups=[[0, 1], [2, 3], [4, 5], [6, 7]],
                                ins=[ci], outs=[co]))
            agk = cCC.n
            kb.wait("g", cSg[0])
            kb.wait("g", cSg[1])
            if do_cc:
                kb.tick("g", cCC, lambda e, ci=ccVi, co=ccV:
                        e.collective_compute(
                            "AllGather", ALU.bypass,
                            replica_groups=[[0, 1], [2, 3], [4, 5], [6, 7]],
                            ins=[ci], outs=[co]))
            agv = cCC.n
            ydma(cM, m_sb[:, :], (m0t if layer == 0 else m1t)[:, :])
            kb.wait("y", cCC, agv)
            if do_vt:
                for r in range(2):
                    for m2 in range(8):
                        a = r * 8 + m2
                        src = ccV[E * r: E * (r + 1), 128 * m2:128 * (m2 + 1)]
                        ydma(cVT, arenaA[:, 2048 * a:2048 * (a + 1)], src,
                             transpose=True)
            return ccK, agk, cT.n

        def attn_phase(layer, cc_out_, agk, kvdone_t, side=None):
            wqw = wq0 if layer == 0 else wq1
            msk = m_sb
            AO = xB
            Pb = arenaB
            scale = SC_L0 if layer == 0 else SC_L1
            fillval = 0.0 if layer == 0 else 1.0
            kload = {}
            headdone = {}
            kb.wait("y", cCC, agk)
            kb.wait("t", cM)
            vt_waited = [False]
            for h in range(H):
                kb.wait("g", cT, kvdone_t if h < WR else headdone[h - WR])
                gdma(cW[h % WR], wb[h % WR][:, :], wqw[h][:, :])
                qldn = cW[h % WR].n
                if h >= 2:
                    kb.wait("y", cT, headdone[h - 2])
                for r in range(2):
                    ydma(cK[h % 2], kbuf[h % 2][:, 1024 * r:1024 * (r + 1)],
                         cc_out_[E * r + 128 * h: E * r + 128 * (h + 1), :])
                kload[h] = cK[h % 2].n
                for tt in range(2):
                    kb.wait("t", cW[h % WR], qldn)
                    kb.wait("t", cS, cS.n)
                    kb.wait("t", cV, cV.n)
                    proj_mms(pb[6], wb[h % WR], tt)
                    mmn = cT.n
                    kb.wait("s", cT, mmn)
                    if layer == 0:
                        kb.wait("s", cV, cV.n)
                        sop(lambda e, t=tt, h_=h: e.activation(
                            wbf[0][:, 512 * t:512 * (t + 1)], pb[6][:, :],
                            AF.Identity, bias=bq_sb[:, h_:h_ + 1]), tick=True)
                        kb.wait("v", cS, cS.n)
                        vop(lambda e, t=tt: e.tensor_mul(
                            wbf[1][:, 512 * t:512 * (t + 1)],
                            wbf[0][:, 512 * t:512 * (t + 1)],
                            ropec_sb[:, 512 * t:512 * (t + 1)]))
                        vop(lambda e, t=tt: e.tensor_mul(
                            wbf[2][:, 512 * t:512 * (t + 1)],
                            wbf[0][:, 512 * t:512 * (t + 1)],
                            ropes_sb[:, 512 * t:512 * (t + 1)]), tick=True)
                        kb.wait("t", cV, cV.n)
                        mm(pb[7][:, :], p128_sb[:, :],
                           wbf[2][:, 512 * tt:512 * (tt + 1)], True, True,
                           tick=True)
                        kb.wait("v", cT, cT.n)
                        vop(lambda e, t=tt: e.tensor_add(
                            qh[:, 512 * t:512 * (t + 1)],
                            wbf[1][:, 512 * t:512 * (t + 1)], pb[7][:, :]),
                            tick=True)
                    else:
                        sop(lambda e, t=tt: e.copy(
                            qh[:, 512 * t:512 * (t + 1)], pb[6][:, :]), tick=True)
                kb.wait("t", cV, cV.n)
                kb.wait("t", cS, cS.n)
                kb.wait("t", cK[h % 2], kload[h])
                exp_tick = {}
                for j in range(2):
                    kb.wait("v", cT, cT.n)
                    fk = [(k, c0_of(j, k)) for k in range(0, 8 * j + 8)
                          if c0_of(j, k) > 0]
                    for fi, (k, c0) in enumerate(fk):
                        vop(lambda e, k_=k, c_=c0: e.memset(
                            Pb[:, 512 * k_:512 * k_ + c_], fillval),
                            tick=(fi == len(fk) - 1))
                    kb.wait("s", cV, cV.n)
                    kb.wait("t", cV, cV.n)
                    for k in range(0, 8 * j + 8):
                        c0 = c0_of(j, k)
                        bank = pb[k % 2]
                        if k >= 2:
                            kb.wait("t", cS, exp_tick[k - 2])
                        has_mask = (layer == 0 and k >= 8 * j)
                        mm(bank[:, c0:512],
                           kbuf[h % 2][:, kcol(k):kcol(k) + 128],
                           qh[:, 512 * j + c0:512 * (j + 1)],
                           start=True, stop=not has_mask, tick=not has_mask)
                        if has_mask:
                            mm(bank[:, c0:c0 + 128], identb_sb[:, :],
                               msk[:, 128 * k:128 * (k + 1)], start=False,
                               stop=True, tick=True)
                        smn = cT.n
                        if layer == 0 or k < 8 * j:
                            kb.wait("s", cT, smn)
                            sop(lambda e, k_=k, c_=c0, b=bank: e.activation(
                                Pb[:, 512 * k_ + c_:512 * (k_ + 1)], b[:, c_:512],
                                AF.Exp, scale=scale), tick=True)
                        else:
                            kb.wait("v", cT, smn)
                            kb.wait("v", cS, cS.n)
                            vop(lambda e, k_=k, c_=c0, b=bank: e.tensor_mul(
                                wk[5][:, 0:128], b[:, c_:c_ + 128],
                                msk[:, 128 * k_:128 * (k_ + 1)]), tick=True)
                            kb.wait("s", cV, cV.n)
                            sop(lambda e, k_=k, c_=c0: e.activation(
                                Pb[:, 512 * k_ + c_:512 * k_ + c_ + 128],
                                wk[5][:, 0:128], AF.Exp, scale=scale),
                                tick=(c0 + 128 >= 512))
                            if c0 + 128 < 512:
                                sop(lambda e, k_=k, c_=c0, b=bank: e.activation(
                                    Pb[:, 512 * k_ + c_ + 128:512 * (k_ + 1)],
                                    b[:, c_ + 128:512], AF.Exp, scale=scale),
                                    tick=True)
                        exp_tick[k] = cS.n
                    kb.wait("t", cS, cS.n)
                    if not vt_waited[0]:
                        kb.wait("t", cVT)
                        vt_waited[0] = True
                    obank, zbank = pb[2 + j], pb[4 + j]
                    krange = list(range(0, 8 * j + 8)) if layer == 0 else (
                        list(range(16)) if j == 0 else list(range(16)))
                    zrange = set(range(0, 8 * j + 8))
                    first = True
                    for k in krange:
                        inwin = k in zrange
                        last = (k == krange[-1])
                        rhs = Pb[:, 512 * k:512 * (k + 1)] if inwin else allones[:, :]
                        vt = arenaA[:, 2048 * vtslot(k) + 128 * h:
                                    2048 * vtslot(k) + 128 * (h + 1)]
                        mm(obank[:, :], vt, rhs, start=first, stop=last,
                           tick=(last and not (inwin and True)))
                        if inwin:
                            mm(zbank[:, :], ones128_sb[:, :], rhs,
                               start=first, stop=last, tick=last)
                        if layer == 1 and first and j == 0:
                            mm(zbank[:, :], c1024_sb[:, :], allones[0:1, 0:512],
                               start=False, stop=False)
                        first = False
                    pvn = cT.n
                    kb.wait("s", cT, pvn)
                    sop(lambda e, z=zbank: e.activation(wk[0][:, :], z[:, :], AF.Ln),
                        tick=True)
                    kb.wait("s", cS, cS.n)
                    sop(lambda e: e.activation(wk[1][:, :], wk[0][:, :], AF.Exp,
                                               scale=-1.0), tick=True)
                    kb.wait("v", cS, cS.n)
                    vop(lambda e, o=obank, h_=h, j_=j: e.tensor_mul(
                        AO[:, 1024 * h_ + 512 * j_:1024 * h_ + 512 * (j_ + 1)],
                        o[:, :], wk[1][:, :]), tick=True)
                headdone[h] = cT.n
                if side is not None and h < len(side):
                    for fn in side[h]:
                        fn()
            kb.wait("t", cV, cV.n)
            return AO

        def wo_phase(layer, AO):
            wow = wo0 if layer == 0 else wo1
            cons = {}
            entry_t = cT.n
            evac = {0: (cV, cV.n), 1: (cV, cV.n)}
            for fh in range(NCH):
                kb.wait("g", cT, cons[fh - WR] if fh >= WR else entry_t)
                gdma(cW[fh % WR], wb2[fh % WR][:, :], wow[fh][:, :])
                ldn = cW[fh % WR].n
                for tt in range(2):
                    bank = pb[tt]
                    kb.wait("t", cW[fh % WR], ldn)
                    ch_, n_ = evac[tt]
                    kb.wait("t", ch_, n_)
                    for ec in range(NCH):
                        mm(bank[:, :], wb2[fh % WR][:, 128 * ec:128 * (ec + 1)],
                           AO[:, 1024 * ec + 512 * tt:1024 * ec + 512 * (tt + 1)],
                           start=(ec == 0), stop=(ec == NCH - 1),
                           tick=(ec == NCH - 1))
                    kb.wait("v", cT, cT.n)
                    if layer == 0:
                        vop(lambda e, b=bank, f=fh, t=tt: e.scalar_tensor_tensor(
                            rr(f, t), b[:, :], bo_sb[:, f:f + 1], cur(f, t),
                            ALU.add, ALU.add), tick=True)
                    else:
                        vop(lambda e, b=bank, f=fh, t=tt: e.tensor_add(
                            rr(f, t), b[:, :], cur(f, t)), tick=True)
                    evac[tt] = (cV, cV.n)
                cons[fh] = cT.n

        def ln_phase(dst, src=None, desc=False):
            if src is None:
                src = rr
            kb.wait("t", cV, cV.n)
            kb.wait("s", cV, cV.n)
            prev_tick = cT.n
            for c in range(NCH):
                for tt in range(2):
                    kb.wait("s", cT, prev_tick)
                    sop(lambda e, c_=c, t=tt: e.activation(
                        wbf[0][:, 512 * t:512 * (t + 1)], src(c_, t), AF.Square))
                    sqn = cS.n
                    mm(pb[tt][:, :], ones128_sb[:, :], src(c, tt),
                       start=(c == 0), stop=(c == NCH - 1))
                    kb.wait("t", cS, sqn)
                    mm(pb[2 + tt][:, :], ones128_sb[:, :],
                       wbf[0][:, 512 * tt:512 * (tt + 1)],
                       start=(c == 0), stop=(c == NCH - 1), tick=True)
                    prev_tick = cT.n
            stn = cT.n
            for tt in range(2):
                nr = wk[3] if tt == 0 else wk[5]
                kb.wait("v", cT, stn)
                vop(lambda e, t=tt: e.tensor_scalar(
                    wk[0][:, :], pb[2 + t][:, :], RDIV, None, ALU.mult))
                vop(lambda e, t=tt: e.tensor_scalar(
                    wk[2][:, :], pb[t][:, :], RDIV, None, ALU.mult))
                kb.wait("s", cV, cV.n)
                sop(lambda e: e.activation(wk[1][:, :], wk[2][:, :], AF.Square))
                kb.wait("v", cS, cS.n)
                vop(lambda e: e.tensor_sub(wk[0][:, :], wk[0][:, :], wk[1][:, :]))
                kb.wait("s", cV, cV.n)
                sop(lambda e: e.activation(wk[1][:, :], wk[0][:, :], AF.Ln,
                                           bias=epsT[:, :]))
                kb.wait("s", cS, cS.n)
                sop(lambda e: e.activation(wk[1][:, :], wk[1][:, :], AF.Exp,
                                           scale=-0.5))
                kb.wait("v", cS, cS.n)
                vop(lambda e, o=nr: e.tensor_scalar(
                    o[:, :], wk[1][:, :], -1.0, None, ALU.mult))
            corder = list(range(NCH))
            ttorder = [0, 1]
            if desc:
                corder = corder[::-1]
                ttorder = [1, 0]
            for c in corder:
                for tt in ttorder:
                    nr = wk[3] if tt == 0 else wk[5]
                    kb.wait("v", cV, cV.n)
                    vop(lambda e, c_=c, t=tt: e.scalar_tensor_tensor(
                        wk[1][:, :], pb[t][:, :], RDIV, src(c_, t),
                        ALU.mult, ALU.subtract))
                    kb.wait("v", cV, cV.n)
                    vop(lambda e, c_=c, t=tt, o=nr: e.tensor_mul(
                        dst(c_, t), wk[1][:, :], o[:, :]))


        def moe_pre():
            kb.wait("t", cV, cV.n)
            for tt in range(2):
                if tt == 1:
                    kb.wait("t", cS, cS.n)
                for ec in range(NCH):
                    mm(pb[6][0:8, :], rw_sb[:, 8 * ec:8 * (ec + 1)], cur(ec, tt),
                       start=(ec == 0), stop=(ec == NCH - 1), tick=(ec == NCH - 1))
                kb.wait("s", cT, cT.n)
                sop(lambda e, t=tt: e.copy(wk[2 + t][0:8, :], pb[6][0:8, :]))
            for lb in range(8):
                tt, co = (0, 128 * lb) if lb < 4 else (1, 128 * (lb - 4))
                kb.wait("t", cS, cS.n)
                kb.wait("t", cV, cV.n)
                mm(pb[7][:, 0:8], wk[2 + tt][0:8, co:co + 128],
                   identf_sb[0:8, 0:8], start=True, stop=True, tick=True)
                kb.wait("v", cT, cT.n)
                vop(lambda e: e.tensor_copy(mt8[0][:, :], pb[7][:, 0:8]))
                vop(lambda e: e.reduce_max(mt8[1][:, 0:1], mt8[0][:, :], AX.X,
                                           negate=True))
                kb.wait("s", cV, cV.n)
                sop(lambda e: e.activation(mt8[2][:, :], mt8[0][:, :], AF.Exp,
                                           bias=mt8[1][:, 0:1],
                                           accum_out=mt8[1][:, 1:2]))
                kb.wait("v", cS, cS.n)
                vop(lambda e: e.reciprocal(mt8[1][:, 2:3], mt8[1][:, 1:2]))
                vop(lambda e: e.tensor_scalar(mt8[2][:, :], mt8[2][:, :],
                                              mt8[1][:, 2:3], None, ALU.mult))

                def top_one(m_dst, oh_dst_bf):
                    vop(lambda e, m=m_dst: e.reduce_max(m, mt8[2][:, :], AX.X))
                    vop(lambda e, m=m_dst: e.tensor_scalar(
                        mt8[0][:, :], mt8[2][:, :], m, None, ALU.is_equal))
                    vop(lambda e: e.tensor_copy(mt8[3][:, :], mt8[0][:, :]))
                    vop(lambda e: e.tensor_add(mt8[3][:, 1:8], mt8[0][:, 1:8],
                                               mt8[0][:, 0:7]))
                    vop(lambda e: e.tensor_copy(wk[0][:, 0:8], mt8[3][:, :]))
                    vop(lambda e: e.tensor_add(wk[0][:, 2:8], mt8[3][:, 2:8],
                                               mt8[3][:, 0:6]))
                    vop(lambda e: e.tensor_copy(mt8[3][:, :], wk[0][:, 0:8]))
                    vop(lambda e: e.tensor_add(mt8[3][:, 4:8], wk[0][:, 4:8],
                                               wk[0][:, 0:4]))
                    vop(lambda e: e.tensor_scalar(mt8[3][:, :], mt8[3][:, :], 1.0,
                                                  None, ALU.is_equal))
                    vop(lambda e: e.tensor_mul(mt8[0][:, :], mt8[0][:, :],
                                               mt8[3][:, :]))
                    vop(lambda e, o=oh_dst_bf: e.tensor_copy(o, mt8[0][:, :]))

                top_one(moe_m1[:, lb:lb + 1], moe_oh1[:, 8 * lb:8 * lb + 8])
                vop(lambda e: e.tensor_scalar(mt8[3][:, :], mt8[0][:, :], -1.0,
                                              1.0, ALU.mult, ALU.add))
                vop(lambda e: e.tensor_mul(mt8[2][:, :], mt8[2][:, :],
                                           mt8[3][:, :]))
                top_one(moe_m2[:, lb:lb + 1], moe_oh2[:, 8 * lb:8 * lb + 8])
            kb.wait("t", cV, cV.n)
            first = True
            for choice in range(2):
                ohs = moe_oh1 if choice == 0 else moe_oh2
                for lb in range(8):
                    co = 8 * (choice * 8 + lb)
                    mm(pb[7][0:1, co:co + 8], ones128_sb[:, 0:1],
                       ohs[:, 8 * lb:8 * lb + 8], start=first, stop=False,
                       tick=(choice == 1 and lb == 7))
                    first = False
            kb.wait("s", cT, cT.n)
            sop(lambda e: e.copy(cntrow[:, :], pb[7][0:1, 0:128]))
            kb.wait("g", cS, cS.n)
            gdma(cA, cnt_in[:, :], cntrow[:, :])
            kb.wait("g", cA)
            kb.tick("g", cCC, lambda e: e.collective_compute(
                "AllGather", ALU.bypass,
                replica_groups=[[0, 1, 2, 3, 4, 5, 6, 7]],
                ins=[cnt_in], outs=[cnt_out]))
            kb.wait("g", cCC, cCC.n)
            for c_ in range(2):
                srcap = cnt_out[:, 64 * c_:64 * (c_ + 1)].rearrange(
                    "core (blk e) -> core blk e", blk=8)
                gdma(cA, cnt_sb1[64 * c_:64 * (c_ + 1), :], srcap,
                     allow_slow_non_contiguous=True)

        def moe_gates():
            kb.wait("t", cA)
            kb.wait("t", cS, cS.n)
            mm(pb[7][0:16, 0:8], mbig_sb[:, :], cnt_sb1[:, :], start=True,
               stop=True, tick=True)
            kb.wait("s", cT, cT.n)
            sop(lambda e: e.copy(moe_s1[0:16, 0:8], pb[7][0:16, 0:8]))
            kb.wait("g", cS, cS.n)
            gdma(cA, offrow[0:1, 0:128], moe_s1[0:16, 0:8],
                 allow_slow_non_contiguous=True)
            kb.wait("t", cA)
            for lb in range(8):
                for choice in range(2):
                    ohs = moe_oh1 if choice == 0 else moe_oh2
                    ms = moe_m1 if choice == 0 else moe_m2
                    r = choice * 8 + lb
                    kb.wait("t", cV, cV.n)
                    mm(pb[6][:, 0:8], u128_sb[:, :], ohs[:, 8 * lb:8 * lb + 8],
                       start=True, stop=False)
                    mm(pb[6][:, 0:8], onesrow_f_sb[:, :],
                       offrow[0:1, 8 * r:8 * r + 8], start=False, stop=True,
                       tick=True)
                    kb.wait("v", cT, cT.n)
                    vop(lambda e, o=ohs, l=lb: e.tensor_tensor(
                        mt8[0][:, :], pb[6][:, 0:8], o[:, 8 * l:8 * l + 8],
                        ALU.mult))
                    vop(lambda e: e.reduce_sum(mt8[1][:, 0:1], mt8[0][:, :], AX.X))
                    vop(lambda e: e.tensor_scalar(mt8[1][:, 1:2], mt8[1][:, 0:1],
                                                  512.0, None, ALU.is_le))
                    vop(lambda e, m=ms, l=lb, ch_=choice: e.tensor_tensor(
                        mt8[2][:, ch_:ch_ + 1], mt8[1][:, 1:2], m[:, l:l + 1],
                        ALU.mult))
                vop(lambda e, l=lb: e.tensor_add(moe_g[:, l:l + 1],
                                                 mt8[2][:, 0:1], mt8[2][:, 1:2]))
            kb.wait("t", cV, cV.n)
            mm(pb[7][0:8, 0:128], moe_g[:, :], identf_sb[:, :],
               start=True, stop=True, tick=True)
            kb.wait("s", cT, cT.n)
            sop(lambda e: e.copy(moe_s1[0:8, :], pb[7][0:8, 0:128]))
            kb.wait("g", cS, cS.n)
            gdma(cA, grow[0:1, 0:1024], moe_s1[0:8, :],
                 allow_slow_non_contiguous=True)
            kb.wait("t", cA)
            for tt in range(2):
                kb.wait("t", cS, cS.n)
                mm(pb[7][:, :], onesrow_f_sb[:, :],
                   grow[0:1, 512 * tt:512 * (tt + 1)], start=True, stop=True,
                   tick=True)
                kb.wait("s", cT, cT.n)
                sop(lambda e, t=tt: e.copy(gb_sb[:, 512 * t:512 * (t + 1)],
                                           pb[7][:, :]))

        def ffn_phase():
            wcons = {}
            lni = 0
            for tt in range(2):
                gel = {0: (cV, cV.n), 1: (cV, cV.n)}
                for f in range(64):
                    li = lni + f
                    kb.wait("g", cT, wcons[li - WR] if li >= WR else cT.n)
                    gdma(cW[li % WR], wb[li % WR][:, :], w1[f][:, :])
                    ldn = cW[li % WR].n
                    bank = pb[f % 2]
                    kb.wait("t", cW[li % WR], ldn)
                    ch_, n_ = gel[f % 2]
                    kb.wait("t", ch_, n_)
                    for ec in range(NCH):
                        mm(bank[:, :], wb[li % WR][:, 128 * ec:128 * (ec + 1)],
                           cur(ec, tt), start=(ec == 0), stop=(ec == NCH - 1),
                           tick=(ec == NCH - 1))
                    wcons[li] = cT.n
                    kb.wait("s", cT, cT.n)
                    sop(lambda e, b=bank, f_=f: e.activation(
                        arenaA[:, 512 * f_:512 * (f_ + 1)], b[:, :], AF.Gelu,
                        bias=b1_sb[:, f_:f_ + 1]))
                    gel[f % 2] = (cS, cS.n)
                lni += 64
                if tt == 0:
                    moe_gates()
                kb.wait("t", cS, cS.n)
                ev2 = {0: (cV, cV.n), 1: (cV, cV.n)}
                for oh in range(NCH):
                    for qq in range(4):
                        li = lni + oh * 4 + qq
                        kb.wait("g", cT, wcons[li - WR] if li - WR in wcons else cT.n)
                        gdma(cW[li % WR], wb[li % WR][:, :],
                             w2[oh][:, 2048 * qq:2048 * (qq + 1)])
                        ldn = cW[li % WR].n
                        bank = pb[2 + oh % 2]
                        kb.wait("t", cW[li % WR], ldn)
                        if qq == 0:
                            ch_, n_ = ev2[oh % 2]
                            kb.wait("t", ch_, n_)
                        for fi in range(16):
                            fc = 16 * qq + fi
                            mm(bank[:, :], wb[li % WR][:, 128 * fi:128 * (fi + 1)],
                               arenaA[:, 512 * fc:512 * (fc + 1)],
                               start=(fc == 0), stop=(fc == 63),
                               tick=(fi == 15))
                        wcons[li] = cT.n
                    kb.wait("v", cT, cT.n)
                    vop(lambda e, b=pb[2 + oh % 2], o=oh, t=tt:
                        e.scalar_tensor_tensor(
                            wk[4][:, :], b[:, :], b2_sb[:, o:o + 1],
                            gb_sb[:, 512 * t:512 * (t + 1)], ALU.add, ALU.mult))
                    kb.wait("v", cV, cV.n)
                    vop(lambda e, o=oh, t=tt: e.tensor_add(
                        aox(o, t), wk[4][:, :], cur(o, t)))
                    ev2[oh % 2] = (cV, cV.n)
                lni += 64

        def emit_out_stream(srcfn):
            idx = 0
            for c in range(NCH):
                for tt in range(2):
                    buf = wk[idx % 2]
                    if idx >= 2:
                        kb.wait("s", cO, cO.n)
                    kb.wait("s", cV, cV.n)
                    sop(lambda e, c_=c, t=tt, b=buf: e.copy(b[:, :], srcfn(c_, t)))
                    kb.wait("y", cS, cS.n)
                    ydma(cO, y[128 * c:128 * (c + 1), 512 * tt:512 * (tt + 1)],
                         buf[:, :])
                    idx += 1
            kb.wait("y", cO, cO.n)

        def quant_chain(src, src_i, seg448, segS, out_bf, slots=None):
            if slots is None:
                Af, Bf, Cf = (arenaB_f[:, 1024:2048], arenaB_f[:, 2048:3072],
                              arenaB_f[:, 3072:4096])
                Ai, Bi, Ci = (arenaB_i[:, 1024:2048], arenaB_i[:, 2048:3072],
                              arenaB_i[:, 3072:4096])
            else:
                (Af, Ai), (Bf, Bi), (Cf, Ci) = slots
            nseg = src.shape[-1] // 128
            vop(lambda e: e.tensor_scalar(Bi, src_i, 0x7FFFFFFF, None,
                                          ALU.bitwise_and))
            vop(lambda e: e.tensor_scalar(Af, Bf, 1e-30, None, ALU.max))
            kb.wait("s", cV, cV.n)
            for sg in range(nseg):
                sop(lambda e, g=sg: e.activation(
                    Bf[:, 128 * g:128 * (g + 1)], Af[:, 128 * g:128 * (g + 1)],
                    AF.Ln, scale=seg448[g]))
            kb.wait("v", cS, cS.n)
            vop(lambda e: e.tensor_scalar(Af, Bf, INV_LN2, -0.5, ALU.mult,
                                          ALU.add))
            vop(lambda e: e.tensor_scalar(Ci, Af, -6.5, None, ALU.max))
            vop(lambda e: e.tensor_scalar(Ai, Ci, -1, 130, ALU.mult, ALU.add))
            vop(lambda e: e.tensor_scalar(Bi, Ai, 23, None,
                                          ALU.arith_shift_left))
            for sg in range(nseg):
                vop(lambda e, g=sg: e.tensor_scalar(
                    Af[:, 128 * g:128 * (g + 1)], src[:, 128 * g:128 * (g + 1)],
                    seg448[g], None, ALU.mult))
            vop(lambda e: e.tensor_tensor(src_i, Af, Bf, ALU.mult))
            vop(lambda e: e.tensor_copy(Af, src_i))
            vop(lambda e: e.tensor_scalar(Bi, Ci, 124, None, ALU.add))
            vop(lambda e: e.tensor_scalar(Ci, Bi, 23, None,
                                          ALU.arith_shift_left))
            vop(lambda e: e.tensor_tensor(Bf, Af, Cf, ALU.mult))
            for sg in range(nseg):
                vop(lambda e, g=sg: e.tensor_scalar(
                    out_bf[:, 128 * g:128 * (g + 1)], Bf[:, 128 * g:128 * (g + 1)],
                    segS[g], None, ALU.mult))

        _wbf2f = wbf[2].bitcast(F32)
        _wk2i = wk[2].bitcast(I32)
        _wk3i = wk[3].bitcast(I32)
        wq_slots = ((_wbf2f[:, 0:512], wbf[2].bitcast(I32)[:, 0:512]),
                    (wk[2][:, :], _wk2i[:, :]),
                    (wk[3][:, :], _wk3i[:, :]))

        def wquant_passA(oh):
            for hf in range(2):
                kb.wait("g", cV, cV.n)
                gdma(cA, wfb[:, 0:1024],
                     wf1[oh][:, 1024 * hf:1024 * (hf + 1)])
                kb.wait("v", cA)
                for sg in range(8):
                    vop(lambda e, g=sg, h_=hf: e.reduce_max(
                        wq_cm[:, 8 * h_ + g:8 * h_ + g + 1],
                        wfb[:, 128 * g:128 * (g + 1)], AX.X,
                        apply_absolute_value=True))
            kb.wait("t", cV, cV.n)
            kb.wait("t", cS, cS.n)
            mm(pb[7][0:16, 0:128], wq_cm[:, :], identf_sb[:, :],
               start=True, stop=True, tick=True)
            kb.wait("v", cT, cT.n)
            vop(lambda e: e.tensor_copy(moe_s1[0:16, :], pb[7][0:16, 0:128]))
            vop(lambda e: e.reduce_max(mt8[0][0:16, 0:1], moe_s1[0:16, :],
                                       AX.X))
            vop(lambda e: e.tensor_scalar(mt8[0][0:16, 1:2],
                                          mt8[0][0:16, 0:1], 1e-4, None,
                                          ALU.max))
            kb.wait("g", cV, cV.n)
            gdma(cA, wq_rows[0:1, 16 * oh:16 * (oh + 1)],
                 mt8[0][0:16, 1:2], allow_slow_non_contiguous=True)

        def wquant_prep():
            kb.wait("v", cA)
            vop(lambda e: e.reciprocal(wq_rows[0:1, 256:512], wq_rows[0:1, 0:256]))
            vop(lambda e: e.tensor_scalar(wq_rows[0:1, 256:512], wq_rows[0:1, 256:512], 448.0,
                                          None, ALU.mult))
            vop(lambda e: e.tensor_scalar(wq_rows[0:1, 0:256], wq_rows[0:1, 0:256],
                                          float(1.0 / 448.0), None, ALU.mult))
            kb.wait("t", cV, cV.n)
            mm(pb[7][:, 0:256], onesrow_f_sb[:, :], wq_rows[0:1, 256:512], start=True,
               stop=True, tick=True)
            kb.wait("s", cT, cT.n)
            sop(lambda e: e.copy(wq_rb448[:, :], pb[7][:, 0:256]))
            kb.wait("t", cS, cS.n)
            mm(pb[7][:, 0:256], onesrow_f_sb[:, :], wq_rows[0:1, 0:256], start=True,
               stop=True, tick=True)
            kb.wait("s", cT, cT.n)
            sop(lambda e: e.copy(wq_s[:, :], pb[7][:, 0:256]))
            kb.wait("v", cS, cS.n)

        def wquant_passB(oh):
            for hf in range(2):
                kb.wait("g", cV, cV.n)
                gdma(cA, wfb[:, 0:1024],
                     wf1[oh][:, 1024 * hf:1024 * (hf + 1)])
                kb.wait("v", cA)
                out_bf = wbf[3 + hf]
                kb.wait("v", cO, cO.n)
                for q2 in range(2):
                    seg448 = [wq_rb448[:, 16 * oh + 8 * hf + 4 * q2 + g:
                                       16 * oh + 8 * hf + 4 * q2 + g + 1]
                              for g in range(4)]
                    segS = [wq_s[:, 16 * oh + 8 * hf + 4 * q2 + g:
                                 16 * oh + 8 * hf + 4 * q2 + g + 1]
                            for g in range(4)]
                    quant_chain(wfb[:, 512 * q2:512 * (q2 + 1)],
                                wfb_i[:, 512 * q2:512 * (q2 + 1)], seg448, segS,
                                out_bf[:, 512 * q2:512 * (q2 + 1)],
                                slots=wq_slots)
                kb.wait("y", cV, cV.n)
                ydma(cO, wdq_d[oh][:, 1024 * hf:1024 * (hf + 1)],
                     out_bf[:, :])

        def xquant_block(tb, hf):
            tt, co = (tb // 4, 128 * (tb % 4))
            for g in range(8):
                c = 8 * hf + g
                kb.wait("t", cV, cV.n)
                mm(pb[6][:, 0:128],
                   arenaA_f[:, 1024 * c + 512 * tt + co:
                            1024 * c + 512 * tt + co + 128],
                   identf_sb[:, :], start=True, stop=True, tick=True)
                kb.wait("v", cT, cT.n)
                vop(lambda e, g_=g: e.tensor_copy(
                    arenaB_f[:, 128 * g_:128 * (g_ + 1)], pb[6][:, 0:128]))
            for g in range(8):
                vop(lambda e, g_=g: e.reduce_max(
                    q_am[:, g_:g_ + 1], arenaB_f[:, 128 * g_:128 * (g_ + 1)],
                    AX.X, apply_absolute_value=True))
            vop(lambda e: e.tensor_scalar(q_am[:, 0:8], q_am[:, 0:8], 1e-4,
                                          None, ALU.max))
            vop(lambda e: e.reciprocal(q_rc448[:, 0:8], q_am[:, 0:8]))
            vop(lambda e: e.tensor_scalar(q_rc448[:, 0:8], q_rc448[:, 0:8],
                                          448.0, None, ALU.mult))
            vop(lambda e: e.tensor_scalar(q_s[:, 0:8], q_am[:, 0:8],
                                          float(1.0 / 448.0), None, ALU.mult))
            seg448 = [q_rc448[:, g:g + 1] for g in range(8)]
            segS = [q_s[:, g:g + 1] for g in range(8)]
            quant_chain(arenaB_f[:, 0:1024], arenaB_i[:, 0:1024], seg448, segS,
                        wbf[0][:, :])
            for g in range(8):
                c = 8 * hf + g
                kb.wait("t", cV, cV.n)
                mm(pb[6][:, 0:128], wbf[0][:, 128 * g:128 * (g + 1)],
                   identb_sb[:, :], start=True, stop=True, tick=True)
                kb.wait("v", cT, cT.n)
                vop(lambda e, c_=c: e.tensor_copy(
                    xB[:, 1024 * c_ + 512 * tt + co:
                       1024 * c_ + 512 * tt + co + 128], pb[6][:, 0:128]))

        def ffn1_phase():
            wcons = {}
            kb.wait("g", cO, cO.n)
            hf_tick = {}
            for hf in range(2):
                for tb in range(4):
                    xquant_block(tb, hf)
                hf_tick[(0, hf)] = cV.n
            for tt in range(2):
                kb.wait("t", cV, cV.n)
                ev = {0: (cV, cV.n), 1: (cV, cV.n)}
                for oh in range(NCH):
                    li = tt * NCH + oh
                    kb.wait("g", cT, wcons[li - WR] if li - WR in wcons else cT.n)
                    gdma(cW[li % WR], wb[li % WR][:, :], wdq_d[oh][:, :])
                    ldn = cW[li % WR].n
                    bank = pb[tt]
                    kb.wait("t", cW[li % WR], ldn)
                    ch_, n_ = ev[tt]
                    kb.wait("t", ch_, n_)
                    for ec in range(NCH):
                        if ec % 8 == 0:
                            kb.wait("t", cV, hf_tick[(tt, ec // 8)])
                        mm(bank[:, :], wb[li % WR][:, 128 * ec:128 * (ec + 1)],
                           xB[:, 1024 * ec + 512 * tt:1024 * ec + 512 * (tt + 1)],
                           start=(ec == 0), stop=(ec == NCH - 1),
                           tick=(ec == NCH - 1))
                    wcons[li] = cT.n
                    kb.wait("s", cT, cT.n)
                    sop(lambda e, b=bank: e.copy(wbf[1][:, 0:512], b[:, :]))
                    kb.wait("v", cS, cS.n)
                    vop(lambda e, o=oh, t=tt: e.tensor_add(
                        cur(o, t), arenaA_f[:, 1024 * o + 512 * t:
                                            1024 * o + 512 * (t + 1)],
                        wbf[1][:, 0:512]))
                    ev[tt] = (cV, cV.n)
                    if tt == 0 and oh in (1, 3):
                        hf = (oh - 1) // 2
                        for tb in range(4 + 0, 8):
                            xquant_block(tb, hf)
                        hf_tick[(1, hf)] = cV.n

        def final_ln():
            kb.wait("t", cV, cV.n)
            kb.wait("s", cV, cV.n)
            prev_tick = cT.n
            for c in range(NCH):
                for tt in range(2):
                    kb.wait("s", cT, prev_tick)
                    sop(lambda e, c_=c, t=tt: e.activation(
                        wbf[0][:, 512 * t:512 * (t + 1)], cur(c_, t), AF.Square))
                    sqn = cS.n
                    mm(pb[tt][:, :], ones128_sb[:, :], cur(c, tt),
                       start=(c == 0), stop=(c == NCH - 1))
                    kb.wait("t", cS, sqn)
                    mm(pb[2 + tt][:, :], ones128_sb[:, :],
                       wbf[0][:, 512 * tt:512 * (tt + 1)],
                       start=(c == 0), stop=(c == NCH - 1), tick=True)
                    prev_tick = cT.n
            stn = cT.n
            for tt in range(2):
                nr = wk[3] if tt == 0 else wk[5]
                kb.wait("v", cT, stn)
                vop(lambda e, t=tt: e.tensor_scalar(
                    wk[0][:, :], pb[2 + t][:, :], RDIV, None, ALU.mult))
                vop(lambda e, t=tt: e.tensor_scalar(
                    wk[2][:, :], pb[t][:, :], RDIV, None, ALU.mult))
                kb.wait("s", cV, cV.n)
                sop(lambda e: e.activation(wk[1][:, :], wk[2][:, :], AF.Square))
                kb.wait("v", cS, cS.n)
                vop(lambda e: e.tensor_sub(wk[0][:, :], wk[0][:, :], wk[1][:, :]))
                kb.wait("s", cV, cV.n)
                sop(lambda e: e.activation(wk[1][:, :], wk[0][:, :], AF.Ln,
                                           bias=epsT[:, :]))
                sop(lambda e: e.activation(wk[1][:, :], wk[1][:, :], AF.Exp,
                                           scale=-0.5))
                kb.wait("v", cS, cS.n)
                vop(lambda e, o=nr: e.tensor_scalar(
                    o[:, :], wk[1][:, :], -1.0, None, ALU.mult))
                for c in range(NCH):
                    buf = wk[2] if c % 2 == 0 else wk[4]
                    vop(lambda e, c_=c, t=tt: e.scalar_tensor_tensor(
                        wk[1][:, :], pb[t][:, :], RDIV, cur(c_, t),
                        ALU.mult, ALU.subtract))
                    kb.wait("v", cO, cO.n)
                    vop(lambda e, b=buf, o=nr: e.tensor_mul(
                        b[:, :], wk[1][:, :], o[:, :]))
                    kb.wait("y", cV, cV.n)
                    ydma(cO, y[128 * c:128 * (c + 1), 512 * tt:512 * (tt + 1)],
                         buf[:, :])
            kb.wait("y", cO, cO.n)

        if stage == "ld":
            emit_out_stream(cur)
        elif stage in ("kv", "kvt", "kvnc"):
            kv_phase(0, do_vt=(stage == "kvt"), do_cc=(stage != "kvnc"))
            emit_out_stream(cur)
        else:
            cc0o, agk0, kvt0 = kv_phase(0)
            AO0 = attn_phase(0, cc0o, agk0, kvt0)
            if stage == "attn":
                emit_out_stream(lambda c, t: AO0[:, 1024 * c + 512 * t:
                                                 1024 * c + 512 * (t + 1)])
            else:
                wo_phase(0, AO0)
                ln_phase(cur)
                if stage == "l0a":
                    emit_out_stream(cur)
                else:
                    moe_pre()
                    ffn_phase()
                    ln_phase(cur, src=aox)
                    if stage == "l0f":
                        emit_out_stream(cur)
                    else:
                        cc1o, agk1, kvt1 = kv_phase(1)
                        side = []
                        for h in range(8):
                            side.append([(lambda o=2 * h: wquant_passA(o)),
                                         (lambda o=2 * h + 1: wquant_passA(o))])
                        side.append([wquant_prep,
                                     (lambda: wquant_passB(0)),
                                     (lambda: wquant_passB(1))])
                        for h in range(9, 16):
                            side.append([(lambda o=2 * (h - 8): wquant_passB(o)),
                                         (lambda o=2 * (h - 8) + 1: wquant_passB(o))])
                        AO1 = attn_phase(1, cc1o, agk1, kvt1, side=side)
                        wo_phase(1, AO1)
                        x3f = lambda c, t: arenaA_f[:, 1024 * c + 512 * t:
                                                    1024 * c + 512 * (t + 1)]
                        ln_phase(x3f, src=rr, desc=True)
                        if stage == "l1a":
                            emit_out_stream(x3f)
                        elif stage == "xd":
                            for tb in range(8):
                                for hf in range(2):
                                    xquant_block(tb, hf)
                            emit_out_stream(lambda c, t: xB[:, 1024 * c + 512 * t:
                                                            1024 * c + 512 * (t + 1)])
                        else:
                            ffn1_phase()
                            final_ln()

        with nc.Block() as block:
            block.tensor(lambda e: [f(e) for f in kb.q["t"]])
            block.vector(lambda e: [f(e) for f in kb.q["v"]])
            block.scalar(lambda e: [f(e) for f in kb.q["s"]])
            block.gpsimd(lambda e: [f(e) for f in kb.q["g"]])
            block.sync(lambda e: [f(e) for f in kb.q["y"]])

    return nc



def _tile_lhsT(Wct, fh_chunks):
    C, F = Wct.shape
    A = Wct.reshape(C // 128, 128, F // 128, 128).transpose(2, 1, 0, 3)
    return np.ascontiguousarray(A.reshape(F // 128, 128, C)).astype(np.float32)


def _bias_cols(b):
    return np.ascontiguousarray(b.reshape(-1, 128).T).astype(np.float32)


def make_core_inputs(inp, c):
    b, p = c // 2, c % 2
    cols = np.concatenate([np.arange(128 * (2 * lb + p), 128 * (2 * lb + p) + 128)
                           for lb in range(8)])
    x = np.asarray(inp["L0_attn_x"], np.float32)[b][cols]
    d = {}
    d["x0"] = np.ascontiguousarray(x.T)
    d["wq0"] = _tile_lhsT(np.asarray(inp["L0_attn_Wq"], np.float32).T, H)
    d["wk0"] = _tile_lhsT(np.asarray(inp["L0_attn_Wk"], np.float32).T, H)
    d["wv0"] = _tile_lhsT(np.asarray(inp["L0_attn_Wv"], np.float32).T, H)
    d["wo0"] = _tile_lhsT(np.asarray(inp["L0_attn_Wo"], np.float32).T, NCH)
    d["bq_c"] = _bias_cols(np.asarray(inp["L0_attn_bq"], np.float32))
    d["bk_c"] = _bias_cols(np.asarray(inp["L0_attn_bk"], np.float32))
    d["bv_c"] = _bias_cols(np.asarray(inp["L0_attn_bv"], np.float32))
    d["bo_c"] = _bias_cols(np.asarray(inp["L0_attn_bo"], np.float32))
    d["b1_c"] = _bias_cols(np.asarray(inp["L0_ffn_b1"], np.float32))
    d["b2_c"] = _bias_cols(np.asarray(inp["L0_ffn_b2"], np.float32))
    rwt = np.asarray(inp["L0_ffn_router_w"], np.float32).T
    d["rw"] = np.ascontiguousarray(
        rwt.reshape(NCH, 128, 8).transpose(1, 0, 2).reshape(128, NCH * 8)).astype(np.float32)
    d["w1"] = _tile_lhsT(np.asarray(inp["L0_ffn_w1"], np.float32).T, 64)
    d["w2"] = _tile_lhsT(np.asarray(inp["L0_ffn_w2"], np.float32).T, NCH)
    d["wq1"] = _tile_lhsT(np.asarray(inp["L1_attn_Wq"], np.float32), H)
    d["wk1"] = _tile_lhsT(np.asarray(inp["L1_attn_Wk"], np.float32), H)
    d["wv1"] = _tile_lhsT(np.asarray(inp["L1_attn_Wv"], np.float32), H)
    d["wo1"] = _tile_lhsT(np.asarray(inp["L1_attn_Wo"], np.float32), H)
    d["wf1"] = _tile_lhsT(np.asarray(inp["L1_ffn_weight"], np.float32).T, NCH)

    pos = (cols).astype(np.float64)
    theta = np.repeat(-np.arange(0, 128, 2, dtype=np.float32) / 128.0, 2)
    theta = np.power(np.float32(10000.0), theta.astype(np.float32)).astype(np.float64)
    ang = pos[None, :] * 0 + 0.0
    ang = theta[:, None] * pos[None, :]
    d["ropec"] = np.cos(ang).astype(ml_dtypes.bfloat16)
    d["ropes"] = np.sin(ang).astype(ml_dtypes.bfloat16)
    P = np.zeros((128, 128), np.float32)
    for i2 in range(0, 128, 2):
        P[i2, i2 + 1] = -1.0
        P[i2 + 1, i2] = 1.0
    d["p128"] = np.ascontiguousarray(P.T).astype(ml_dtypes.bfloat16)
    d["identb"] = np.eye(128, dtype=np.float32).astype(ml_dtypes.bfloat16)
    d["identf"] = np.eye(128, dtype=np.float32)
    d["ones128"] = np.ones((128, 128), ml_dtypes.bfloat16)
    d["onesrow"] = np.ones((1, 128), ml_dtypes.bfloat16)
    d["onesrow_f"] = np.ones((1, 128), np.float32)
    d["u128"] = np.triu(np.ones((128, 128), np.float32)).astype(ml_dtypes.bfloat16)
    d["c1024"] = np.full((1, 128), 1024.0, ml_dtypes.bfloat16)

    tri_strict = np.tril(np.ones((128, 128), np.float32), -1)
    m0 = np.zeros((16, 128, 128), np.float32)
    m1 = np.zeros((16, 128, 128), np.float32)
    for k in range(16):
        j = 0 if k < 8 else 1
        i_min = max(0, int(math.ceil((k - 8 * j - 1) / 2.0)))
        qa = 8 * j + 2 * i_min + p
        if qa == k:
            m0[k] = tri_strict * -30000.0
            m1[k] = np.triu(np.ones((128, 128), np.float32), 0)
        elif qa < k:
            m0[k] = -30000.0
            m1[k] = 0.0
        else:
            m0[k] = 0.0
            m1[k] = 1.0
    d["m0t"] = np.ascontiguousarray(
        m0.transpose(1, 0, 2).reshape(128, 16 * 128)).astype(ml_dtypes.bfloat16)
    d["m1t"] = np.ascontiguousarray(
        m1.transpose(1, 0, 2).reshape(128, 16 * 128)).astype(ml_dtypes.bfloat16)

    myabs = [(b * 16 + 2 * lb + p) for lb in range(8)]
    Mb = np.zeros((128, 16), np.float32)
    for r in range(8):
        a_r = myabs[r]
        for core in range(8):
            for blk in range(8):
                a = (core // 2) * 16 + 2 * blk + (core % 2)
                if a < a_r:
                    Mb[core * 8 + blk, r] = 1.0
                    Mb[64 + core * 8 + blk, 8 + r] = 1.0
        Mb[0:64, 8 + r] = 1.0
    d["mbig"] = Mb.astype(ml_dtypes.bfloat16)
    return d


def kernel(**inputs):
    nc = build(stage="full")
    in_maps = [make_core_inputs(inputs, c) for c in range(8)]
    res = run_bass_kernel_spmd(nc, in_maps, core_ids=list(range(8)))
    out = np.zeros((B, S, E), np.float32)
    for c in range(8):
        b, p = c // 2, c % 2
        yt = res.results[c]["y"]
        xt = np.ascontiguousarray(yt.T)
        for lb in range(8):
            out[b, 128 * (2 * lb + p):128 * (2 * lb + p) + 128, :] = \
                xt[128 * lb:128 * (lb + 1)]
    return out


# version 12
# speedup vs baseline: 1.2079x; 1.2079x over previous
import math
import contextlib
import numpy as np
import ml_dtypes
import concourse.bass as bass
import concourse.mybir as mybir
from concourse.bass_utils import run_bass_kernel_spmd

F32 = mybir.dt.float32
BF16 = mybir.dt.bfloat16
I32 = mybir.dt.int32
AF = mybir.ActivationFunctionType
ALU = mybir.AluOpType
AX = mybir.AxisListType

B, S, E, H, D = 4, 2048, 2048, 16, 128
T = 1024
NCH = 16
LN2C = float(np.log(2.0))
INV_LN2 = float(1.0 / np.log(2.0))
SC_L0 = float(1.0 / np.sqrt(128.0))
SC_L1 = float(1.0 / np.sqrt(2048.0))
RDIV = float(1.0 / 2048.0)


def c0_of(j, k):
    if k < 8 * j:
        return 0
    return 128 * max(0, int(math.ceil((k - 8 * j - 1) / 2.0)))


def kcol(k):
    return (k % 2) * 1024 + (k // 2) * 128


def vtslot(k):
    return (k % 2) * 8 + (k // 2)


class Ch:
    def __init__(self, stack, nc, name, step=1):
        self.h = stack.enter_context(nc.semaphore(name))
        self.n = 0
        self.step = step


class KB:
    def __init__(self, nc):
        self.q = {k: [] for k in "tvsgy"}

    def on(self, ek, fn):
        self.q[ek].append(fn)

    def wait(self, ek, ch, n=None):
        n = ch.n if n is None else n
        if n > 0:
            self.on(ek, lambda e, h=ch.h, m=n: e.wait_ge(h, m))

    def tick(self, ek, ch, fn):
        ch.n += ch.step
        self.on(ek, lambda e, f=fn, h=ch.h, s=ch.step: f(e).then_inc(h, s))


def build(stage="full"):
    nc = bass.Bass("TRN2", target_bir_lowering=False, debug=False)

    def din(name, shape, dtype=F32):
        return nc.dram_tensor(name, list(shape), dtype, kind="ExternalInput").ap()

    x0 = din("x0", (E, T))
    wq0 = din("wq0", (H, 128, E)); wk0 = din("wk0", (H, 128, E))
    wv0 = din("wv0", (H, 128, E)); wo0 = din("wo0", (NCH, 128, E))
    bq_c = din("bq_c", (128, NCH)); bk_c = din("bk_c", (128, NCH))
    bv_c = din("bv_c", (128, NCH)); bo_c = din("bo_c", (128, NCH))
    b1_c = din("b1_c", (128, 64)); b2_c = din("b2_c", (128, NCH))
    rw = din("rw", (128, NCH * 8))
    w1 = din("w1", (64, 128, E))
    w2 = din("w2", (NCH, 128, 8192))
    wq1 = din("wq1", (H, 128, E)); wk1 = din("wk1", (H, 128, E))
    wv1 = din("wv1", (H, 128, E)); wo1 = din("wo1", (NCH, 128, E))
    wf1 = din("wf1", (2, 128, E))
    ropec = din("ropec", (128, T), BF16); ropes = din("ropes", (128, T), BF16)
    p128 = din("p128", (128, 128), BF16)
    identb = din("identb", (128, 128), BF16); identf = din("identf", (128, 128))
    ones128 = din("ones128", (128, 128), BF16)
    onesrow = din("onesrow", (1, 128), BF16); onesrow_f = din("onesrow_f", (1, 128))
    m0t = din("m0t", (128, 16 * 128), BF16); m1t = din("m1t", (128, 16 * 128), BF16)
    u128 = din("u128", (128, 128), BF16)
    mbig = din("mbig", (128, 16), BF16)
    c1024 = din("c1024", (1, 128), BF16)

    y = nc.dram_tensor("y", [E, T], F32, kind="ExternalOutput").ap()

    ccT = {}
    for L in (0, 1):
        for half in (0, 1):
            ccT[(L, "ki", half)] = nc.dram_tensor(
                f"cc{L}K{half}_in", [1024, T], BF16, kind="Internal").ap()
            ccT[(L, "ko", half)] = nc.dram_tensor(
                f"cc{L}K{half}_out", [2048, T], BF16, kind="Internal").ap()
            ccT[(L, "vi", half)] = nc.dram_tensor(
                f"cc{L}V{half}_in", [E, 512], BF16, kind="Internal").ap()
            ccT[(L, "vo", half)] = nc.dram_tensor(
                f"cc{L}V{half}_out", [2 * E, 512], BF16, kind="Internal").ap()
    cnt_in = nc.dram_tensor("cnt_in", [1, 128], F32, kind="Internal").ap()
    cnt_out = nc.dram_tensor("cnt_out", [8, 128], F32, kind="Internal").ap()
    wdq_loc = nc.dram_tensor("wdq_loc", [256, E], BF16, kind="Internal").ap()
    wdqG = nc.dram_tensor("wdqG", [NCH * 128, E], BF16, kind="Internal").ap()

    stack = contextlib.ExitStack()
    with stack:
        sb = lambda name, shape, dtype: stack.enter_context(
            nc.sbuf_tensor(name, list(shape), dtype))

        xA = sb("xA", (128, NCH * 1024), BF16)
        xB = sb("xB", (128, NCH * 1024), BF16)
        arenaA = sb("arenaA", (128, 32768), BF16)
        arenaB = sb("arenaB", (128, 8192), BF16)
        arenaA_f = arenaA.bitcast(F32)
        arenaB_f = arenaB.bitcast(F32)
        arenaB_i = arenaB.bitcast(I32)

        kbuf = [sb(f"kbuf{i}", (128, 2048), BF16) for i in range(2)]
        qh = sb("qh", (128, 1024), BF16)
        WR = 3
        wb = [sb(f"wb{i}", (128, 2048), BF16) for i in range(WR)]
        wb2 = wb
        wfb = sb("wfb", (128, 1024), F32)
        gb_sb = kbuf[0].bitcast(F32)
        grow = kbuf[1].bitcast(F32)
        ropec_sb = sb("ropec_sb", (128, T), BF16)
        ropes_sb = sb("ropes_sb", (128, T), BF16)
        p128_sb = sb("p128_sb", (128, 128), BF16)
        identb_sb = sb("identb_sb", (128, 128), BF16)
        identf_sb = sb("identf_sb", (128, 128), F32)
        ones128_sb = sb("ones128_sb", (128, 128), BF16)
        onesrow_sb = sb("onesrow_sb", (1, 128), BF16)
        onesrow_f_sb = sb("onesrow_f_sb", (1, 128), F32)
        allones = sb("allones", (128, 512), BF16)
        m_sb = sb("m_sb", (128, 16 * 128), BF16)
        u128_sb = sb("u128_sb", (128, 128), BF16)
        mbig_sb = sb("mbig_sb", (128, 16), BF16)
        c1024_sb = sb("c1024_sb", (1, 128), BF16)
        bq_sb = sb("bq_sb", (128, NCH), F32)
        bk_sb = sb("bk_sb", (128, NCH), F32)
        bv_sb = sb("bv_sb", (128, NCH), F32)
        bo_sb = sb("bo_sb", (128, NCH), F32)
        b1_sb = sb("b1_sb", (128, 64), F32)
        b2_sb = sb("b2_sb", (128, NCH), F32)
        rw_sb = sb("rw_sb", (128, NCH * 8), BF16)
        wbf = [sb(f"wbf{i}", (128, 1024), BF16) for i in range(5)]
        wk = [sb(f"wkt{i}", (128, 512), F32) for i in range(4)]
        wk.append(wbf[0].bitcast(F32))
        wk.append(sb("wkt5", (128, 512), F32))
        epsT = sb("epsT", (128, 1), F32)
        moe_g = sb("moe_g", (128, 8), F32)
        moe_m1 = sb("moe_m1", (128, 8), F32)
        moe_m2 = sb("moe_m2", (128, 8), F32)
        moe_oh1 = sb("moe_oh1", (128, 8 * 8), BF16)
        moe_oh2 = sb("moe_oh2", (128, 8 * 8), BF16)
        mt8 = [sb(f"mt8_{i}", (128, 8), F32) for i in range(4)]
        cntrow = sb("cntrow", (1, 128), F32)
        cnt_sb1 = sb("cnt_sb1", (128, 8), BF16)
        cnt_sb2 = sb("cnt_sb2", (128, 8), BF16)
        offrow = sb("offrow", (1, 128), F32)
        moe_s1 = sb("moe_s1", (16, 128), F32)
        q_am = sb("q_am", (128, 16), F32)
        q_rc448 = sb("q_rc448", (128, 16), F32)
        q_s = sb("q_s", (128, 16), F32)
        wq_cm = sb("wq_cm", (128, 16), F32)
        wq_rows = sb("wq_rows", (1, 512), F32)
        wq_amrow = wq_rows[0:1, 0:256]
        wq_row2 = wq_rows[0:1, 256:512]
        _wbf1f = wbf[1].bitcast(F32)
        wq_rb448 = _wbf1f[:, 0:256]
        wq_s = _wbf1f[:, 256:512]
        wfb_i = wfb.bitcast(I32)

        pb = [stack.enter_context(nc.psum_tensor(f"pb{i}", [128, 512], F32))
              for i in range(8)]

        assert nc.sbuf_bytes_remaining >= 0, nc.sbuf_bytes_remaining
        assert nc.psum_banks_remaining >= 0

        kb = KB(nc)
        cC = Ch(stack, nc, "cC", 16)
        cX = Ch(stack, nc, "cX", 16)
        cW = [Ch(stack, nc, f"cW{i}", 16) for i in range(4)]
        cSg = [Ch(stack, nc, f"cSg{i}", 16) for i in range(2)]
        cK = [Ch(stack, nc, f"cK{i}", 16) for i in range(2)]
        cVT = Ch(stack, nc, "cVT", 16)
        cM = Ch(stack, nc, "cM", 16)
        cO = Ch(stack, nc, "cO", 16)
        cA = Ch(stack, nc, "cA", 16)
        cT = Ch(stack, nc, "cT", 1)
        cV = Ch(stack, nc, "cV", 1)
        cS = Ch(stack, nc, "cS", 1)
        cCC = Ch(stack, nc, "cCC", 1)

        def ydma(ch, out, in_, **kw):
            kb.tick("y", ch, lambda e, o=out, i=in_, k=kw: e.dma_start(out=o, in_=i, **k))

        def gdma(ch, out, in_, **kw):
            kb.tick("g", ch, lambda e, o=out, i=in_, k=kw: e.dma_start(out=o, in_=i, **k))

        def mm(out, lhsT, rhs, start, stop, tick=False):
            fn = lambda e, o=out, l=lhsT, r=rhs, s1=start, s2=stop: \
                e.matmul(o, l, r, start=s1, stop=s2, skip_group_check=True)
            if tick:
                kb.tick("t", cT, fn)
            else:
                kb.on("t", fn)

        def vop(fn, tick=True):
            kb.wait("v", cV, cV.n)
            kb.tick("v", cV, fn)

        def sop(fn, tick=True):
            kb.wait("s", cS, cS.n)
            kb.tick("s", cS, fn)

        for dst, src in [
            (ropec_sb, ropec), (ropes_sb, ropes), (identb_sb, identb),
            (identf_sb, identf), (ones128_sb, ones128), (onesrow_sb, onesrow),
            (onesrow_f_sb, onesrow_f), (p128_sb, p128),
            (u128_sb, u128), (mbig_sb, mbig),
            (c1024_sb, c1024), (bq_sb, bq_c), (bk_sb, bk_c), (bv_sb, bv_c),
            (bo_sb, bo_c), (b1_sb, b1_c), (b2_sb, b2_c),
        ]:
            ydma(cC, dst[:, :], src[:, :])
        n_const = cC.n
        gdma(cX, rw_sb[:, :], rw[:, :])
        for c in range(NCH):
            gdma(cX, xA[:, 1024 * c:1024 * (c + 1)], x0[128 * c:128 * (c + 1), :])
        vop(lambda e: e.memset(epsT[:, :], 1e-5))
        vop(lambda e: e.memset(allones[:, :], 1.0), tick=True)
        kb.wait("t", cC, n_const)
        kb.wait("v", cC, n_const)
        kb.wait("s", cC, n_const)
        kb.wait("t", cX)

        cur = lambda c, t=None: xA[:, 1024 * c + (0 if t is None else 512 * t):
                                   1024 * c + (1024 if t is None else 512 * (t + 1))]
        aox = lambda c, t=None: xB[:, 1024 * c + (0 if t is None else 512 * t):
                                   1024 * c + (1024 if t is None else 512 * (t + 1))]
        rr = lambda c, t=None: arenaA[:, 1024 * c + (0 if t is None else 512 * t):
                                      1024 * c + (1024 if t is None else 512 * (t + 1))]

        def proj_mms(bank, wslot, tt):
            for ec in range(NCH):
                mm(bank[:, :], wslot[:, 128 * ec:128 * (ec + 1)],
                   cur(ec, tt), start=(ec == 0), stop=(ec == NCH - 1),
                   tick=(ec == NCH - 1))

        def kv_phase(layer, do_vt=True, do_cc=True):
            wkw = wk0 if layer == 0 else wk1
            wvw = wv0 if layer == 0 else wv1
            grp = [[0, 1], [2, 3], [4, 5], [6, 7]]
            steps = ([("k", h) for h in range(H)] + [("v", h) for h in range(H)])
            cons = {}
            dmad = {}
            entry_t = cT.n
            agk = [0, 0]
            bank_free = {0: (cV, cV.n), 1: (cV, cV.n)}
            for i, (kind, h) in enumerate(steps):
                kb.wait("g", cT, cons[i - WR] if i >= WR else entry_t)
                gdma(cW[i % WR], wb[i % WR][:, :], (wkw if kind == "k" else wvw)[h][:, :])
                ldn = cW[i % WR].n
                stg = wbf[3 + (i % 2)]
                if i >= 2:
                    kb.wait("s", cSg[i % 2], dmad[i - 2])
                    kb.wait("v", cSg[i % 2], dmad[i - 2])
                for tt in range(2):
                    bank = pb[tt]
                    kb.wait("t", cW[i % WR], ldn)
                    ch_, n_ = bank_free[tt]
                    kb.wait("t", ch_, n_)
                    proj_mms(bank, wb[i % WR], tt)
                    mmn = cT.n
                    if layer == 0 and kind == "k":
                        bcol = bk_sb[:, h:h + 1]
                        kb.wait("s", cT, mmn)
                        kb.wait("s", cV, cV.n)
                        sop(lambda e, b=bank, t=tt, bc=bcol: e.activation(
                            wbf[0][:, 512 * t:512 * (t + 1)], b[:, :], AF.Identity,
                            bias=bc))
                        kb.wait("v", cS, cS.n)
                        vop(lambda e, t=tt: e.tensor_mul(
                            wbf[1][:, 512 * t:512 * (t + 1)],
                            wbf[0][:, 512 * t:512 * (t + 1)],
                            ropec_sb[:, 512 * t:512 * (t + 1)]))
                        vop(lambda e, t=tt: e.tensor_mul(
                            wbf[2][:, 512 * t:512 * (t + 1)],
                            wbf[0][:, 512 * t:512 * (t + 1)],
                            ropes_sb[:, 512 * t:512 * (t + 1)]))
                        kb.wait("t", cV, cV.n)
                        mm(pb[7][:, :], p128_sb[:, :],
                           wbf[2][:, 512 * tt:512 * (tt + 1)], True, True,
                           tick=True)
                        kb.wait("v", cT, cT.n)
                        vop(lambda e, t=tt, o=stg: e.tensor_add(
                            o[:, 512 * t:512 * (t + 1)],
                            wbf[1][:, 512 * t:512 * (t + 1)], pb[7][:, :]))
                        bank_free[tt] = (cS, cS.n)
                    else:
                        kb.wait("s", cT, mmn)
                        if layer == 0:
                            bcol = bv_sb[:, h:h + 1]
                            sop(lambda e, b=bank, o=stg, t=tt, bc=bcol:
                                e.activation(o[:, 512 * t:512 * (t + 1)], b[:, :],
                                             AF.Identity, bias=bc))
                        else:
                            sop(lambda e, b=bank, o=stg, t=tt: e.copy(
                                o[:, 512 * t:512 * (t + 1)], b[:, :]))
                        bank_free[tt] = (cS, cS.n)
                cons[i] = cT.n
                if layer == 0 and kind == "k":
                    kb.wait("y", cV, cV.n)
                else:
                    kb.wait("y", cS, cS.n)
                if kind == "k":
                    half = h // 8
                    dst = ccT[(layer, "ki", half)]
                    ydma(cSg[i % 2], dst[128 * (h % 8):128 * (h % 8) + 128, :],
                         stg[:, :])
                    dmad[i] = cSg[i % 2].n
                    if h % 8 == 7 and do_cc:
                        kb.wait("g", cSg[0])
                        kb.wait("g", cSg[1])
                        kb.tick("g", cCC, lambda e, ci=dst,
                                co=ccT[(layer, "ko", half)]: e.collective_compute(
                                    "AllGather", ALU.bypass, replica_groups=grp,
                                    ins=[ci], outs=[co]))
                        agk[half] = cCC.n
                else:
                    for vh in range(2):
                        ydma(cSg[i % 2],
                             ccT[(layer, "vi", vh)][128 * h:128 * (h + 1), :],
                             stg[:, 512 * vh:512 * (vh + 1)])
                    dmad[i] = cSg[i % 2].n
            agv = [0, 0]
            for vh in range(2):
                kb.wait("g", cSg[0])
                kb.wait("g", cSg[1])
                if do_cc:
                    kb.tick("g", cCC, lambda e, ci=ccT[(layer, "vi", vh)],
                            co=ccT[(layer, "vo", vh)]: e.collective_compute(
                                "AllGather", ALU.bypass, replica_groups=grp,
                                ins=[ci], outs=[co]))
                agv[vh] = cCC.n
            ydma(cM, m_sb[:, :], (m0t if layer == 0 else m1t)[:, :])
            vtn = [0, 0]
            if do_vt:
                for vh in range(2):
                    kb.wait("y", cCC, agv[vh])
                    for r in range(2):
                        for m2 in range(4 * vh, 4 * vh + 4):
                            a = r * 8 + m2
                            src = ccT[(layer, "vo", vh)][E * r: E * (r + 1),
                                                         128 * (m2 - 4 * vh):
                                                         128 * (m2 - 4 * vh) + 128]
                            ydma(cVT, arenaA[:, 2048 * a:2048 * (a + 1)], src,
                                 transpose=True)
                    vtn[vh] = cVT.n
            return layer, agk, vtn, cT.n

        def attn_phase(layer, agk, vtn, kvdone_t, side=None):
            wqw = wq0 if layer == 0 else wq1
            msk = m_sb
            AO = xB
            Pb = arenaB
            scale = SC_L0 if layer == 0 else SC_L1
            fillval = 0.0 if layer == 0 else 1.0
            kload = {}
            headdone = {}
            kb.wait("t", cM)
            vt_waited = [False, False]
            agk_waited = [False, False]
            for h in range(H):
                kb.wait("g", cT, kvdone_t if h < WR else headdone[h - WR])
                gdma(cW[h % WR], wb[h % WR][:, :], wqw[h][:, :])
                qldn = cW[h % WR].n
                if h >= 2:
                    kb.wait("y", cT, headdone[h - 2])
                if not agk_waited[h // 8]:
                    kb.wait("y", cCC, agk[h // 8])
                    agk_waited[h // 8] = True
                ko = ccT[(layer, "ko", h // 8)]
                for r in range(2):
                    ydma(cK[h % 2], kbuf[h % 2][:, 1024 * r:1024 * (r + 1)],
                         ko[1024 * r + 128 * (h % 8): 1024 * r + 128 * (h % 8) + 128, :])
                kload[h] = cK[h % 2].n
                for tt in range(2):
                    kb.wait("t", cW[h % WR], qldn)
                    kb.wait("t", cS, cS.n)
                    kb.wait("t", cV, cV.n)
                    proj_mms(pb[6], wb[h % WR], tt)
                    mmn = cT.n
                    kb.wait("s", cT, mmn)
                    if layer == 0:
                        kb.wait("s", cV, cV.n)
                        sop(lambda e, t=tt, h_=h: e.activation(
                            wbf[0][:, 512 * t:512 * (t + 1)], pb[6][:, :],
                            AF.Identity, bias=bq_sb[:, h_:h_ + 1]), tick=True)
                        kb.wait("v", cS, cS.n)
                        vop(lambda e, t=tt: e.tensor_mul(
                            wbf[1][:, 512 * t:512 * (t + 1)],
                            wbf[0][:, 512 * t:512 * (t + 1)],
                            ropec_sb[:, 512 * t:512 * (t + 1)]))
                        vop(lambda e, t=tt: e.tensor_mul(
                            wbf[2][:, 512 * t:512 * (t + 1)],
                            wbf[0][:, 512 * t:512 * (t + 1)],
                            ropes_sb[:, 512 * t:512 * (t + 1)]), tick=True)
                        kb.wait("t", cV, cV.n)
                        mm(pb[7][:, :], p128_sb[:, :],
                           wbf[2][:, 512 * tt:512 * (tt + 1)], True, True,
                           tick=True)
                        kb.wait("v", cT, cT.n)
                        vop(lambda e, t=tt: e.tensor_add(
                            qh[:, 512 * t:512 * (t + 1)],
                            wbf[1][:, 512 * t:512 * (t + 1)], pb[7][:, :]),
                            tick=True)
                    else:
                        sop(lambda e, t=tt: e.copy(
                            qh[:, 512 * t:512 * (t + 1)], pb[6][:, :]), tick=True)
                kb.wait("t", cV, cV.n)
                kb.wait("t", cS, cS.n)
                kb.wait("t", cK[h % 2], kload[h])
                exp_tick = {}
                for j in range(2):
                    kb.wait("v", cT, cT.n)
                    fk = [(k, c0_of(j, k)) for k in range(0, 8 * j + 8)
                          if c0_of(j, k) > 0]
                    for fi, (k, c0) in enumerate(fk):
                        vop(lambda e, k_=k, c_=c0: e.memset(
                            Pb[:, 512 * k_:512 * k_ + c_], fillval),
                            tick=(fi == len(fk) - 1))
                    kb.wait("s", cV, cV.n)
                    kb.wait("t", cV, cV.n)
                    for k in range(0, 8 * j + 8):
                        c0 = c0_of(j, k)
                        bank = pb[k % 2]
                        if k >= 2:
                            kb.wait("t", cS, exp_tick[k - 2])
                        has_mask = (layer == 0 and k >= 8 * j)
                        mm(bank[:, c0:512],
                           kbuf[h % 2][:, kcol(k):kcol(k) + 128],
                           qh[:, 512 * j + c0:512 * (j + 1)],
                           start=True, stop=not has_mask, tick=not has_mask)
                        if has_mask:
                            mm(bank[:, c0:c0 + 128], identb_sb[:, :],
                               msk[:, 128 * k:128 * (k + 1)], start=False,
                               stop=True, tick=True)
                        smn = cT.n
                        if layer == 0 or k < 8 * j:
                            kb.wait("s", cT, smn)
                            sop(lambda e, k_=k, c_=c0, b=bank: e.activation(
                                Pb[:, 512 * k_ + c_:512 * (k_ + 1)], b[:, c_:512],
                                AF.Exp, scale=scale), tick=True)
                        else:
                            kb.wait("v", cT, smn)
                            kb.wait("v", cS, cS.n)
                            vop(lambda e, k_=k, c_=c0, b=bank: e.tensor_mul(
                                wk[5][:, 0:128], b[:, c_:c_ + 128],
                                msk[:, 128 * k_:128 * (k_ + 1)]), tick=True)
                            kb.wait("s", cV, cV.n)
                            sop(lambda e, k_=k, c_=c0: e.activation(
                                Pb[:, 512 * k_ + c_:512 * k_ + c_ + 128],
                                wk[5][:, 0:128], AF.Exp, scale=scale),
                                tick=(c0 + 128 >= 512))
                            if c0 + 128 < 512:
                                sop(lambda e, k_=k, c_=c0, b=bank: e.activation(
                                    Pb[:, 512 * k_ + c_ + 128:512 * (k_ + 1)],
                                    b[:, c_ + 128:512], AF.Exp, scale=scale),
                                    tick=True)
                        exp_tick[k] = cS.n
                    kb.wait("t", cS, cS.n)
                    vtg = 1 if layer == 1 else j
                    if not vt_waited[vtg]:
                        kb.wait("t", cVT, vtn[vtg])
                        vt_waited[vtg] = True
                    obank, zbank = pb[2 + j], pb[4 + j]
                    krange = list(range(0, 8 * j + 8)) if layer == 0 else (
                        list(range(16)) if j == 0 else list(range(16)))
                    zrange = set(range(0, 8 * j + 8))
                    first = True
                    for k in krange:
                        inwin = k in zrange
                        last = (k == krange[-1])
                        rhs = Pb[:, 512 * k:512 * (k + 1)] if inwin else allones[:, :]
                        vt = arenaA[:, 2048 * vtslot(k) + 128 * h:
                                    2048 * vtslot(k) + 128 * (h + 1)]
                        mm(obank[:, :], vt, rhs, start=first, stop=last,
                           tick=(last and not (inwin and True)))
                        if inwin:
                            mm(zbank[:, :], ones128_sb[:, :], rhs,
                               start=first, stop=last, tick=last)
                        if layer == 1 and first and j == 0:
                            mm(zbank[:, :], c1024_sb[:, :], allones[0:1, 0:512],
                               start=False, stop=False)
                        first = False
                    pvn = cT.n
                    kb.wait("s", cT, pvn)
                    sop(lambda e, z=zbank: e.activation(wk[0][:, :], z[:, :], AF.Ln),
                        tick=True)
                    kb.wait("s", cS, cS.n)
                    sop(lambda e: e.activation(wk[1][:, :], wk[0][:, :], AF.Exp,
                                               scale=-1.0), tick=True)
                    kb.wait("v", cS, cS.n)
                    vop(lambda e, o=obank, h_=h, j_=j: e.tensor_mul(
                        AO[:, 1024 * h_ + 512 * j_:1024 * h_ + 512 * (j_ + 1)],
                        o[:, :], wk[1][:, :]), tick=True)
                headdone[h] = cT.n
                if side is not None and h < len(side):
                    for fn in side[h]:
                        fn()
            kb.wait("t", cV, cV.n)
            return AO

        def wo_phase(layer, AO):
            wow = wo0 if layer == 0 else wo1
            cons = {}
            entry_t = cT.n
            evac = {0: (cV, cV.n), 1: (cV, cV.n)}
            for fh in range(NCH):
                kb.wait("g", cT, cons[fh - WR] if fh >= WR else entry_t)
                gdma(cW[fh % WR], wb2[fh % WR][:, :], wow[fh][:, :])
                ldn = cW[fh % WR].n
                for tt in range(2):
                    bank = pb[tt]
                    kb.wait("t", cW[fh % WR], ldn)
                    ch_, n_ = evac[tt]
                    kb.wait("t", ch_, n_)
                    for ec in range(NCH):
                        mm(bank[:, :], wb2[fh % WR][:, 128 * ec:128 * (ec + 1)],
                           AO[:, 1024 * ec + 512 * tt:1024 * ec + 512 * (tt + 1)],
                           start=(ec == 0), stop=(ec == NCH - 1),
                           tick=(ec == NCH - 1))
                    kb.wait("v", cT, cT.n)
                    if layer == 0:
                        vop(lambda e, b=bank, f=fh, t=tt: e.scalar_tensor_tensor(
                            rr(f, t), b[:, :], bo_sb[:, f:f + 1], cur(f, t),
                            ALU.add, ALU.add), tick=True)
                    else:
                        vop(lambda e, b=bank, f=fh, t=tt: e.tensor_add(
                            rr(f, t), b[:, :], cur(f, t)), tick=True)
                    evac[tt] = (cV, cV.n)
                cons[fh] = cT.n

        def ln_phase(dst, src=None, desc=False):
            if src is None:
                src = rr
            kb.wait("t", cV, cV.n)
            kb.wait("s", cV, cV.n)
            prev_tick = cT.n
            for c in range(NCH):
                for tt in range(2):
                    kb.wait("s", cT, prev_tick)
                    sop(lambda e, c_=c, t=tt: e.activation(
                        wbf[0][:, 512 * t:512 * (t + 1)], src(c_, t), AF.Square))
                    sqn = cS.n
                    mm(pb[tt][:, :], ones128_sb[:, :], src(c, tt),
                       start=(c == 0), stop=(c == NCH - 1))
                    kb.wait("t", cS, sqn)
                    mm(pb[2 + tt][:, :], ones128_sb[:, :],
                       wbf[0][:, 512 * tt:512 * (tt + 1)],
                       start=(c == 0), stop=(c == NCH - 1), tick=True)
                    prev_tick = cT.n
            stn = cT.n
            for tt in range(2):
                nr = wk[3] if tt == 0 else wk[5]
                kb.wait("v", cT, stn)
                vop(lambda e, t=tt: e.tensor_scalar(
                    wk[0][:, :], pb[2 + t][:, :], RDIV, None, ALU.mult))
                vop(lambda e, t=tt: e.tensor_scalar(
                    wk[2][:, :], pb[t][:, :], RDIV, None, ALU.mult))
                kb.wait("s", cV, cV.n)
                sop(lambda e: e.activation(wk[1][:, :], wk[2][:, :], AF.Square))
                kb.wait("v", cS, cS.n)
                vop(lambda e: e.tensor_sub(wk[0][:, :], wk[0][:, :], wk[1][:, :]))
                kb.wait("s", cV, cV.n)
                sop(lambda e: e.activation(wk[1][:, :], wk[0][:, :], AF.Ln,
                                           bias=epsT[:, :]))
                kb.wait("s", cS, cS.n)
                sop(lambda e: e.activation(wk[1][:, :], wk[1][:, :], AF.Exp,
                                           scale=-0.5))
                kb.wait("v", cS, cS.n)
                vop(lambda e, o=nr: e.tensor_scalar(
                    o[:, :], wk[1][:, :], -1.0, None, ALU.mult))
            corder = list(range(NCH))
            ttorder = [0, 1]
            if desc:
                corder = corder[::-1]
                ttorder = [1, 0]
            for c in corder:
                for tt in ttorder:
                    nr = wk[3] if tt == 0 else wk[5]
                    kb.wait("v", cV, cV.n)
                    vop(lambda e, c_=c, t=tt: e.scalar_tensor_tensor(
                        wk[1][:, :], pb[t][:, :], RDIV, src(c_, t),
                        ALU.mult, ALU.subtract))
                    kb.wait("v", cV, cV.n)
                    vop(lambda e, c_=c, t=tt, o=nr: e.tensor_mul(
                        dst(c_, t), wk[1][:, :], o[:, :]))


        def moe_pre():
            kb.wait("t", cV, cV.n)
            for tt in range(2):
                if tt == 1:
                    kb.wait("t", cS, cS.n)
                for ec in range(NCH):
                    mm(pb[6][0:8, :], rw_sb[:, 8 * ec:8 * (ec + 1)], cur(ec, tt),
                       start=(ec == 0), stop=(ec == NCH - 1), tick=(ec == NCH - 1))
                kb.wait("s", cT, cT.n)
                sop(lambda e, t=tt: e.copy(wk[2 + t][0:8, :], pb[6][0:8, :]))
            for lb in range(8):
                tt, co = (0, 128 * lb) if lb < 4 else (1, 128 * (lb - 4))
                kb.wait("t", cS, cS.n)
                kb.wait("t", cV, cV.n)
                mm(pb[7][:, 0:8], wk[2 + tt][0:8, co:co + 128],
                   identf_sb[0:8, 0:8], start=True, stop=True, tick=True)
                kb.wait("v", cT, cT.n)
                vop(lambda e: e.tensor_copy(mt8[0][:, :], pb[7][:, 0:8]))
                vop(lambda e: e.reduce_max(mt8[1][:, 0:1], mt8[0][:, :], AX.X,
                                           negate=True))
                kb.wait("s", cV, cV.n)
                sop(lambda e: e.activation(mt8[2][:, :], mt8[0][:, :], AF.Exp,
                                           bias=mt8[1][:, 0:1],
                                           accum_out=mt8[1][:, 1:2]))
                kb.wait("v", cS, cS.n)
                vop(lambda e: e.reciprocal(mt8[1][:, 2:3], mt8[1][:, 1:2]))
                vop(lambda e: e.tensor_scalar(mt8[2][:, :], mt8[2][:, :],
                                              mt8[1][:, 2:3], None, ALU.mult))

                def top_one(m_dst, oh_dst_bf):
                    vop(lambda e, m=m_dst: e.reduce_max(m, mt8[2][:, :], AX.X))
                    vop(lambda e, m=m_dst: e.tensor_scalar(
                        mt8[0][:, :], mt8[2][:, :], m, None, ALU.is_equal))
                    vop(lambda e: e.tensor_copy(mt8[3][:, :], mt8[0][:, :]))
                    vop(lambda e: e.tensor_add(mt8[3][:, 1:8], mt8[0][:, 1:8],
                                               mt8[0][:, 0:7]))
                    vop(lambda e: e.tensor_copy(wk[0][:, 0:8], mt8[3][:, :]))
                    vop(lambda e: e.tensor_add(wk[0][:, 2:8], mt8[3][:, 2:8],
                                               mt8[3][:, 0:6]))
                    vop(lambda e: e.tensor_copy(mt8[3][:, :], wk[0][:, 0:8]))
                    vop(lambda e: e.tensor_add(mt8[3][:, 4:8], wk[0][:, 4:8],
                                               wk[0][:, 0:4]))
                    vop(lambda e: e.tensor_scalar(mt8[3][:, :], mt8[3][:, :], 1.0,
                                                  None, ALU.is_equal))
                    vop(lambda e: e.tensor_mul(mt8[0][:, :], mt8[0][:, :],
                                               mt8[3][:, :]))
                    vop(lambda e, o=oh_dst_bf: e.tensor_copy(o, mt8[0][:, :]))

                top_one(moe_m1[:, lb:lb + 1], moe_oh1[:, 8 * lb:8 * lb + 8])
                vop(lambda e: e.tensor_scalar(mt8[3][:, :], mt8[0][:, :], -1.0,
                                              1.0, ALU.mult, ALU.add))
                vop(lambda e: e.tensor_mul(mt8[2][:, :], mt8[2][:, :],
                                           mt8[3][:, :]))
                top_one(moe_m2[:, lb:lb + 1], moe_oh2[:, 8 * lb:8 * lb + 8])
            kb.wait("t", cV, cV.n)
            first = True
            for choice in range(2):
                ohs = moe_oh1 if choice == 0 else moe_oh2
                for lb in range(8):
                    co = 8 * (choice * 8 + lb)
                    mm(pb[7][0:1, co:co + 8], ones128_sb[:, 0:1],
                       ohs[:, 8 * lb:8 * lb + 8], start=first, stop=False,
                       tick=(choice == 1 and lb == 7))
                    first = False
            kb.wait("s", cT, cT.n)
            sop(lambda e: e.copy(cntrow[:, :], pb[7][0:1, 0:128]))
            kb.wait("g", cS, cS.n)
            gdma(cA, cnt_in[:, :], cntrow[:, :])
            kb.wait("g", cA)
            kb.tick("g", cCC, lambda e: e.collective_compute(
                "AllGather", ALU.bypass,
                replica_groups=[[0, 1, 2, 3, 4, 5, 6, 7]],
                ins=[cnt_in], outs=[cnt_out]))
            kb.wait("g", cCC, cCC.n)
            for c_ in range(2):
                srcap = cnt_out[:, 64 * c_:64 * (c_ + 1)].rearrange(
                    "core (blk e) -> core blk e", blk=8)
                gdma(cA, cnt_sb1[64 * c_:64 * (c_ + 1), :], srcap,
                     allow_slow_non_contiguous=True)

        def moe_gates():
            kb.wait("t", cA)
            kb.wait("t", cS, cS.n)
            mm(pb[7][0:16, 0:8], mbig_sb[:, :], cnt_sb1[:, :], start=True,
               stop=True, tick=True)
            kb.wait("s", cT, cT.n)
            sop(lambda e: e.copy(moe_s1[0:16, 0:8], pb[7][0:16, 0:8]))
            kb.wait("g", cS, cS.n)
            gdma(cA, offrow[0:1, 0:128], moe_s1[0:16, 0:8],
                 allow_slow_non_contiguous=True)
            kb.wait("t", cA)
            for lb in range(8):
                for choice in range(2):
                    ohs = moe_oh1 if choice == 0 else moe_oh2
                    ms = moe_m1 if choice == 0 else moe_m2
                    r = choice * 8 + lb
                    kb.wait("t", cV, cV.n)
                    mm(pb[6][:, 0:8], u128_sb[:, :], ohs[:, 8 * lb:8 * lb + 8],
                       start=True, stop=False)
                    mm(pb[6][:, 0:8], onesrow_f_sb[:, :],
                       offrow[0:1, 8 * r:8 * r + 8], start=False, stop=True,
                       tick=True)
                    kb.wait("v", cT, cT.n)
                    vop(lambda e, o=ohs, l=lb: e.tensor_tensor(
                        mt8[0][:, :], pb[6][:, 0:8], o[:, 8 * l:8 * l + 8],
                        ALU.mult))
                    vop(lambda e: e.reduce_sum(mt8[1][:, 0:1], mt8[0][:, :], AX.X))
                    vop(lambda e: e.tensor_scalar(mt8[1][:, 1:2], mt8[1][:, 0:1],
                                                  512.0, None, ALU.is_le))
                    vop(lambda e, m=ms, l=lb, ch_=choice: e.tensor_tensor(
                        mt8[2][:, ch_:ch_ + 1], mt8[1][:, 1:2], m[:, l:l + 1],
                        ALU.mult))
                vop(lambda e, l=lb: e.tensor_add(moe_g[:, l:l + 1],
                                                 mt8[2][:, 0:1], mt8[2][:, 1:2]))
            kb.wait("t", cV, cV.n)
            mm(pb[7][0:8, 0:128], moe_g[:, :], identf_sb[:, :],
               start=True, stop=True, tick=True)
            kb.wait("s", cT, cT.n)
            sop(lambda e: e.copy(moe_s1[0:8, :], pb[7][0:8, 0:128]))
            kb.wait("g", cS, cS.n)
            gdma(cA, grow[0:1, 0:1024], moe_s1[0:8, :],
                 allow_slow_non_contiguous=True)
            kb.wait("t", cA)
            for tt in range(2):
                kb.wait("t", cS, cS.n)
                mm(pb[7][:, :], onesrow_f_sb[:, :],
                   grow[0:1, 512 * tt:512 * (tt + 1)], start=True, stop=True,
                   tick=True)
                kb.wait("s", cT, cT.n)
                sop(lambda e, t=tt: e.copy(gb_sb[:, 512 * t:512 * (t + 1)],
                                           pb[7][:, :]))

        def ffn_phase():
            wcons = {}
            lni = 0
            for tt in range(2):
                gel = {0: (cV, cV.n), 1: (cV, cV.n)}
                for f in range(64):
                    li = lni + f
                    kb.wait("g", cT, wcons[li - WR] if li >= WR else cT.n)
                    gdma(cW[li % WR], wb[li % WR][:, :], w1[f][:, :])
                    ldn = cW[li % WR].n
                    bank = pb[f % 2]
                    kb.wait("t", cW[li % WR], ldn)
                    ch_, n_ = gel[f % 2]
                    kb.wait("t", ch_, n_)
                    for ec in range(NCH):
                        mm(bank[:, :], wb[li % WR][:, 128 * ec:128 * (ec + 1)],
                           cur(ec, tt), start=(ec == 0), stop=(ec == NCH - 1),
                           tick=(ec == NCH - 1))
                    wcons[li] = cT.n
                    kb.wait("s", cT, cT.n)
                    sop(lambda e, b=bank, f_=f: e.activation(
                        arenaA[:, 512 * f_:512 * (f_ + 1)], b[:, :], AF.Gelu,
                        bias=b1_sb[:, f_:f_ + 1]))
                    gel[f % 2] = (cS, cS.n)
                lni += 64
                if tt == 0:
                    moe_gates()
                kb.wait("t", cS, cS.n)
                ev2 = {0: (cV, cV.n), 1: (cV, cV.n)}
                for oh in range(NCH):
                    for qq in range(4):
                        li = lni + oh * 4 + qq
                        kb.wait("g", cT, wcons[li - WR] if li - WR in wcons else cT.n)
                        gdma(cW[li % WR], wb[li % WR][:, :],
                             w2[oh][:, 2048 * qq:2048 * (qq + 1)])
                        ldn = cW[li % WR].n
                        bank = pb[2 + oh % 2]
                        kb.wait("t", cW[li % WR], ldn)
                        if qq == 0:
                            ch_, n_ = ev2[oh % 2]
                            kb.wait("t", ch_, n_)
                        for fi in range(16):
                            fc = 16 * qq + fi
                            mm(bank[:, :], wb[li % WR][:, 128 * fi:128 * (fi + 1)],
                               arenaA[:, 512 * fc:512 * (fc + 1)],
                               start=(fc == 0), stop=(fc == 63),
                               tick=(fi == 15))
                        wcons[li] = cT.n
                    kb.wait("v", cT, cT.n)
                    vop(lambda e, b=pb[2 + oh % 2], o=oh, t=tt:
                        e.scalar_tensor_tensor(
                            wk[4][:, :], b[:, :], b2_sb[:, o:o + 1],
                            gb_sb[:, 512 * t:512 * (t + 1)], ALU.add, ALU.mult))
                    kb.wait("v", cV, cV.n)
                    vop(lambda e, o=oh, t=tt: e.tensor_add(
                        aox(o, t), wk[4][:, :], cur(o, t)))
                    ev2[oh % 2] = (cV, cV.n)
                lni += 64

        def emit_out_stream(srcfn):
            idx = 0
            for c in range(NCH):
                for tt in range(2):
                    buf = wk[idx % 2]
                    if idx >= 2:
                        kb.wait("s", cO, cO.n)
                    kb.wait("s", cV, cV.n)
                    sop(lambda e, c_=c, t=tt, b=buf: e.copy(b[:, :], srcfn(c_, t)))
                    kb.wait("y", cS, cS.n)
                    ydma(cO, y[128 * c:128 * (c + 1), 512 * tt:512 * (tt + 1)],
                         buf[:, :])
                    idx += 1
            kb.wait("y", cO, cO.n)

        def quant_chain(src, src_i, seg448, segS, out_bf, slots=None):
            if slots is None:
                Af, Bf, Cf = (arenaB_f[:, 1024:2048], arenaB_f[:, 2048:3072],
                              arenaB_f[:, 3072:4096])
                Ai, Bi, Ci = (arenaB_i[:, 1024:2048], arenaB_i[:, 2048:3072],
                              arenaB_i[:, 3072:4096])
            else:
                (Af, Ai), (Bf, Bi), (Cf, Ci) = slots
            nseg = src.shape[-1] // 128
            vop(lambda e: e.tensor_scalar(Bi, src_i, 0x7FFFFFFF, None,
                                          ALU.bitwise_and))
            vop(lambda e: e.tensor_scalar(Af, Bf, 1e-30, None, ALU.max))
            kb.wait("s", cV, cV.n)
            for sg in range(nseg):
                sop(lambda e, g=sg: e.activation(
                    Bf[:, 128 * g:128 * (g + 1)], Af[:, 128 * g:128 * (g + 1)],
                    AF.Ln, scale=seg448[g]))
            kb.wait("v", cS, cS.n)
            vop(lambda e: e.tensor_scalar(Af, Bf, INV_LN2, -0.5, ALU.mult,
                                          ALU.add))
            vop(lambda e: e.tensor_scalar(Ci, Af, -6.5, None, ALU.max))
            vop(lambda e: e.tensor_scalar(Ai, Ci, -1, 130, ALU.mult, ALU.add))
            vop(lambda e: e.tensor_scalar(Bi, Ai, 23, None,
                                          ALU.arith_shift_left))
            for sg in range(nseg):
                vop(lambda e, g=sg: e.tensor_scalar(
                    Af[:, 128 * g:128 * (g + 1)], src[:, 128 * g:128 * (g + 1)],
                    seg448[g], None, ALU.mult))
            vop(lambda e: e.tensor_tensor(src_i, Af, Bf, ALU.mult))
            vop(lambda e: e.tensor_copy(Af, src_i))
            vop(lambda e: e.tensor_scalar(Bi, Ci, 124, None, ALU.add))
            vop(lambda e: e.tensor_scalar(Ci, Bi, 23, None,
                                          ALU.arith_shift_left))
            vop(lambda e: e.tensor_tensor(Bf, Af, Cf, ALU.mult))
            for sg in range(nseg):
                vop(lambda e, g=sg: e.tensor_scalar(
                    out_bf[:, 128 * g:128 * (g + 1)], Bf[:, 128 * g:128 * (g + 1)],
                    segS[g], None, ALU.mult))

        _wbf2f = wbf[2].bitcast(F32)
        _wk2i = wk[2].bitcast(I32)
        _wk3i = wk[3].bitcast(I32)
        wq_slots = ((_wbf2f[:, 0:512], wbf[2].bitcast(I32)[:, 0:512]),
                    (wk[2][:, :], _wk2i[:, :]),
                    (wk[3][:, :], _wk3i[:, :]))

        def wquant_passA(oh):
            for hf in range(2):
                kb.wait("g", cV, cV.n)
                gdma(cA, wfb[:, 0:1024],
                     wf1[oh][:, 1024 * hf:1024 * (hf + 1)])
                kb.wait("v", cA)
                for sg in range(8):
                    vop(lambda e, g=sg, h_=hf: e.reduce_max(
                        wq_cm[:, 8 * h_ + g:8 * h_ + g + 1],
                        wfb[:, 128 * g:128 * (g + 1)], AX.X,
                        apply_absolute_value=True))
            kb.wait("t", cV, cV.n)
            kb.wait("t", cS, cS.n)
            mm(pb[7][0:16, 0:128], wq_cm[:, :], identf_sb[:, :],
               start=True, stop=True, tick=True)
            kb.wait("v", cT, cT.n)
            vop(lambda e: e.tensor_copy(moe_s1[0:16, :], pb[7][0:16, 0:128]))
            vop(lambda e: e.reduce_max(mt8[0][0:16, 0:1], moe_s1[0:16, :],
                                       AX.X))
            vop(lambda e: e.tensor_scalar(mt8[0][0:16, 1:2],
                                          mt8[0][0:16, 0:1], 1e-4, None,
                                          ALU.max))
            kb.wait("g", cV, cV.n)
            gdma(cA, wq_rows[0:1, 16 * oh:16 * (oh + 1)],
                 mt8[0][0:16, 1:2], allow_slow_non_contiguous=True)

        def wquant_prep():
            kb.wait("v", cA)
            vop(lambda e: e.reciprocal(wq_rows[0:1, 256:288], wq_rows[0:1, 0:32]))
            vop(lambda e: e.tensor_scalar(wq_rows[0:1, 256:288], wq_rows[0:1, 256:288], 448.0,
                                          None, ALU.mult))
            vop(lambda e: e.tensor_scalar(wq_rows[0:1, 0:32], wq_rows[0:1, 0:32],
                                          float(1.0 / 448.0), None, ALU.mult))
            kb.wait("t", cV, cV.n)
            mm(pb[7][:, 0:32], onesrow_f_sb[:, :], wq_rows[0:1, 256:288], start=True,
               stop=True, tick=True)
            kb.wait("s", cT, cT.n)
            sop(lambda e: e.copy(wq_rb448[:, 0:32], pb[7][:, 0:32]))
            kb.wait("t", cS, cS.n)
            mm(pb[7][:, 0:32], onesrow_f_sb[:, :], wq_rows[0:1, 0:32], start=True,
               stop=True, tick=True)
            kb.wait("s", cT, cT.n)
            sop(lambda e: e.copy(wq_s[:, 0:32], pb[7][:, 0:32]))
            kb.wait("v", cS, cS.n)

        def wquant_passB(oh):
            for hf in range(2):
                kb.wait("g", cV, cV.n)
                gdma(cA, wfb[:, 0:1024],
                     wf1[oh][:, 1024 * hf:1024 * (hf + 1)])
                kb.wait("v", cA)
                out_bf = wbf[3 + hf]
                kb.wait("v", cO, cO.n)
                for q2 in range(2):
                    seg448 = [wq_rb448[:, 16 * oh + 8 * hf + 4 * q2 + g:
                                       16 * oh + 8 * hf + 4 * q2 + g + 1]
                              for g in range(4)]
                    segS = [wq_s[:, 16 * oh + 8 * hf + 4 * q2 + g:
                                 16 * oh + 8 * hf + 4 * q2 + g + 1]
                            for g in range(4)]
                    quant_chain(wfb[:, 512 * q2:512 * (q2 + 1)],
                                wfb_i[:, 512 * q2:512 * (q2 + 1)], seg448, segS,
                                out_bf[:, 512 * q2:512 * (q2 + 1)],
                                slots=wq_slots)
                kb.wait("y", cV, cV.n)
                ydma(cO, wdq_loc[128 * oh:128 * (oh + 1),
                                 1024 * hf:1024 * (hf + 1)], out_bf[:, :])

        def xquant_block(tb, hf):
            tt, co = (tb // 4, 128 * (tb % 4))
            for g in range(8):
                c = 8 * hf + g
                kb.wait("t", cV, cV.n)
                mm(pb[6][:, 0:128],
                   arenaA_f[:, 1024 * c + 512 * tt + co:
                            1024 * c + 512 * tt + co + 128],
                   identf_sb[:, :], start=True, stop=True, tick=True)
                kb.wait("v", cT, cT.n)
                vop(lambda e, g_=g: e.tensor_copy(
                    arenaB_f[:, 128 * g_:128 * (g_ + 1)], pb[6][:, 0:128]))
            for g in range(8):
                vop(lambda e, g_=g: e.reduce_max(
                    q_am[:, g_:g_ + 1], arenaB_f[:, 128 * g_:128 * (g_ + 1)],
                    AX.X, apply_absolute_value=True))
            vop(lambda e: e.tensor_scalar(q_am[:, 0:8], q_am[:, 0:8], 1e-4,
                                          None, ALU.max))
            vop(lambda e: e.reciprocal(q_rc448[:, 0:8], q_am[:, 0:8]))
            vop(lambda e: e.tensor_scalar(q_rc448[:, 0:8], q_rc448[:, 0:8],
                                          448.0, None, ALU.mult))
            vop(lambda e: e.tensor_scalar(q_s[:, 0:8], q_am[:, 0:8],
                                          float(1.0 / 448.0), None, ALU.mult))
            seg448 = [q_rc448[:, g:g + 1] for g in range(8)]
            segS = [q_s[:, g:g + 1] for g in range(8)]
            quant_chain(arenaB_f[:, 0:1024], arenaB_i[:, 0:1024], seg448, segS,
                        wbf[0][:, :])
            for g in range(8):
                c = 8 * hf + g
                kb.wait("t", cV, cV.n)
                mm(pb[6][:, 0:128], wbf[0][:, 128 * g:128 * (g + 1)],
                   identb_sb[:, :], start=True, stop=True, tick=True)
                kb.wait("v", cT, cT.n)
                vop(lambda e, c_=c: e.tensor_copy(
                    xB[:, 1024 * c_ + 512 * tt + co:
                       1024 * c_ + 512 * tt + co + 128], pb[6][:, 0:128]))

        def wquant_gather():
            kb.wait("g", cO, cO.n)
            kb.tick("g", cCC, lambda e: e.collective_compute(
                "AllGather", ALU.bypass,
                replica_groups=[[0, 1, 2, 3, 4, 5, 6, 7]],
                ins=[wdq_loc], outs=[wdqG]))

        def ffn1_phase(agw):
            wcons = {}
            kb.wait("g", cCC, agw)
            hf_tick = {}
            for hf in range(2):
                for tb in range(4):
                    xquant_block(tb, hf)
                hf_tick[(0, hf)] = cV.n
            for tt in range(2):
                kb.wait("t", cV, cV.n)
                ev = {0: (cV, cV.n), 1: (cV, cV.n)}
                for oh in range(NCH):
                    li = tt * NCH + oh
                    kb.wait("g", cT, wcons[li - WR] if li - WR in wcons else cT.n)
                    gdma(cW[li % WR], wb[li % WR][:, :],
                         wdqG[128 * oh:128 * (oh + 1), :])
                    ldn = cW[li % WR].n
                    bank = pb[tt]
                    kb.wait("t", cW[li % WR], ldn)
                    ch_, n_ = ev[tt]
                    kb.wait("t", ch_, n_)
                    for ec in range(NCH):
                        if ec % 8 == 0:
                            kb.wait("t", cV, hf_tick[(tt, ec // 8)])
                        mm(bank[:, :], wb[li % WR][:, 128 * ec:128 * (ec + 1)],
                           xB[:, 1024 * ec + 512 * tt:1024 * ec + 512 * (tt + 1)],
                           start=(ec == 0), stop=(ec == NCH - 1),
                           tick=(ec == NCH - 1))
                    wcons[li] = cT.n
                    kb.wait("s", cT, cT.n)
                    sop(lambda e, b=bank: e.copy(wbf[1][:, 0:512], b[:, :]))
                    kb.wait("v", cS, cS.n)
                    vop(lambda e, o=oh, t=tt: e.tensor_add(
                        cur(o, t), arenaA_f[:, 1024 * o + 512 * t:
                                            1024 * o + 512 * (t + 1)],
                        wbf[1][:, 0:512]))
                    ev[tt] = (cV, cV.n)
                    if tt == 0 and oh in (1, 3):
                        hf = (oh - 1) // 2
                        for tb in range(4 + 0, 8):
                            xquant_block(tb, hf)
                        hf_tick[(1, hf)] = cV.n

        def final_ln():
            kb.wait("t", cV, cV.n)
            kb.wait("s", cV, cV.n)
            prev_tick = cT.n
            for c in range(NCH):
                for tt in range(2):
                    kb.wait("s", cT, prev_tick)
                    sop(lambda e, c_=c, t=tt: e.activation(
                        wbf[0][:, 512 * t:512 * (t + 1)], cur(c_, t), AF.Square))
                    sqn = cS.n
                    mm(pb[tt][:, :], ones128_sb[:, :], cur(c, tt),
                       start=(c == 0), stop=(c == NCH - 1))
                    kb.wait("t", cS, sqn)
                    mm(pb[2 + tt][:, :], ones128_sb[:, :],
                       wbf[0][:, 512 * tt:512 * (tt + 1)],
                       start=(c == 0), stop=(c == NCH - 1), tick=True)
                    prev_tick = cT.n
            stn = cT.n
            for tt in range(2):
                nr = wk[3] if tt == 0 else wk[5]
                kb.wait("v", cT, stn)
                vop(lambda e, t=tt: e.tensor_scalar(
                    wk[0][:, :], pb[2 + t][:, :], RDIV, None, ALU.mult))
                vop(lambda e, t=tt: e.tensor_scalar(
                    wk[2][:, :], pb[t][:, :], RDIV, None, ALU.mult))
                kb.wait("s", cV, cV.n)
                sop(lambda e: e.activation(wk[1][:, :], wk[2][:, :], AF.Square))
                kb.wait("v", cS, cS.n)
                vop(lambda e: e.tensor_sub(wk[0][:, :], wk[0][:, :], wk[1][:, :]))
                kb.wait("s", cV, cV.n)
                sop(lambda e: e.activation(wk[1][:, :], wk[0][:, :], AF.Ln,
                                           bias=epsT[:, :]))
                sop(lambda e: e.activation(wk[1][:, :], wk[1][:, :], AF.Exp,
                                           scale=-0.5))
                kb.wait("v", cS, cS.n)
                vop(lambda e, o=nr: e.tensor_scalar(
                    o[:, :], wk[1][:, :], -1.0, None, ALU.mult))
                for c in range(NCH):
                    buf = wk[2] if c % 2 == 0 else wk[4]
                    vop(lambda e, c_=c, t=tt: e.scalar_tensor_tensor(
                        wk[1][:, :], pb[t][:, :], RDIV, cur(c_, t),
                        ALU.mult, ALU.subtract))
                    kb.wait("v", cO, cO.n)
                    vop(lambda e, b=buf, o=nr: e.tensor_mul(
                        b[:, :], wk[1][:, :], o[:, :]))
                    kb.wait("y", cV, cV.n)
                    ydma(cO, y[128 * c:128 * (c + 1), 512 * tt:512 * (tt + 1)],
                         buf[:, :])
            kb.wait("y", cO, cO.n)

        if stage == "ld":
            emit_out_stream(cur)
        elif stage in ("kv", "kvt", "kvnc"):
            kv_phase(0, do_vt=(stage == "kvt"), do_cc=(stage != "kvnc"))
            emit_out_stream(cur)
        else:
            _, agk0, vtn0, kvt0 = kv_phase(0)
            AO0 = attn_phase(0, agk0, vtn0, kvt0)
            if stage == "attn":
                emit_out_stream(lambda c, t: AO0[:, 1024 * c + 512 * t:
                                                 1024 * c + 512 * (t + 1)])
            else:
                wo_phase(0, AO0)
                ln_phase(cur)
                if stage == "l0a":
                    emit_out_stream(cur)
                else:
                    moe_pre()
                    ffn_phase()
                    ln_phase(cur, src=aox)
                    if stage == "l0f":
                        emit_out_stream(cur)
                    else:
                        _, agk1, vtn1, kvt1 = kv_phase(1)
                        side = [
                            [(lambda: wquant_passA(0))],
                            [(lambda: wquant_passA(1))],
                            [wquant_prep],
                            [(lambda: wquant_passB(0))],
                            [(lambda: wquant_passB(1))],
                            [wquant_gather],
                        ]
                        AO1 = attn_phase(1, agk1, vtn1, kvt1, side=side)
                        agw = cCC.n
                        wo_phase(1, AO1)
                        x3f = lambda c, t: arenaA_f[:, 1024 * c + 512 * t:
                                                    1024 * c + 512 * (t + 1)]
                        ln_phase(x3f, src=rr, desc=True)
                        if stage == "l1a":
                            emit_out_stream(x3f)
                        elif stage == "xd":
                            for tb in range(8):
                                for hf in range(2):
                                    xquant_block(tb, hf)
                            emit_out_stream(lambda c, t: xB[:, 1024 * c + 512 * t:
                                                            1024 * c + 512 * (t + 1)])
                        else:
                            ffn1_phase(agw)
                            final_ln()

        with nc.Block() as block:
            block.tensor(lambda e: [f(e) for f in kb.q["t"]])
            block.vector(lambda e: [f(e) for f in kb.q["v"]])
            block.scalar(lambda e: [f(e) for f in kb.q["s"]])
            block.gpsimd(lambda e: [f(e) for f in kb.q["g"]])
            block.sync(lambda e: [f(e) for f in kb.q["y"]])

    return nc



def _tile_lhsT(Wct, fh_chunks):
    C, F = Wct.shape
    A = Wct.reshape(C // 128, 128, F // 128, 128).transpose(2, 1, 0, 3)
    return np.ascontiguousarray(A.reshape(F // 128, 128, C)).astype(np.float32)


def _bias_cols(b):
    return np.ascontiguousarray(b.reshape(-1, 128).T).astype(np.float32)


def make_core_inputs(inp, c):
    b, p = c // 2, c % 2
    cols = np.concatenate([np.arange(128 * (2 * lb + p), 128 * (2 * lb + p) + 128)
                           for lb in range(8)])
    x = np.asarray(inp["L0_attn_x"], np.float32)[b][cols]
    d = {}
    d["x0"] = np.ascontiguousarray(x.T)
    d["wq0"] = _tile_lhsT(np.asarray(inp["L0_attn_Wq"], np.float32).T, H)
    d["wk0"] = _tile_lhsT(np.asarray(inp["L0_attn_Wk"], np.float32).T, H)
    d["wv0"] = _tile_lhsT(np.asarray(inp["L0_attn_Wv"], np.float32).T, H)
    d["wo0"] = _tile_lhsT(np.asarray(inp["L0_attn_Wo"], np.float32).T, NCH)
    d["bq_c"] = _bias_cols(np.asarray(inp["L0_attn_bq"], np.float32))
    d["bk_c"] = _bias_cols(np.asarray(inp["L0_attn_bk"], np.float32))
    d["bv_c"] = _bias_cols(np.asarray(inp["L0_attn_bv"], np.float32))
    d["bo_c"] = _bias_cols(np.asarray(inp["L0_attn_bo"], np.float32))
    d["b1_c"] = _bias_cols(np.asarray(inp["L0_ffn_b1"], np.float32))
    d["b2_c"] = _bias_cols(np.asarray(inp["L0_ffn_b2"], np.float32))
    rwt = np.asarray(inp["L0_ffn_router_w"], np.float32).T
    d["rw"] = np.ascontiguousarray(
        rwt.reshape(NCH, 128, 8).transpose(1, 0, 2).reshape(128, NCH * 8)).astype(np.float32)
    d["w1"] = _tile_lhsT(np.asarray(inp["L0_ffn_w1"], np.float32).T, 64)
    d["w2"] = _tile_lhsT(np.asarray(inp["L0_ffn_w2"], np.float32).T, NCH)
    d["wq1"] = _tile_lhsT(np.asarray(inp["L1_attn_Wq"], np.float32), H)
    d["wk1"] = _tile_lhsT(np.asarray(inp["L1_attn_Wk"], np.float32), H)
    d["wv1"] = _tile_lhsT(np.asarray(inp["L1_attn_Wv"], np.float32), H)
    d["wo1"] = _tile_lhsT(np.asarray(inp["L1_attn_Wo"], np.float32), H)
    _wf1_all = _tile_lhsT(np.asarray(inp["L1_ffn_weight"], np.float32).T, NCH)
    d["wf1"] = np.ascontiguousarray(_wf1_all[[2 * c, 2 * c + 1]])

    pos = (cols).astype(np.float64)
    theta = np.repeat(-np.arange(0, 128, 2, dtype=np.float32) / 128.0, 2)
    theta = np.power(np.float32(10000.0), theta.astype(np.float32)).astype(np.float64)
    ang = pos[None, :] * 0 + 0.0
    ang = theta[:, None] * pos[None, :]
    d["ropec"] = np.cos(ang).astype(ml_dtypes.bfloat16)
    d["ropes"] = np.sin(ang).astype(ml_dtypes.bfloat16)
    P = np.zeros((128, 128), np.float32)
    for i2 in range(0, 128, 2):
        P[i2, i2 + 1] = -1.0
        P[i2 + 1, i2] = 1.0
    d["p128"] = np.ascontiguousarray(P.T).astype(ml_dtypes.bfloat16)
    d["identb"] = np.eye(128, dtype=np.float32).astype(ml_dtypes.bfloat16)
    d["identf"] = np.eye(128, dtype=np.float32)
    d["ones128"] = np.ones((128, 128), ml_dtypes.bfloat16)
    d["onesrow"] = np.ones((1, 128), ml_dtypes.bfloat16)
    d["onesrow_f"] = np.ones((1, 128), np.float32)
    d["u128"] = np.triu(np.ones((128, 128), np.float32)).astype(ml_dtypes.bfloat16)
    d["c1024"] = np.full((1, 128), 1024.0, ml_dtypes.bfloat16)

    tri_strict = np.tril(np.ones((128, 128), np.float32), -1)
    m0 = np.zeros((16, 128, 128), np.float32)
    m1 = np.zeros((16, 128, 128), np.float32)
    for k in range(16):
        j = 0 if k < 8 else 1
        i_min = max(0, int(math.ceil((k - 8 * j - 1) / 2.0)))
        qa = 8 * j + 2 * i_min + p
        if qa == k:
            m0[k] = tri_strict * -30000.0
            m1[k] = np.triu(np.ones((128, 128), np.float32), 0)
        elif qa < k:
            m0[k] = -30000.0
            m1[k] = 0.0
        else:
            m0[k] = 0.0
            m1[k] = 1.0
    d["m0t"] = np.ascontiguousarray(
        m0.transpose(1, 0, 2).reshape(128, 16 * 128)).astype(ml_dtypes.bfloat16)
    d["m1t"] = np.ascontiguousarray(
        m1.transpose(1, 0, 2).reshape(128, 16 * 128)).astype(ml_dtypes.bfloat16)

    myabs = [(b * 16 + 2 * lb + p) for lb in range(8)]
    Mb = np.zeros((128, 16), np.float32)
    for r in range(8):
        a_r = myabs[r]
        for core in range(8):
            for blk in range(8):
                a = (core // 2) * 16 + 2 * blk + (core % 2)
                if a < a_r:
                    Mb[core * 8 + blk, r] = 1.0
                    Mb[64 + core * 8 + blk, 8 + r] = 1.0
        Mb[0:64, 8 + r] = 1.0
    d["mbig"] = Mb.astype(ml_dtypes.bfloat16)
    return d


def kernel(**inputs):
    nc = build(stage="full")
    in_maps = [make_core_inputs(inputs, c) for c in range(8)]
    res = run_bass_kernel_spmd(nc, in_maps, core_ids=list(range(8)))
    out = np.zeros((B, S, E), np.float32)
    for c in range(8):
        b, p = c // 2, c % 2
        yt = res.results[c]["y"]
        xt = np.ascontiguousarray(yt.T)
        for lb in range(8):
            out[b, 128 * (2 * lb + p):128 * (2 * lb + p) + 128, :] = \
                xt[128 * lb:128 * (lb + 1)]
    return out


# version 14
# speedup vs baseline: 1.2183x; 1.0086x over previous
import math
import contextlib
import numpy as np
import ml_dtypes
import concourse.bass as bass
import concourse.mybir as mybir
from concourse.bass_utils import run_bass_kernel_spmd

F32 = mybir.dt.float32
BF16 = mybir.dt.bfloat16
I32 = mybir.dt.int32
AF = mybir.ActivationFunctionType
ALU = mybir.AluOpType
AX = mybir.AxisListType

B, S, E, H, D = 4, 2048, 2048, 16, 128
T = 1024
NCH = 16
LN2C = float(np.log(2.0))
INV_LN2 = float(1.0 / np.log(2.0))
LN2 = float(np.log(2.0))
SC_L0 = float(1.0 / np.sqrt(128.0))
SC_L1 = float(1.0 / np.sqrt(2048.0))
RDIV = float(1.0 / 2048.0)


def c0_of(j, k):
    if k < 8 * j:
        return 0
    return 128 * max(0, int(math.ceil((k - 8 * j - 1) / 2.0)))


def kcol(k):
    return (k % 2) * 1024 + (k // 2) * 128


def vtslot(k):
    return (k % 2) * 8 + (k // 2)


class Ch:
    def __init__(self, stack, nc, name, step=1):
        self.h = stack.enter_context(nc.semaphore(name))
        self.n = 0
        self.step = step


class KB:
    def __init__(self, nc):
        self.q = {k: [] for k in "tvsgy"}

    def on(self, ek, fn):
        self.q[ek].append(fn)

    def wait(self, ek, ch, n=None):
        n = ch.n if n is None else n
        if n > 0:
            self.on(ek, lambda e, h=ch.h, m=n: e.wait_ge(h, m))

    def tick(self, ek, ch, fn):
        ch.n += ch.step
        self.on(ek, lambda e, f=fn, h=ch.h, s=ch.step: f(e).then_inc(h, s))


def build(stage="full"):
    nc = bass.Bass("TRN2", target_bir_lowering=False, debug=False)

    def din(name, shape, dtype=F32):
        return nc.dram_tensor(name, list(shape), dtype, kind="ExternalInput").ap()

    x0 = din("x0", (E, T))
    wq0 = din("wq0", (H, 128, E)); wk0 = din("wk0", (H, 128, E))
    wv0 = din("wv0", (H, 128, E)); wo0 = din("wo0", (NCH, 128, E))
    bq_c = din("bq_c", (128, NCH)); bk_c = din("bk_c", (128, NCH))
    bv_c = din("bv_c", (128, NCH)); bo_c = din("bo_c", (128, NCH))
    b1_c = din("b1_c", (128, 64)); b2_c = din("b2_c", (128, NCH))
    rw = din("rw", (128, NCH * 8))
    w1 = din("w1", (64, 128, E))
    w2 = din("w2", (NCH, 128, 8192))
    wq1 = din("wq1", (H, 128, E)); wk1 = din("wk1", (H, 128, E))
    wv1 = din("wv1", (H, 128, E)); wo1 = din("wo1", (NCH, 128, E))
    wf1 = din("wf1", (2, 128, E))
    ropec = din("ropec", (128, T), BF16); ropes = din("ropes", (128, T), BF16)
    p128 = din("p128", (128, 128), BF16)
    identb = din("identb", (128, 128), BF16); identf = din("identf", (128, 128))
    ones128 = din("ones128", (128, 128), BF16)
    onesrow = din("onesrow", (1, 128), BF16); onesrow_f = din("onesrow_f", (1, 128))
    m0t = din("m0t", (128, 16 * 128), BF16); m1t = din("m1t", (128, 16 * 128), BF16)
    u128 = din("u128", (128, 128), BF16)
    mbig = din("mbig", (128, 16), BF16)
    c1024 = din("c1024", (1, 128), BF16)

    y = nc.dram_tensor("y", [E, T], F32, kind="ExternalOutput").ap()

    ccT = {}
    for L in (0, 1):
        for qtr in range(4):
            ccT[(L, "ki", qtr)] = nc.dram_tensor(
                f"cc{L}K{qtr}_in", [512, T], BF16, kind="Internal").ap()
            ccT[(L, "ko", qtr)] = nc.dram_tensor(
                f"cc{L}K{qtr}_out", [1024, T], BF16, kind="Internal").ap()
            ccT[(L, "vi", qtr)] = nc.dram_tensor(
                f"cc{L}V{qtr}_in", [E, 256], BF16, kind="Internal").ap()
            ccT[(L, "vo", qtr)] = nc.dram_tensor(
                f"cc{L}V{qtr}_out", [2 * E, 256], BF16, kind="Internal").ap()
    cnt_in = nc.dram_tensor("cnt_in", [1, 128], F32, kind="Internal").ap()
    cnt_out = nc.dram_tensor("cnt_out", [8, 128], F32, kind="Internal").ap()
    wdq_loc = nc.dram_tensor("wdq_loc", [256, E], BF16, kind="Internal").ap()
    wdqG = nc.dram_tensor("wdqG", [NCH * 128, E], BF16, kind="Internal").ap()

    stack = contextlib.ExitStack()
    with stack:
        sb = lambda name, shape, dtype: stack.enter_context(
            nc.sbuf_tensor(name, list(shape), dtype))

        xA = sb("xA", (128, NCH * 1024), BF16)
        xB = sb("xB", (128, NCH * 1024), BF16)
        arenaA = sb("arenaA", (128, 32768), BF16)
        arenaB = sb("arenaB", (128, 8192), BF16)
        arenaA_f = arenaA.bitcast(F32)
        arenaB_f = arenaB.bitcast(F32)
        arenaB_i = arenaB.bitcast(I32)

        kbuf = [sb(f"kbuf{i}", (128, 2048), BF16) for i in range(2)]
        qh = sb("qh", (128, 1024), BF16)
        WR = 3
        wb = [sb(f"wb{i}", (128, 2048), BF16) for i in range(WR)]
        wb2 = wb
        wfb = sb("wfb", (128, 1024), F32)
        gb_sb = kbuf[0].bitcast(F32)
        grow = kbuf[1].bitcast(F32)
        ropec_sb = sb("ropec_sb", (128, T), BF16)
        ropes_sb = sb("ropes_sb", (128, T), BF16)
        p128_sb = sb("p128_sb", (128, 128), BF16)
        identb_sb = sb("identb_sb", (128, 128), BF16)
        identf_sb = sb("identf_sb", (128, 128), F32)
        ones128_sb = sb("ones128_sb", (128, 128), BF16)
        onesrow_sb = sb("onesrow_sb", (1, 128), BF16)
        onesrow_f_sb = sb("onesrow_f_sb", (1, 128), F32)
        allones = sb("allones", (128, 512), BF16)
        m_sb = sb("m_sb", (128, 16 * 128), BF16)
        u128_sb = sb("u128_sb", (128, 128), BF16)
        mbig_sb = sb("mbig_sb", (128, 16), BF16)
        c1024_sb = sb("c1024_sb", (1, 128), BF16)
        bq_sb = sb("bq_sb", (128, NCH), F32)
        bk_sb = sb("bk_sb", (128, NCH), F32)
        bv_sb = sb("bv_sb", (128, NCH), F32)
        bo_sb = sb("bo_sb", (128, NCH), F32)
        b1_sb = sb("b1_sb", (128, 64), F32)
        b2_sb = sb("b2_sb", (128, NCH), F32)
        rw_sb = sb("rw_sb", (128, NCH * 8), BF16)
        wbf = [sb(f"wbf{i}", (128, 1024), BF16) for i in range(5)]
        wk = [sb(f"wkt{i}", (128, 512), F32) for i in range(4)]
        wk.append(wbf[0].bitcast(F32))
        wk.append(sb("wkt5", (128, 512), F32))
        epsT = sb("epsT", (128, 1), F32)
        b3l2T = sb("b3l2T", (128, 1), F32)
        bm3l2T = sb("bm3l2T", (128, 1), F32)
        moe_g = sb("moe_g", (128, 8), F32)
        moe_m1 = sb("moe_m1", (128, 8), F32)
        moe_m2 = sb("moe_m2", (128, 8), F32)
        moe_oh1 = sb("moe_oh1", (128, 8 * 8), BF16)
        moe_oh2 = sb("moe_oh2", (128, 8 * 8), BF16)
        mt8 = [sb(f"mt8_{i}", (128, 8), F32) for i in range(4)]
        cntrow = sb("cntrow", (1, 128), F32)
        cnt_sb1 = sb("cnt_sb1", (128, 8), BF16)
        cnt_sb2 = sb("cnt_sb2", (128, 8), BF16)
        offrow = sb("offrow", (1, 128), F32)
        moe_s1 = sb("moe_s1", (16, 128), F32)
        q_am = sb("q_am", (128, 16), F32)
        q_rc448 = sb("q_rc448", (128, 16), F32)
        q_s = sb("q_s", (128, 16), F32)
        wq_cm = sb("wq_cm", (128, 16), F32)
        wq_rows = sb("wq_rows", (1, 512), F32)
        wq_amrow = wq_rows[0:1, 0:256]
        wq_row2 = wq_rows[0:1, 256:512]
        _wbf1f = wbf[1].bitcast(F32)
        wq_rb448 = _wbf1f[:, 0:256]
        wq_s = _wbf1f[:, 256:512]
        wfb_i = wfb.bitcast(I32)

        pb = [stack.enter_context(nc.psum_tensor(f"pb{i}", [128, 512], F32))
              for i in range(8)]

        assert nc.sbuf_bytes_remaining >= 0, nc.sbuf_bytes_remaining
        assert nc.psum_banks_remaining >= 0

        kb = KB(nc)
        cC = Ch(stack, nc, "cC", 16)
        cX = Ch(stack, nc, "cX", 16)
        cW = [Ch(stack, nc, f"cW{i}", 16) for i in range(4)]
        cSg = [Ch(stack, nc, f"cSg{i}", 16) for i in range(2)]
        cK = [Ch(stack, nc, f"cK{i}", 16) for i in range(2)]
        cVT = Ch(stack, nc, "cVT", 16)
        cM = Ch(stack, nc, "cM", 16)
        cO = Ch(stack, nc, "cO", 16)
        cA = Ch(stack, nc, "cA", 16)
        cT = Ch(stack, nc, "cT", 1)
        cV = Ch(stack, nc, "cV", 1)
        cS = Ch(stack, nc, "cS", 1)
        cCC = Ch(stack, nc, "cCC", 1)

        def ydma(ch, out, in_, **kw):
            kb.tick("y", ch, lambda e, o=out, i=in_, k=kw: e.dma_start(out=o, in_=i, **k))

        def gdma(ch, out, in_, **kw):
            kb.tick("g", ch, lambda e, o=out, i=in_, k=kw: e.dma_start(out=o, in_=i, **k))

        def mm(out, lhsT, rhs, start, stop, tick=False):
            fn = lambda e, o=out, l=lhsT, r=rhs, s1=start, s2=stop: \
                e.matmul(o, l, r, start=s1, stop=s2, skip_group_check=True)
            if tick:
                kb.tick("t", cT, fn)
            else:
                kb.on("t", fn)

        def vop(fn, tick=True):
            kb.wait("v", cV, cV.n)
            kb.tick("v", cV, fn)

        def sop(fn, tick=True):
            kb.wait("s", cS, cS.n)
            kb.tick("s", cS, fn)

        for dst, src in [
            (ropec_sb, ropec), (ropes_sb, ropes), (identb_sb, identb),
            (identf_sb, identf), (ones128_sb, ones128), (onesrow_sb, onesrow),
            (onesrow_f_sb, onesrow_f), (p128_sb, p128),
            (u128_sb, u128), (mbig_sb, mbig),
            (c1024_sb, c1024), (bq_sb, bq_c), (bk_sb, bk_c), (bv_sb, bv_c),
            (bo_sb, bo_c), (b1_sb, b1_c), (b2_sb, b2_c),
        ]:
            ydma(cC, dst[:, :], src[:, :])
        n_const = cC.n
        gdma(cX, rw_sb[:, :], rw[:, :])
        for c in range(NCH):
            gdma(cX, xA[:, 1024 * c:1024 * (c + 1)], x0[128 * c:128 * (c + 1), :])
        vop(lambda e: e.memset(epsT[:, :], 1e-5))
        vop(lambda e: e.memset(b3l2T[:, :], float(3.0 * np.log(2.0))))
        vop(lambda e: e.memset(bm3l2T[:, :], float(-3.0 * np.log(2.0))))
        vop(lambda e: e.memset(allones[:, :], 1.0), tick=True)
        kb.wait("t", cC, n_const)
        kb.wait("v", cC, n_const)
        kb.wait("s", cC, n_const)
        kb.wait("t", cX)

        cur = lambda c, t=None: xA[:, 1024 * c + (0 if t is None else 512 * t):
                                   1024 * c + (1024 if t is None else 512 * (t + 1))]
        aox = lambda c, t=None: xB[:, 1024 * c + (0 if t is None else 512 * t):
                                   1024 * c + (1024 if t is None else 512 * (t + 1))]
        rr = lambda c, t=None: arenaA[:, 1024 * c + (0 if t is None else 512 * t):
                                      1024 * c + (1024 if t is None else 512 * (t + 1))]

        def proj_mms(bank, wslot, tt, ecw=None, fullw=None):
            if fullw is not None:
                kb.wait("t", cV, fullw)
            for ec in range(NCH):
                if ecw is not None:
                    kb.wait("t", cV, ecw[(ec, tt)])
                mm(bank[:, :], wslot[:, 128 * ec:128 * (ec + 1)],
                   cur(ec, tt), start=(ec == 0), stop=(ec == NCH - 1),
                   tick=(ec == NCH - 1))

        def kv_phase(layer, do_vt=True, do_cc=True, lnt=None, bankfree=None):
            wkw = wk0 if layer == 0 else wk1
            wvw = wv0 if layer == 0 else wv1
            grp = [[0, 1], [2, 3], [4, 5], [6, 7]]
            steps = ([("k", h) for h in range(H)] + [("v", h) for h in range(H)])
            cons = {}
            dmad = {}
            entry_t = cT.n
            agk = [0, 0, 0, 0]
            bf0 = bankfree if bankfree is not None else cV.n
            bank_free = {0: (cV, bf0), 1: (cV, bf0)}
            for i, (kind, h) in enumerate(steps):
                kb.wait("g", cT, cons[i - WR] if i >= WR else entry_t)
                gdma(cW[i % WR], wb[i % WR][:, :], (wkw if kind == "k" else wvw)[h][:, :])
                ldn = cW[i % WR].n
                stg = wbf[3 + (i % 2)]
                if i >= 2:
                    kb.wait("s", cSg[i % 2], dmad[i - 2])
                    kb.wait("v", cSg[i % 2], dmad[i - 2])
                for tt in range(2):
                    bank = pb[tt]
                    kb.wait("t", cW[i % WR], ldn)
                    ch_, n_ = bank_free[tt]
                    kb.wait("t", ch_, n_)
                    if lnt is not None and i < 8:
                        proj_mms(bank, wb[i % WR], tt,
                                 ecw=(lnt if i == 0 else None),
                                 fullw=(lnt[(NCH - 1, 1)] if i == 1 else None))
                    else:
                        proj_mms(bank, wb[i % WR], tt)
                    mmn = cT.n
                    if layer == 0 and kind == "k":
                        bcol = bk_sb[:, h:h + 1]
                        kb.wait("s", cT, mmn)
                        kb.wait("s", cV, cV.n)
                        sop(lambda e, b=bank, t=tt, bc=bcol: e.activation(
                            wbf[0][:, 512 * t:512 * (t + 1)], b[:, :], AF.Identity,
                            bias=bc))
                        kb.wait("v", cS, cS.n)
                        vop(lambda e, t=tt: e.tensor_mul(
                            wbf[1][:, 512 * t:512 * (t + 1)],
                            wbf[0][:, 512 * t:512 * (t + 1)],
                            ropec_sb[:, 512 * t:512 * (t + 1)]))
                        vop(lambda e, t=tt: e.tensor_mul(
                            wbf[2][:, 512 * t:512 * (t + 1)],
                            wbf[0][:, 512 * t:512 * (t + 1)],
                            ropes_sb[:, 512 * t:512 * (t + 1)]))
                        kb.wait("t", cV, cV.n)
                        mm(pb[7][:, :], p128_sb[:, :],
                           wbf[2][:, 512 * tt:512 * (tt + 1)], True, True,
                           tick=True)
                        kb.wait("v", cT, cT.n)
                        vop(lambda e, t=tt, o=stg: e.tensor_add(
                            o[:, 512 * t:512 * (t + 1)],
                            wbf[1][:, 512 * t:512 * (t + 1)], pb[7][:, :]))
                        bank_free[tt] = (cS, cS.n)
                    else:
                        kb.wait("s", cT, mmn)
                        if layer == 0:
                            bcol = bv_sb[:, h:h + 1]
                            sop(lambda e, b=bank, o=stg, t=tt, bc=bcol:
                                e.activation(o[:, 512 * t:512 * (t + 1)], b[:, :],
                                             AF.Identity, bias=bc))
                        else:
                            sop(lambda e, b=bank, o=stg, t=tt: e.copy(
                                o[:, 512 * t:512 * (t + 1)], b[:, :]))
                        bank_free[tt] = (cS, cS.n)
                cons[i] = cT.n
                if layer == 0 and kind == "k":
                    kb.wait("y", cV, cV.n)
                else:
                    kb.wait("y", cS, cS.n)
                if kind == "k":
                    qtr = h // 4
                    dst = ccT[(layer, "ki", qtr)]
                    ydma(cSg[i % 2], dst[128 * (h % 4):128 * (h % 4) + 128, :],
                         stg[:, :])
                    dmad[i] = cSg[i % 2].n
                    if h % 4 == 3 and do_cc:
                        kb.wait("g", cSg[0])
                        kb.wait("g", cSg[1])
                        kb.tick("g", cCC, lambda e, ci=dst,
                                co=ccT[(layer, "ko", qtr)]: e.collective_compute(
                                    "AllGather", ALU.bypass, replica_groups=grp,
                                    ins=[ci], outs=[co]))
                        agk[qtr] = cCC.n
                else:
                    for vh in range(4):
                        ydma(cSg[i % 2],
                             ccT[(layer, "vi", vh)][128 * h:128 * (h + 1), :],
                             stg[:, 256 * vh:256 * (vh + 1)])
                    dmad[i] = cSg[i % 2].n
            agv = [0, 0, 0, 0]
            for vh in range(4):
                kb.wait("g", cSg[0])
                kb.wait("g", cSg[1])
                if do_cc:
                    kb.tick("g", cCC, lambda e, ci=ccT[(layer, "vi", vh)],
                            co=ccT[(layer, "vo", vh)]: e.collective_compute(
                                "AllGather", ALU.bypass, replica_groups=grp,
                                ins=[ci], outs=[co]))
                agv[vh] = cCC.n
            ydma(cM, m_sb[:, :], (m0t if layer == 0 else m1t)[:, :])
            vtn = [0, 0, 0, 0]
            if do_vt:
                for vh in range(4):
                    kb.wait("y", cCC, agv[vh])
                    for r in range(2):
                        for m2 in range(2 * vh, 2 * vh + 2):
                            a = r * 8 + m2
                            src = ccT[(layer, "vo", vh)][E * r: E * (r + 1),
                                                         128 * (m2 - 2 * vh):
                                                         128 * (m2 - 2 * vh) + 128]
                            ydma(cVT, arenaA[:, 2048 * a:2048 * (a + 1)], src,
                                 transpose=True)
                    vtn[vh] = cVT.n
            return layer, agk, vtn, cT.n

        def attn_phase(layer, agk, vtn, kvdone_t, side=None):
            wqw = wq0 if layer == 0 else wq1
            msk = m_sb
            AO = xB
            Pb = arenaB
            scale = SC_L0 if layer == 0 else SC_L1
            fillval = 0.0 if layer == 0 else 1.0
            kload = {}
            headdone = {}
            kb.wait("t", cM)
            vt_waited = [False] * 4
            agk_waited = [False] * 4
            for h in range(H):
                kb.wait("g", cT, kvdone_t if h < WR else headdone[h - WR])
                gdma(cW[h % WR], wb[h % WR][:, :], wqw[h][:, :])
                qldn = cW[h % WR].n
                if h >= 2:
                    kb.wait("y", cT, headdone[h - 2])
                if not agk_waited[h // 4]:
                    kb.wait("y", cCC, agk[h // 4])
                    agk_waited[h // 4] = True
                ko = ccT[(layer, "ko", h // 4)]
                for r in range(2):
                    ydma(cK[h % 2], kbuf[h % 2][:, 1024 * r:1024 * (r + 1)],
                         ko[512 * r + 128 * (h % 4): 512 * r + 128 * (h % 4) + 128, :])
                kload[h] = cK[h % 2].n
                for tt in range(2):
                    kb.wait("t", cW[h % WR], qldn)
                    kb.wait("t", cS, cS.n)
                    kb.wait("t", cV, cV.n)
                    proj_mms(pb[6], wb[h % WR], tt)
                    mmn = cT.n
                    kb.wait("s", cT, mmn)
                    if layer == 0:
                        kb.wait("s", cV, cV.n)
                        sop(lambda e, t=tt, h_=h: e.activation(
                            wbf[0][:, 512 * t:512 * (t + 1)], pb[6][:, :],
                            AF.Identity, bias=bq_sb[:, h_:h_ + 1]), tick=True)
                        kb.wait("v", cS, cS.n)
                        vop(lambda e, t=tt: e.tensor_mul(
                            wbf[1][:, 512 * t:512 * (t + 1)],
                            wbf[0][:, 512 * t:512 * (t + 1)],
                            ropec_sb[:, 512 * t:512 * (t + 1)]))
                        vop(lambda e, t=tt: e.tensor_mul(
                            wbf[2][:, 512 * t:512 * (t + 1)],
                            wbf[0][:, 512 * t:512 * (t + 1)],
                            ropes_sb[:, 512 * t:512 * (t + 1)]), tick=True)
                        kb.wait("t", cV, cV.n)
                        mm(pb[7][:, :], p128_sb[:, :],
                           wbf[2][:, 512 * tt:512 * (tt + 1)], True, True,
                           tick=True)
                        kb.wait("v", cT, cT.n)
                        vop(lambda e, t=tt: e.tensor_add(
                            qh[:, 512 * t:512 * (t + 1)],
                            wbf[1][:, 512 * t:512 * (t + 1)], pb[7][:, :]),
                            tick=True)
                    else:
                        sop(lambda e, t=tt: e.copy(
                            qh[:, 512 * t:512 * (t + 1)], pb[6][:, :]), tick=True)
                kb.wait("t", cV, cV.n)
                kb.wait("t", cS, cS.n)
                kb.wait("t", cK[h % 2], kload[h])
                exp_tick = {}
                for j in range(2):
                    kb.wait("v", cT, cT.n)
                    fk = [(k, c0_of(j, k)) for k in range(0, 8 * j + 8)
                          if c0_of(j, k) > 0]
                    for fi, (k, c0) in enumerate(fk):
                        vop(lambda e, k_=k, c_=c0: e.memset(
                            Pb[:, 512 * k_:512 * k_ + c_], fillval),
                            tick=(fi == len(fk) - 1))
                    kb.wait("s", cV, cV.n)
                    kb.wait("t", cV, cV.n)
                    for k in range(0, 8 * j + 8):
                        c0 = c0_of(j, k)
                        bank = pb[k % 2]
                        if k >= 2:
                            kb.wait("t", cS, exp_tick[k - 2])
                        has_mask = (layer == 0 and k >= 8 * j)
                        mm(bank[:, c0:512],
                           kbuf[h % 2][:, kcol(k):kcol(k) + 128],
                           qh[:, 512 * j + c0:512 * (j + 1)],
                           start=True, stop=not has_mask, tick=not has_mask)
                        if has_mask:
                            mm(bank[:, c0:c0 + 128], identb_sb[:, :],
                               msk[:, 128 * k:128 * (k + 1)], start=False,
                               stop=True, tick=True)
                        smn = cT.n
                        if layer == 0 or k < 8 * j:
                            kb.wait("s", cT, smn)
                            sop(lambda e, k_=k, c_=c0, b=bank: e.activation(
                                Pb[:, 512 * k_ + c_:512 * (k_ + 1)], b[:, c_:512],
                                AF.Exp, scale=scale), tick=True)
                        else:
                            kb.wait("v", cT, smn)
                            kb.wait("v", cS, cS.n)
                            vop(lambda e, k_=k, c_=c0, b=bank: e.tensor_mul(
                                wk[5][:, 0:128], b[:, c_:c_ + 128],
                                msk[:, 128 * k_:128 * (k_ + 1)]), tick=True)
                            kb.wait("s", cV, cV.n)
                            sop(lambda e, k_=k, c_=c0: e.activation(
                                Pb[:, 512 * k_ + c_:512 * k_ + c_ + 128],
                                wk[5][:, 0:128], AF.Exp, scale=scale),
                                tick=(c0 + 128 >= 512))
                            if c0 + 128 < 512:
                                sop(lambda e, k_=k, c_=c0, b=bank: e.activation(
                                    Pb[:, 512 * k_ + c_ + 128:512 * (k_ + 1)],
                                    b[:, c_ + 128:512], AF.Exp, scale=scale),
                                    tick=True)
                        exp_tick[k] = cS.n
                    kb.wait("t", cS, cS.n)
                    vtg = 3 if layer == 1 else (1 if j == 0 else 3)
                    if not vt_waited[vtg]:
                        kb.wait("t", cVT, vtn[vtg])
                        vt_waited[vtg] = True
                    obank, zbank = pb[2 + j], pb[4 + j]
                    krange = list(range(0, 8 * j + 8)) if layer == 0 else (
                        list(range(16)) if j == 0 else list(range(16)))
                    zrange = set(range(0, 8 * j + 8))
                    first = True
                    for k in krange:
                        inwin = k in zrange
                        last = (k == krange[-1])
                        rhs = Pb[:, 512 * k:512 * (k + 1)] if inwin else allones[:, :]
                        vt = arenaA[:, 2048 * vtslot(k) + 128 * h:
                                    2048 * vtslot(k) + 128 * (h + 1)]
                        mm(obank[:, :], vt, rhs, start=first, stop=last,
                           tick=(last and not (inwin and True)))
                        if inwin:
                            mm(zbank[:, :], ones128_sb[:, :], rhs,
                               start=first, stop=last, tick=last)
                        if layer == 1 and first and j == 0:
                            mm(zbank[:, :], c1024_sb[:, :], allones[0:1, 0:512],
                               start=False, stop=False)
                        first = False
                    pvn = cT.n
                    kb.wait("s", cT, pvn)
                    sop(lambda e, z=zbank: e.activation(wk[0][:, :], z[:, :], AF.Ln),
                        tick=True)
                    kb.wait("s", cS, cS.n)
                    sop(lambda e: e.activation(wk[1][:, :], wk[0][:, :], AF.Exp,
                                               scale=-1.0), tick=True)
                    kb.wait("v", cS, cS.n)
                    vop(lambda e, o=obank, h_=h, j_=j: e.tensor_mul(
                        AO[:, 1024 * h_ + 512 * j_:1024 * h_ + 512 * (j_ + 1)],
                        o[:, :], wk[1][:, :]), tick=True)
                headdone[h] = cT.n
                if side is not None and h < len(side):
                    for fn in side[h]:
                        fn()
            kb.wait("t", cV, cV.n)
            return AO

        def wo_phase(layer, AO):
            wow = wo0 if layer == 0 else wo1
            cons = {}
            entry_t = cT.n
            evac = {0: (cV, cV.n), 1: (cV, cV.n)}
            for fh in range(NCH):
                kb.wait("g", cT, cons[fh - WR] if fh >= WR else entry_t)
                gdma(cW[fh % WR], wb2[fh % WR][:, :], wow[fh][:, :])
                ldn = cW[fh % WR].n
                for tt in range(2):
                    bank = pb[tt]
                    kb.wait("t", cW[fh % WR], ldn)
                    ch_, n_ = evac[tt]
                    kb.wait("t", ch_, n_)
                    for ec in range(NCH):
                        mm(bank[:, :], wb2[fh % WR][:, 128 * ec:128 * (ec + 1)],
                           AO[:, 1024 * ec + 512 * tt:1024 * ec + 512 * (tt + 1)],
                           start=(ec == 0), stop=(ec == NCH - 1),
                           tick=(ec == NCH - 1))
                    kb.wait("v", cT, cT.n)
                    if layer == 0:
                        vop(lambda e, b=bank, f=fh, t=tt: e.scalar_tensor_tensor(
                            rr(f, t), b[:, :], bo_sb[:, f:f + 1], cur(f, t),
                            ALU.add, ALU.add), tick=True)
                    else:
                        vop(lambda e, b=bank, f=fh, t=tt: e.tensor_add(
                            rr(f, t), b[:, :], cur(f, t)), tick=True)
                    evac[tt] = (cV, cV.n)
                cons[fh] = cT.n

        def ln_phase(dst, src=None, desc=False, mrow=None):
            if src is None:
                src = rr
            kb.wait("t", cV, cV.n)
            kb.wait("s", cV, cV.n)
            prev_tick = cT.n
            for c in range(NCH):
                for tt in range(2):
                    kb.wait("s", cT, prev_tick)
                    sop(lambda e, c_=c, t=tt: e.activation(
                        wbf[0][:, 512 * t:512 * (t + 1)], src(c_, t), AF.Square))
                    sqn = cS.n
                    mm(pb[tt][:, :], ones128_sb[:, :], src(c, tt),
                       start=(c == 0), stop=(c == NCH - 1))
                    kb.wait("t", cS, sqn)
                    mm(pb[2 + tt][:, :], ones128_sb[:, :],
                       wbf[0][:, 512 * tt:512 * (tt + 1)],
                       start=(c == 0), stop=(c == NCH - 1), tick=True)
                    prev_tick = cT.n
            stn = cT.n
            for tt in range(2):
                nr = wk[3] if tt == 0 else wk[5]
                kb.wait("v", cT, stn)
                vop(lambda e, t=tt: e.tensor_scalar(
                    wk[0][:, :], pb[2 + t][:, :], RDIV, None, ALU.mult))
                vop(lambda e, t=tt: e.tensor_scalar(
                    wk[2][:, :], pb[t][:, :], RDIV, None, ALU.mult))
                kb.wait("s", cV, cV.n)
                sop(lambda e: e.activation(wk[1][:, :], wk[2][:, :], AF.Square))
                kb.wait("v", cS, cS.n)
                vop(lambda e: e.tensor_sub(wk[0][:, :], wk[0][:, :], wk[1][:, :]))
                kb.wait("s", cV, cV.n)
                sop(lambda e: e.activation(wk[1][:, :], wk[0][:, :], AF.Ln,
                                           bias=epsT[:, :]))
                kb.wait("s", cS, cS.n)
                sop(lambda e: e.activation(wk[1][:, :], wk[1][:, :], AF.Exp,
                                           scale=-0.5))
                kb.wait("v", cS, cS.n)
                vop(lambda e, o=nr: e.tensor_scalar(
                    o[:, :], wk[1][:, :], -1.0, None, ALU.mult))
                if mrow is not None:
                    vop(lambda e, t=tt: e.tensor_scalar(
                        mrow(t), pb[t][:, :], RDIV, None, ALU.mult))
            bankfree = cV.n
            corder = list(range(NCH))
            ttorder = [0, 1]
            if desc:
                corder = corder[::-1]
                ttorder = [1, 0]
            lnt = {}
            for c in corder:
                for tt in ttorder:
                    nr = wk[3] if tt == 0 else wk[5]
                    kb.wait("v", cV, cV.n)
                    if mrow is not None:
                        vop(lambda e, c_=c, t=tt: e.scalar_tensor_tensor(
                            wk[1][:, :], mrow(t), 1.0, src(c_, t),
                            ALU.mult, ALU.subtract))
                    else:
                        vop(lambda e, c_=c, t=tt: e.scalar_tensor_tensor(
                            wk[1][:, :], pb[t][:, :], RDIV, src(c_, t),
                            ALU.mult, ALU.subtract))
                    kb.wait("v", cV, cV.n)
                    vop(lambda e, c_=c, t=tt, o=nr: e.tensor_mul(
                        dst(c_, t), wk[1][:, :], o[:, :]))
                    lnt[(c, tt)] = cV.n
            return lnt, bankfree


        def moe_pre(lnt=None):
            kb.wait("v", cA, cA.n)
            lgcopy = [0, 0]
            for tt in range(2):
                if tt == 1:
                    kb.wait("t", cS, cS.n)
                for ec in range(NCH):
                    if lnt is not None:
                        kb.wait("t", cV, lnt[(ec, tt)])
                    mm(pb[6][0:8, :], rw_sb[:, 8 * ec:8 * (ec + 1)], cur(ec, tt),
                       start=(ec == 0), stop=(ec == NCH - 1), tick=(ec == NCH - 1))
                kb.wait("s", cT, cT.n)
                sop(lambda e, t=tt: e.copy(arenaB_f[0:8, 512 * t:512 * (t + 1)],
                                           pb[6][0:8, :]))
                lgcopy[tt] = cS.n
            trn = [0] * 8
            for lb in range(8):
                tt, co = (0, 128 * lb) if lb < 4 else (1, 128 * (lb - 4))
                if lb in (0, 4):
                    kb.wait("t", cS, lgcopy[tt])
                mm(pb[7][:, 8 * lb:8 * lb + 8],
                   arenaB_f[0:8, 512 * tt + co:512 * tt + co + 128],
                   identf_sb[0:8, 0:8], start=True, stop=True, tick=True)
                trn[lb] = cT.n
            for lb in range(8):
                kb.wait("v", cT, trn[lb])
                vop(lambda e, l=lb: e.tensor_copy(mt8[0][:, :],
                                                  pb[7][:, 8 * l:8 * l + 8]))
                vop(lambda e: e.reduce_max(mt8[1][:, 0:1], mt8[0][:, :], AX.X,
                                           negate=True))
                kb.wait("s", cV, cV.n)
                sop(lambda e: e.activation(mt8[2][:, :], mt8[0][:, :], AF.Exp,
                                           bias=mt8[1][:, 0:1],
                                           accum_out=mt8[1][:, 1:2]))
                kb.wait("v", cS, cS.n)
                vop(lambda e: e.reciprocal(mt8[1][:, 2:3], mt8[1][:, 1:2]))
                vop(lambda e: e.tensor_scalar(mt8[2][:, :], mt8[2][:, :],
                                              mt8[1][:, 2:3], None, ALU.mult))

                def top_one(m_dst, oh_dst_bf):
                    vop(lambda e, m=m_dst: e.reduce_max(m, mt8[2][:, :], AX.X))
                    vop(lambda e, m=m_dst: e.tensor_scalar(
                        mt8[0][:, :], mt8[2][:, :], m, None, ALU.is_equal))
                    vop(lambda e: e.tensor_copy(mt8[3][:, :], mt8[0][:, :]))
                    vop(lambda e: e.tensor_add(mt8[3][:, 1:8], mt8[0][:, 1:8],
                                               mt8[0][:, 0:7]))
                    vop(lambda e: e.tensor_copy(wk[0][:, 0:8], mt8[3][:, :]))
                    vop(lambda e: e.tensor_add(wk[0][:, 2:8], mt8[3][:, 2:8],
                                               mt8[3][:, 0:6]))
                    vop(lambda e: e.tensor_copy(mt8[3][:, :], wk[0][:, 0:8]))
                    vop(lambda e: e.tensor_add(mt8[3][:, 4:8], wk[0][:, 4:8],
                                               wk[0][:, 0:4]))
                    vop(lambda e: e.tensor_scalar(mt8[3][:, :], mt8[3][:, :], 1.0,
                                                  None, ALU.is_equal))
                    vop(lambda e: e.tensor_mul(mt8[0][:, :], mt8[0][:, :],
                                               mt8[3][:, :]))
                    vop(lambda e, o=oh_dst_bf: e.tensor_copy(o, mt8[0][:, :]))

                top_one(moe_m1[:, lb:lb + 1], moe_oh1[:, 8 * lb:8 * lb + 8])
                vop(lambda e: e.tensor_scalar(mt8[3][:, :], mt8[0][:, :], -1.0,
                                              1.0, ALU.mult, ALU.add))
                vop(lambda e: e.tensor_mul(mt8[2][:, :], mt8[2][:, :],
                                           mt8[3][:, :]))
                top_one(moe_m2[:, lb:lb + 1], moe_oh2[:, 8 * lb:8 * lb + 8])

        def moe_counts():
            kb.wait("t", cV, cV.n)
            first = True
            for choice in range(2):
                ohs = moe_oh1 if choice == 0 else moe_oh2
                for lb in range(8):
                    co = 8 * (choice * 8 + lb)
                    mm(pb[7][0:1, co:co + 8], ones128_sb[:, 0:1],
                       ohs[:, 8 * lb:8 * lb + 8], start=first, stop=False,
                       tick=(choice == 1 and lb == 7))
                    first = False
            kb.wait("s", cT, cT.n)
            sop(lambda e: e.copy(cntrow[:, :], pb[7][0:1, 0:128]))
            kb.wait("g", cS, cS.n)
            gdma(cA, cnt_in[:, :], cntrow[:, :])
            kb.wait("g", cA)
            kb.tick("g", cCC, lambda e: e.collective_compute(
                "AllGather", ALU.bypass,
                replica_groups=[[0, 1, 2, 3, 4, 5, 6, 7]],
                ins=[cnt_in], outs=[cnt_out]))
            kb.wait("g", cCC, cCC.n)
            for c_ in range(2):
                srcap = cnt_out[:, 64 * c_:64 * (c_ + 1)].rearrange(
                    "core (blk e) -> core blk e", blk=8)
                gdma(cA, cnt_sb1[64 * c_:64 * (c_ + 1), :], srcap,
                     allow_slow_non_contiguous=True)

        def moe_gates():
            kb.wait("t", cA)
            kb.wait("t", cS, cS.n)
            mm(pb[7][0:16, 0:8], mbig_sb[:, :], cnt_sb1[:, :], start=True,
               stop=True, tick=True)
            kb.wait("s", cT, cT.n)
            sop(lambda e: e.copy(moe_s1[0:16, 0:8], pb[7][0:16, 0:8]))
            kb.wait("g", cS, cS.n)
            gdma(cA, offrow[0:1, 0:128], moe_s1[0:16, 0:8],
                 allow_slow_non_contiguous=True)
            kb.wait("t", cA)
            for lb in range(8):
                for choice in range(2):
                    ohs = moe_oh1 if choice == 0 else moe_oh2
                    ms = moe_m1 if choice == 0 else moe_m2
                    r = choice * 8 + lb
                    kb.wait("t", cV, cV.n)
                    mm(pb[6][:, 0:8], u128_sb[:, :], ohs[:, 8 * lb:8 * lb + 8],
                       start=True, stop=False)
                    mm(pb[6][:, 0:8], onesrow_f_sb[:, :],
                       offrow[0:1, 8 * r:8 * r + 8], start=False, stop=True,
                       tick=True)
                    kb.wait("v", cT, cT.n)
                    vop(lambda e, o=ohs, l=lb: e.tensor_tensor(
                        mt8[0][:, :], pb[6][:, 0:8], o[:, 8 * l:8 * l + 8],
                        ALU.mult))
                    vop(lambda e: e.reduce_sum(mt8[1][:, 0:1], mt8[0][:, :], AX.X))
                    vop(lambda e: e.tensor_scalar(mt8[1][:, 1:2], mt8[1][:, 0:1],
                                                  512.0, None, ALU.is_le))
                    vop(lambda e, m=ms, l=lb, ch_=choice: e.tensor_tensor(
                        mt8[2][:, ch_:ch_ + 1], mt8[1][:, 1:2], m[:, l:l + 1],
                        ALU.mult))
                vop(lambda e, l=lb: e.tensor_add(moe_g[:, l:l + 1],
                                                 mt8[2][:, 0:1], mt8[2][:, 1:2]))
            kb.wait("t", cV, cV.n)
            mm(pb[7][0:8, 0:128], moe_g[:, :], identf_sb[:, :],
               start=True, stop=True, tick=True)
            kb.wait("s", cT, cT.n)
            sop(lambda e: e.copy(moe_s1[0:8, :], pb[7][0:8, 0:128]))
            kb.wait("g", cS, cS.n)
            gdma(cA, grow[0:1, 0:1024], moe_s1[0:8, :],
                 allow_slow_non_contiguous=True)
            kb.wait("t", cA)
            for tt in range(2):
                kb.wait("t", cS, cS.n)
                mm(pb[7][:, :], onesrow_f_sb[:, :],
                   grow[0:1, 512 * tt:512 * (tt + 1)], start=True, stop=True,
                   tick=True)
                kb.wait("s", cT, cT.n)
                sop(lambda e, t=tt: e.copy(gb_sb[:, 512 * t:512 * (t + 1)],
                                           pb[7][:, :]))

        def ffn_phase(lnt=None, bankfree=None, sides=None):
            wcons = {}
            lni = 0
            for tt in range(2):
                bf0 = bankfree if (tt == 0 and bankfree is not None) else cV.n
                gel = {0: (cV, bf0), 1: (cV, bf0)}
                for f in range(64):
                    if tt == 0 and sides and f in sides:
                        for fn in sides[f]:
                            fn()
                    li = lni + f
                    kb.wait("g", cT, wcons[li - WR] if li >= WR else cT.n)
                    gdma(cW[li % WR], wb[li % WR][:, :], w1[f][:, :])
                    ldn = cW[li % WR].n
                    bank = pb[f % 2]
                    kb.wait("t", cW[li % WR], ldn)
                    ch_, n_ = gel[f % 2]
                    kb.wait("t", ch_, n_)
                    if lnt is not None and tt == 1 and f == 0:
                        kb.wait("t", cV, lnt[(NCH - 1, 1)])
                    for ec in range(NCH):
                        if lnt is not None and tt == 0 and f == 0:
                            kb.wait("t", cV, lnt[(ec, 0)])
                        mm(bank[:, :], wb[li % WR][:, 128 * ec:128 * (ec + 1)],
                           cur(ec, tt), start=(ec == 0), stop=(ec == NCH - 1),
                           tick=(ec == NCH - 1))
                    wcons[li] = cT.n
                    kb.wait("s", cT, cT.n)
                    sop(lambda e, b=bank, f_=f: e.activation(
                        arenaA[:, 512 * f_:512 * (f_ + 1)], b[:, :], AF.Gelu,
                        bias=b1_sb[:, f_:f_ + 1]))
                    gel[f % 2] = (cS, cS.n)
                lni += 64
                if tt == 0:
                    moe_gates()
                kb.wait("t", cS, cS.n)
                ev2 = {0: (cV, cV.n), 1: (cV, cV.n)}
                for oh in range(NCH):
                    for qq in range(4):
                        li = lni + oh * 4 + qq
                        kb.wait("g", cT, wcons[li - WR] if li - WR in wcons else cT.n)
                        gdma(cW[li % WR], wb[li % WR][:, :],
                             w2[oh][:, 2048 * qq:2048 * (qq + 1)])
                        ldn = cW[li % WR].n
                        bank = pb[2 + oh % 2]
                        kb.wait("t", cW[li % WR], ldn)
                        if qq == 0:
                            ch_, n_ = ev2[oh % 2]
                            kb.wait("t", ch_, n_)
                        for fi in range(16):
                            fc = 16 * qq + fi
                            mm(bank[:, :], wb[li % WR][:, 128 * fi:128 * (fi + 1)],
                               arenaA[:, 512 * fc:512 * (fc + 1)],
                               start=(fc == 0), stop=(fc == 63),
                               tick=(fi == 15))
                        wcons[li] = cT.n
                    kb.wait("v", cT, cT.n)
                    vop(lambda e, b=pb[2 + oh % 2], o=oh, t=tt:
                        e.scalar_tensor_tensor(
                            wk[4][:, :], b[:, :], b2_sb[:, o:o + 1],
                            gb_sb[:, 512 * t:512 * (t + 1)], ALU.add, ALU.mult))
                    kb.wait("v", cV, cV.n)
                    vop(lambda e, o=oh, t=tt: e.tensor_add(
                        aox(o, t), wk[4][:, :], cur(o, t)))
                    ev2[oh % 2] = (cV, cV.n)
                lni += 64

        def emit_out_stream(srcfn):
            idx = 0
            for c in range(NCH):
                for tt in range(2):
                    buf = wk[idx % 2]
                    if idx >= 2:
                        kb.wait("s", cO, cO.n)
                    kb.wait("s", cV, cV.n)
                    sop(lambda e, c_=c, t=tt, b=buf: e.copy(b[:, :], srcfn(c_, t)))
                    kb.wait("y", cS, cS.n)
                    ydma(cO, y[128 * c:128 * (c + 1), 512 * tt:512 * (tt + 1)],
                         buf[:, :])
                    idx += 1
            kb.wait("y", cO, cO.n)

        def quant_chain(src, src_i, seg448, segS, out_bf, slots=None):
            if slots is None:
                Af, Bf, Cf = (arenaB_f[:, 1024:2048], arenaB_f[:, 2048:3072],
                              arenaB_f[:, 3072:4096])
                Ai, Bi, Ci = (arenaB_i[:, 1024:2048], arenaB_i[:, 2048:3072],
                              arenaB_i[:, 3072:4096])
            else:
                (Af, Ai), (Bf, Bi), (Cf, Ci) = slots
            nseg = src.shape[-1] // 128
            vop(lambda e: e.tensor_scalar(Bi, src_i, 0x7FFFFFFF, None,
                                          ALU.bitwise_and))
            vop(lambda e: e.tensor_scalar(Af, Bf, 1e-30, None, ALU.max))
            kb.wait("s", cV, cV.n)
            for sg in range(nseg):
                sop(lambda e, g=sg: e.activation(
                    Bf[:, 128 * g:128 * (g + 1)], Af[:, 128 * g:128 * (g + 1)],
                    AF.Ln, scale=seg448[g]))
            kb.wait("v", cS, cS.n)
            vop(lambda e: e.tensor_scalar(Af, Bf, INV_LN2, -0.5, ALU.mult,
                                          ALU.add))
            vop(lambda e: e.tensor_scalar(Ci, Af, -6.5, None, ALU.max))
            kb.wait("s", cV, cV.n)
            sop(lambda e: e.copy(Bf, Ci))
            sop(lambda e: e.activation(Cf, Bf, AF.Exp, scale=-LN2,
                                       bias=b3l2T[:, :]))
            for sg in range(nseg):
                vop(lambda e, g=sg: e.tensor_scalar(
                    Af[:, 128 * g:128 * (g + 1)], src[:, 128 * g:128 * (g + 1)],
                    seg448[g], None, ALU.mult))
            kb.wait("v", cS, cS.n)
            vop(lambda e: e.tensor_tensor(src_i, Af, Cf, ALU.mult))
            kb.wait("s", cV, cV.n)
            sop(lambda e: e.activation(Cf, Bf, AF.Exp, scale=LN2,
                                       bias=bm3l2T[:, :]))
            sop(lambda e: e.copy(Af, src_i))
            kb.wait("v", cS, cS.n)
            vop(lambda e: e.tensor_tensor(Bf, Af, Cf, ALU.mult))
            for sg in range(nseg):
                vop(lambda e, g=sg: e.tensor_scalar(
                    out_bf[:, 128 * g:128 * (g + 1)], Bf[:, 128 * g:128 * (g + 1)],
                    segS[g], None, ALU.mult))

        _wbf2f = wbf[2].bitcast(F32)
        _wk2i = wk[2].bitcast(I32)
        _wk3i = wk[3].bitcast(I32)
        wq_slots = ((_wbf2f[:, 0:512], wbf[2].bitcast(I32)[:, 0:512]),
                    (wk[2][:, :], _wk2i[:, :]),
                    (wk[3][:, :], _wk3i[:, :]))

        def wquant_passA(oh):
            for hf in range(2):
                kb.wait("g", cV, cV.n)
                gdma(cA, wfb[:, 0:1024],
                     wf1[oh][:, 1024 * hf:1024 * (hf + 1)])
                kb.wait("v", cA)
                for sg in range(8):
                    vop(lambda e, g=sg, h_=hf: e.reduce_max(
                        wq_cm[:, 8 * h_ + g:8 * h_ + g + 1],
                        wfb[:, 128 * g:128 * (g + 1)], AX.X,
                        apply_absolute_value=True))
            kb.wait("t", cV, cV.n)
            kb.wait("t", cS, cS.n)
            mm(pb[7][0:16, 0:128], wq_cm[:, :], identf_sb[:, :],
               start=True, stop=True, tick=True)
            kb.wait("v", cT, cT.n)
            vop(lambda e: e.tensor_copy(moe_s1[0:16, :], pb[7][0:16, 0:128]))
            vop(lambda e: e.reduce_max(mt8[0][0:16, 0:1], moe_s1[0:16, :],
                                       AX.X))
            vop(lambda e: e.tensor_scalar(mt8[0][0:16, 1:2],
                                          mt8[0][0:16, 0:1], 1e-4, None,
                                          ALU.max))
            kb.wait("g", cV, cV.n)
            gdma(cA, wq_rows[0:1, 16 * oh:16 * (oh + 1)],
                 mt8[0][0:16, 1:2], allow_slow_non_contiguous=True)

        def wquant_prep():
            kb.wait("v", cA)
            vop(lambda e: e.reciprocal(wq_rows[0:1, 256:288], wq_rows[0:1, 0:32]))
            vop(lambda e: e.tensor_scalar(wq_rows[0:1, 256:288], wq_rows[0:1, 256:288], 448.0,
                                          None, ALU.mult))
            vop(lambda e: e.tensor_scalar(wq_rows[0:1, 0:32], wq_rows[0:1, 0:32],
                                          float(1.0 / 448.0), None, ALU.mult))
            kb.wait("t", cV, cV.n)
            mm(pb[7][:, 0:32], onesrow_f_sb[:, :], wq_rows[0:1, 256:288], start=True,
               stop=True, tick=True)
            kb.wait("s", cT, cT.n)
            sop(lambda e: e.copy(wq_rb448[:, 0:32], pb[7][:, 0:32]))
            kb.wait("t", cS, cS.n)
            mm(pb[7][:, 0:32], onesrow_f_sb[:, :], wq_rows[0:1, 0:32], start=True,
               stop=True, tick=True)
            kb.wait("s", cT, cT.n)
            sop(lambda e: e.copy(wq_s[:, 0:32], pb[7][:, 0:32]))
            kb.wait("v", cS, cS.n)

        def wquant_passB(oh, hfs=(0, 1)):
            for hf in hfs:
                kb.wait("g", cV, cV.n)
                gdma(cA, wfb[:, 0:1024],
                     wf1[oh][:, 1024 * hf:1024 * (hf + 1)])
                kb.wait("v", cA)
                out_bf = wbf[3 + hf]
                kb.wait("v", cO, cO.n)
                for q2 in range(2):
                    seg448 = [wq_rb448[:, 16 * oh + 8 * hf + 4 * q2 + g:
                                       16 * oh + 8 * hf + 4 * q2 + g + 1]
                              for g in range(4)]
                    segS = [wq_s[:, 16 * oh + 8 * hf + 4 * q2 + g:
                                 16 * oh + 8 * hf + 4 * q2 + g + 1]
                            for g in range(4)]
                    quant_chain(wfb[:, 512 * q2:512 * (q2 + 1)],
                                wfb_i[:, 512 * q2:512 * (q2 + 1)], seg448, segS,
                                out_bf[:, 512 * q2:512 * (q2 + 1)],
                                slots=wq_slots)
                kb.wait("y", cV, cV.n)
                ydma(cO, wdq_loc[128 * oh:128 * (oh + 1),
                                 1024 * hf:1024 * (hf + 1)], out_bf[:, :])

        xq_pb6 = [0]
        xq_srcfree = [0]

        def xquant_block(tb, hf, lnw=None):
            tt, co = (tb // 4, 128 * (tb % 4))
            kb.wait("s", cV, xq_srcfree[0])
            for g in range(8):
                c = 8 * hf + g
                if lnw is not None:
                    kb.wait("t", cV, lnw[(c, tt)])
                else:
                    kb.wait("t", cV, cV.n)
                kb.wait("t", cS, xq_pb6[0])
                mm(pb[6][:, 0:128],
                   arenaA_f[:, 1024 * c + 512 * tt + co:
                            1024 * c + 512 * tt + co + 128],
                   identf_sb[:, :], start=True, stop=True, tick=True)
                kb.wait("s", cT, cT.n)
                sop(lambda e, g_=g: e.copy(
                    arenaB_f[:, 128 * g_:128 * (g_ + 1)], pb[6][:, 0:128]))
                xq_pb6[0] = cS.n
            kb.wait("v", cS, cS.n)
            for g in range(8):
                vop(lambda e, g_=g: e.reduce_max(
                    q_am[:, g_:g_ + 1], arenaB_f[:, 128 * g_:128 * (g_ + 1)],
                    AX.X, apply_absolute_value=True))
            vop(lambda e: e.tensor_scalar(q_am[:, 0:8], q_am[:, 0:8], 1e-4,
                                          None, ALU.max))
            vop(lambda e: e.reciprocal(q_rc448[:, 0:8], q_am[:, 0:8]))
            vop(lambda e: e.tensor_scalar(q_rc448[:, 0:8], q_rc448[:, 0:8],
                                          448.0, None, ALU.mult))
            vop(lambda e: e.tensor_scalar(q_s[:, 0:8], q_am[:, 0:8],
                                          float(1.0 / 448.0), None, ALU.mult))
            seg448 = [q_rc448[:, g:g + 1] for g in range(8)]
            segS = [q_s[:, g:g + 1] for g in range(8)]
            quant_chain(arenaB_f[:, 0:1024], arenaB_i[:, 0:1024], seg448, segS,
                        wbf[0][:, :])
            xq_srcfree[0] = cV.n
            for g in range(8):
                c = 8 * hf + g
                kb.wait("t", cV, cV.n)
                kb.wait("t", cS, xq_pb6[0])
                mm(pb[6][:, 0:128], wbf[0][:, 128 * g:128 * (g + 1)],
                   identb_sb[:, :], start=True, stop=True, tick=True)
                kb.wait("s", cT, cT.n)
                sop(lambda e, c_=c, t=tt, co_=co: e.copy(
                    xB[:, 1024 * c_ + 512 * t + co_:
                       1024 * c_ + 512 * t + co_ + 128], pb[6][:, 0:128]))
                xq_pb6[0] = cS.n

        agw_n = [0]

        def wquant_gather():
            kb.wait("g", cO, cO.n)
            kb.tick("g", cCC, lambda e: e.collective_compute(
                "AllGather", ALU.bypass,
                replica_groups=[[0, 1, 2, 3, 4, 5, 6, 7]],
                ins=[wdq_loc], outs=[wdqG]))
            agw_n[0] = cCC.n

        def ffn1_phase(agw, lnt=None, bankfree=None):
            wcons = {}
            kb.wait("g", cCC, agw)
            hf_tick = {}
            entry_t = cT.n
            for hf in (1, 0):
                for tb in range(4):
                    xquant_block(tb, hf, lnw=lnt)
                hf_tick[(0, hf)] = cS.n
            for tt in range(2):
                kb.wait("t", cV, cV.n)
                bf0 = bankfree if (tt == 0 and bankfree is not None) else cV.n
                ev = {0: (cV, bf0), 1: (cV, bf0)}
                for oh in range(NCH):
                    li = tt * NCH + oh
                    kb.wait("g", cT,
                            wcons[li - WR] if li - WR in wcons else entry_t)
                    gdma(cW[li % WR], wb[li % WR][:, :],
                         wdqG[128 * oh:128 * (oh + 1), :])
                    ldn = cW[li % WR].n
                    bank = pb[tt]
                    kb.wait("t", cW[li % WR], ldn)
                    ch_, n_ = ev[tt]
                    kb.wait("t", ch_, n_)
                    ecs = list(range(8, 16)) + list(range(8))
                    for pos, ec in enumerate(ecs):
                        if pos % 8 == 0:
                            kb.wait("t", cS, hf_tick[(tt, ec // 8)])
                        mm(bank[:, :], wb[li % WR][:, 128 * ec:128 * (ec + 1)],
                           xB[:, 1024 * ec + 512 * tt:1024 * ec + 512 * (tt + 1)],
                           start=(pos == 0), stop=(pos == NCH - 1),
                           tick=(pos == NCH - 1))
                    wcons[li] = cT.n
                    kb.wait("s", cT, cT.n)
                    sop(lambda e, b=bank: e.copy(wbf[1][:, 0:512], b[:, :]))
                    kb.wait("v", cS, cS.n)
                    vop(lambda e, o=oh, t=tt: e.tensor_add(
                        cur(o, t), arenaA_f[:, 1024 * o + 512 * t:
                                            1024 * o + 512 * (t + 1)],
                        wbf[1][:, 0:512]))
                    ev[tt] = (cV, cV.n)
                    if tt == 0 and oh in (1, 3):
                        hf = 1 if oh == 1 else 0
                        for tb in range(4, 8):
                            xquant_block(tb, hf)
                        hf_tick[(1, hf)] = cS.n

        def final_ln():
            kb.wait("t", cV, cV.n)
            kb.wait("s", cV, cV.n)
            prev_tick = cT.n
            for c in range(NCH):
                for tt in range(2):
                    kb.wait("s", cT, prev_tick)
                    sop(lambda e, c_=c, t=tt: e.activation(
                        wbf[0][:, 512 * t:512 * (t + 1)], cur(c_, t), AF.Square))
                    sqn = cS.n
                    mm(pb[tt][:, :], ones128_sb[:, :], cur(c, tt),
                       start=(c == 0), stop=(c == NCH - 1))
                    kb.wait("t", cS, sqn)
                    mm(pb[2 + tt][:, :], ones128_sb[:, :],
                       wbf[0][:, 512 * tt:512 * (tt + 1)],
                       start=(c == 0), stop=(c == NCH - 1), tick=True)
                    prev_tick = cT.n
            stn = cT.n
            for tt in range(2):
                nr = wk[3] if tt == 0 else wk[5]
                kb.wait("v", cT, stn)
                vop(lambda e, t=tt: e.tensor_scalar(
                    wk[0][:, :], pb[2 + t][:, :], RDIV, None, ALU.mult))
                vop(lambda e, t=tt: e.tensor_scalar(
                    wk[2][:, :], pb[t][:, :], RDIV, None, ALU.mult))
                kb.wait("s", cV, cV.n)
                sop(lambda e: e.activation(wk[1][:, :], wk[2][:, :], AF.Square))
                kb.wait("v", cS, cS.n)
                vop(lambda e: e.tensor_sub(wk[0][:, :], wk[0][:, :], wk[1][:, :]))
                kb.wait("s", cV, cV.n)
                sop(lambda e: e.activation(wk[1][:, :], wk[0][:, :], AF.Ln,
                                           bias=epsT[:, :]))
                sop(lambda e: e.activation(wk[1][:, :], wk[1][:, :], AF.Exp,
                                           scale=-0.5))
                kb.wait("v", cS, cS.n)
                vop(lambda e, o=nr: e.tensor_scalar(
                    o[:, :], wk[1][:, :], -1.0, None, ALU.mult))
                for c in range(NCH):
                    buf = wk[2] if c % 2 == 0 else wk[4]
                    vop(lambda e, c_=c, t=tt: e.scalar_tensor_tensor(
                        wk[1][:, :], pb[t][:, :], RDIV, cur(c_, t),
                        ALU.mult, ALU.subtract))
                    kb.wait("v", cO, cO.n)
                    vop(lambda e, b=buf, o=nr: e.tensor_mul(
                        b[:, :], wk[1][:, :], o[:, :]))
                    kb.wait("y", cV, cV.n)
                    ydma(cO, y[128 * c:128 * (c + 1), 512 * tt:512 * (tt + 1)],
                         buf[:, :])
            kb.wait("y", cO, cO.n)

        if stage == "ld":
            emit_out_stream(cur)
        elif stage in ("kv", "kvt", "kvnc"):
            kv_phase(0, do_vt=(stage == "kvt"), do_cc=(stage != "kvnc"))
            emit_out_stream(cur)
        else:
            _, agk0, vtn0, kvt0 = kv_phase(0)
            AO0 = attn_phase(0, agk0, vtn0, kvt0)
            if stage == "attn":
                emit_out_stream(lambda c, t: AO0[:, 1024 * c + 512 * t:
                                                 1024 * c + 512 * (t + 1)])
            else:
                wo_phase(0, AO0)
                lnt1, bf1 = ln_phase(
                    cur, mrow=lambda t: arenaB_f[:, 1024 + 512 * t:
                                                 1536 + 512 * t])
                if stage == "l0a":
                    emit_out_stream(cur)
                else:
                    wquant_passA(0)
                    wquant_passA(1)
                    wquant_prep()
                    moe_pre(lnt=lnt1)
                    ffn_phase(lnt=lnt1, bankfree=bf1, sides={
                        32: [moe_counts],
                        36: [lambda: wquant_passB(0, (0,))],
                        42: [lambda: wquant_passB(0, (1,))],
                        48: [lambda: wquant_passB(1, (0,))],
                        54: [lambda: wquant_passB(1, (1,))],
                        60: [wquant_gather],
                    })
                    lnt2, bf2 = ln_phase(
                        cur, src=aox,
                        mrow=lambda t: gb_sb[:, 512 * t:512 * (t + 1)])
                    if stage == "l0f":
                        emit_out_stream(cur)
                    else:
                        _, agk1, vtn1, kvt1 = kv_phase(1, lnt=lnt2,
                                                       bankfree=bf2)
                        AO1 = attn_phase(1, agk1, vtn1, kvt1)
                        agw = agw_n[0]
                        wo_phase(1, AO1)
                        x3f = lambda c, t: arenaA_f[:, 1024 * c + 512 * t:
                                                    1024 * c + 512 * (t + 1)]
                        lnt3, bf3 = ln_phase(
                            x3f, src=rr, desc=True,
                            mrow=lambda t: gb_sb[:, 512 * t:512 * (t + 1)])
                        if stage == "l1a":
                            emit_out_stream(x3f)
                        elif stage == "xd":
                            for tb in range(8):
                                for hf in range(2):
                                    xquant_block(tb, hf)
                            emit_out_stream(lambda c, t: xB[:, 1024 * c + 512 * t:
                                                            1024 * c + 512 * (t + 1)])
                        else:
                            xq_pb6[0] = cS.n
                            xq_srcfree[0] = bf3
                            ffn1_phase(agw, lnt=lnt3, bankfree=bf3)
                            final_ln()

        with nc.Block() as block:
            block.tensor(lambda e: [f(e) for f in kb.q["t"]])
            block.vector(lambda e: [f(e) for f in kb.q["v"]])
            block.scalar(lambda e: [f(e) for f in kb.q["s"]])
            block.gpsimd(lambda e: [f(e) for f in kb.q["g"]])
            block.sync(lambda e: [f(e) for f in kb.q["y"]])

    return nc



def _tile_lhsT(Wct, fh_chunks):
    C, F = Wct.shape
    A = Wct.reshape(C // 128, 128, F // 128, 128).transpose(2, 1, 0, 3)
    return np.ascontiguousarray(A.reshape(F // 128, 128, C)).astype(np.float32)


def _bias_cols(b):
    return np.ascontiguousarray(b.reshape(-1, 128).T).astype(np.float32)


def make_core_inputs(inp, c):
    b, p = c // 2, c % 2
    cols = np.concatenate([np.arange(128 * (2 * lb + p), 128 * (2 * lb + p) + 128)
                           for lb in range(8)])
    x = np.asarray(inp["L0_attn_x"], np.float32)[b][cols]
    d = {}
    d["x0"] = np.ascontiguousarray(x.T)
    d["wq0"] = _tile_lhsT(np.asarray(inp["L0_attn_Wq"], np.float32).T, H)
    d["wk0"] = _tile_lhsT(np.asarray(inp["L0_attn_Wk"], np.float32).T, H)
    d["wv0"] = _tile_lhsT(np.asarray(inp["L0_attn_Wv"], np.float32).T, H)
    d["wo0"] = _tile_lhsT(np.asarray(inp["L0_attn_Wo"], np.float32).T, NCH)
    d["bq_c"] = _bias_cols(np.asarray(inp["L0_attn_bq"], np.float32))
    d["bk_c"] = _bias_cols(np.asarray(inp["L0_attn_bk"], np.float32))
    d["bv_c"] = _bias_cols(np.asarray(inp["L0_attn_bv"], np.float32))
    d["bo_c"] = _bias_cols(np.asarray(inp["L0_attn_bo"], np.float32))
    d["b1_c"] = _bias_cols(np.asarray(inp["L0_ffn_b1"], np.float32))
    d["b2_c"] = _bias_cols(np.asarray(inp["L0_ffn_b2"], np.float32))
    rwt = np.asarray(inp["L0_ffn_router_w"], np.float32).T
    d["rw"] = np.ascontiguousarray(
        rwt.reshape(NCH, 128, 8).transpose(1, 0, 2).reshape(128, NCH * 8)).astype(np.float32)
    d["w1"] = _tile_lhsT(np.asarray(inp["L0_ffn_w1"], np.float32).T, 64)
    d["w2"] = _tile_lhsT(np.asarray(inp["L0_ffn_w2"], np.float32).T, NCH)
    d["wq1"] = _tile_lhsT(np.asarray(inp["L1_attn_Wq"], np.float32), H)
    d["wk1"] = _tile_lhsT(np.asarray(inp["L1_attn_Wk"], np.float32), H)
    d["wv1"] = _tile_lhsT(np.asarray(inp["L1_attn_Wv"], np.float32), H)
    d["wo1"] = _tile_lhsT(np.asarray(inp["L1_attn_Wo"], np.float32), H)
    _wf1_all = _tile_lhsT(np.asarray(inp["L1_ffn_weight"], np.float32).T, NCH)
    d["wf1"] = np.ascontiguousarray(_wf1_all[[2 * c, 2 * c + 1]])

    pos = (cols).astype(np.float64)
    theta = np.repeat(-np.arange(0, 128, 2, dtype=np.float32) / 128.0, 2)
    theta = np.power(np.float32(10000.0), theta.astype(np.float32)).astype(np.float64)
    ang = pos[None, :] * 0 + 0.0
    ang = theta[:, None] * pos[None, :]
    d["ropec"] = np.cos(ang).astype(ml_dtypes.bfloat16)
    d["ropes"] = np.sin(ang).astype(ml_dtypes.bfloat16)
    P = np.zeros((128, 128), np.float32)
    for i2 in range(0, 128, 2):
        P[i2, i2 + 1] = -1.0
        P[i2 + 1, i2] = 1.0
    d["p128"] = np.ascontiguousarray(P.T).astype(ml_dtypes.bfloat16)
    d["identb"] = np.eye(128, dtype=np.float32).astype(ml_dtypes.bfloat16)
    d["identf"] = np.eye(128, dtype=np.float32)
    d["ones128"] = np.ones((128, 128), ml_dtypes.bfloat16)
    d["onesrow"] = np.ones((1, 128), ml_dtypes.bfloat16)
    d["onesrow_f"] = np.ones((1, 128), np.float32)
    d["u128"] = np.triu(np.ones((128, 128), np.float32)).astype(ml_dtypes.bfloat16)
    d["c1024"] = np.full((1, 128), 1024.0, ml_dtypes.bfloat16)

    tri_strict = np.tril(np.ones((128, 128), np.float32), -1)
    m0 = np.zeros((16, 128, 128), np.float32)
    m1 = np.zeros((16, 128, 128), np.float32)
    for k in range(16):
        j = 0 if k < 8 else 1
        i_min = max(0, int(math.ceil((k - 8 * j - 1) / 2.0)))
        qa = 8 * j + 2 * i_min + p
        if qa == k:
            m0[k] = tri_strict * -30000.0
            m1[k] = np.triu(np.ones((128, 128), np.float32), 0)
        elif qa < k:
            m0[k] = -30000.0
            m1[k] = 0.0
        else:
            m0[k] = 0.0
            m1[k] = 1.0
    d["m0t"] = np.ascontiguousarray(
        m0.transpose(1, 0, 2).reshape(128, 16 * 128)).astype(ml_dtypes.bfloat16)
    d["m1t"] = np.ascontiguousarray(
        m1.transpose(1, 0, 2).reshape(128, 16 * 128)).astype(ml_dtypes.bfloat16)

    myabs = [(b * 16 + 2 * lb + p) for lb in range(8)]
    Mb = np.zeros((128, 16), np.float32)
    for r in range(8):
        a_r = myabs[r]
        for core in range(8):
            for blk in range(8):
                a = (core // 2) * 16 + 2 * blk + (core % 2)
                if a < a_r:
                    Mb[core * 8 + blk, r] = 1.0
                    Mb[64 + core * 8 + blk, 8 + r] = 1.0
        Mb[0:64, 8 + r] = 1.0
    d["mbig"] = Mb.astype(ml_dtypes.bfloat16)
    return d


def kernel(**inputs):
    nc = build(stage="full")
    in_maps = [make_core_inputs(inputs, c) for c in range(8)]
    res = run_bass_kernel_spmd(nc, in_maps, core_ids=list(range(8)))
    out = np.zeros((B, S, E), np.float32)
    for c in range(8):
        b, p = c // 2, c % 2
        yt = res.results[c]["y"]
        xt = np.ascontiguousarray(yt.T)
        for lb in range(8):
            out[b, 128 * (2 * lb + p):128 * (2 * lb + p) + 128, :] = \
                xt[128 * lb:128 * (lb + 1)]
    return out


# version 16
# speedup vs baseline: 1.2618x; 1.0357x over previous
import math
import contextlib
import numpy as np
import ml_dtypes
import concourse.bass as bass
import concourse.mybir as mybir
from concourse.bass_utils import run_bass_kernel_spmd

F32 = mybir.dt.float32
BF16 = mybir.dt.bfloat16
I32 = mybir.dt.int32
AF = mybir.ActivationFunctionType
ALU = mybir.AluOpType
AX = mybir.AxisListType

B, S, E, H, D = 4, 2048, 2048, 16, 128
T = 1024
NCH = 16
LN2C = float(np.log(2.0))
INV_LN2 = float(1.0 / np.log(2.0))
LN2 = float(np.log(2.0))
SC_L0 = float(1.0 / np.sqrt(128.0))
SC_L1 = float(1.0 / np.sqrt(2048.0))
RDIV = float(1.0 / 2048.0)


def c0_of(j, k):
    if k < 8 * j:
        return 0
    return 128 * max(0, int(math.ceil((k - 8 * j - 1) / 2.0)))


def kcol(k):
    return (k % 2) * 1024 + (k // 2) * 128


def vtslot(k):
    return (k % 2) * 8 + (k // 2)


class Ch:
    def __init__(self, stack, nc, name, step=1):
        self.h = stack.enter_context(nc.semaphore(name))
        self.n = 0
        self.step = step


class KB:
    def __init__(self, nc):
        self.q = {k: [] for k in "tvsgy"}

    def on(self, ek, fn):
        self.q[ek].append(fn)

    def wait(self, ek, ch, n=None):
        n = ch.n if n is None else n
        if n > 0:
            self.on(ek, lambda e, h=ch.h, m=n: e.wait_ge(h, m))

    def tick(self, ek, ch, fn):
        ch.n += ch.step
        self.on(ek, lambda e, f=fn, h=ch.h, s=ch.step: f(e).then_inc(h, s))


def build(stage="full"):
    nc = bass.Bass("TRN2", target_bir_lowering=False, debug=False)

    def din(name, shape, dtype=F32):
        return nc.dram_tensor(name, list(shape), dtype, kind="ExternalInput").ap()

    x0 = din("x0", (E, T))
    wq0 = din("wq0", (H, 128, E)); wk0 = din("wk0", (H, 128, E))
    wv0 = din("wv0", (H, 128, E)); wo0 = din("wo0", (NCH, 128, E))
    bq_c = din("bq_c", (128, NCH)); bk_c = din("bk_c", (128, NCH))
    bv_c = din("bv_c", (128, NCH)); bo_c = din("bo_c", (128, NCH))
    b1_c = din("b1_c", (128, 64)); b2_c = din("b2_c", (128, NCH))
    rw = din("rw", (128, NCH * 8))
    w1 = din("w1", (64, 128, E))
    w2 = din("w2", (NCH, 128, 8192))
    wq1 = din("wq1", (H, 128, E)); wk1 = din("wk1", (H, 128, E))
    wv1 = din("wv1", (H, 128, E)); wo1 = din("wo1", (NCH, 128, E))
    wf1 = din("wf1", (2, 128, E))
    ropec = din("ropec", (128, T), BF16); ropes = din("ropes", (128, T), BF16)
    p128 = din("p128", (128, 128), BF16)
    identb = din("identb", (128, 128), BF16); identf = din("identf", (128, 128))
    ones128 = din("ones128", (128, 128), BF16)
    onesrow = din("onesrow", (1, 128), BF16); onesrow_f = din("onesrow_f", (1, 128))
    m0t = din("m0t", (128, 16 * 128), BF16); m1t = din("m1t", (128, 16 * 128), BF16)
    u128 = din("u128", (128, 128), BF16)
    mbig = din("mbig", (128, 16), BF16)
    c1024 = din("c1024", (1, 128), BF16)

    y = nc.dram_tensor("y", [E, T], F32, kind="ExternalOutput").ap()

    ccT = {}
    for L in (0, 1):
        for qtr in range(4):
            ccT[(L, "ki", qtr)] = nc.dram_tensor(
                f"cc{L}K{qtr}_in", [512, T], BF16, kind="Internal").ap()
            ccT[(L, "ko", qtr)] = nc.dram_tensor(
                f"cc{L}K{qtr}_out", [1024, T], BF16, kind="Internal").ap()
            ccT[(L, "vi", qtr)] = nc.dram_tensor(
                f"cc{L}V{qtr}_in", [E, 256], BF16, kind="Internal").ap()
            ccT[(L, "vo", qtr)] = nc.dram_tensor(
                f"cc{L}V{qtr}_out", [2 * E, 256], BF16, kind="Internal").ap()
    cnt_in = nc.dram_tensor("cnt_in", [1, 128], F32, kind="Internal").ap()
    cnt_out = nc.dram_tensor("cnt_out", [8, 128], F32, kind="Internal").ap()
    wdq_loc = nc.dram_tensor("wdq_loc", [256, E], BF16, kind="Internal").ap()
    wdqG = nc.dram_tensor("wdqG", [NCH * 128, E], BF16, kind="Internal").ap()

    stack = contextlib.ExitStack()
    with stack:
        sb = lambda name, shape, dtype: stack.enter_context(
            nc.sbuf_tensor(name, list(shape), dtype))

        xA = sb("xA", (128, NCH * 1024), BF16)
        xB = sb("xB", (128, NCH * 1024), BF16)
        arenaA = sb("arenaA", (128, 32768), BF16)
        arenaB = sb("arenaB", (128, 8192), BF16)
        arenaA_f = arenaA.bitcast(F32)
        arenaB_f = arenaB.bitcast(F32)
        arenaB_i = arenaB.bitcast(I32)

        kbuf = [sb(f"kbuf{i}", (128, 2048), BF16) for i in range(2)]
        qh = sb("qh", (128, 1024), BF16)
        WR = 3
        wb = [sb(f"wb{i}", (128, 2048), BF16) for i in range(WR)]
        wb2 = wb
        wfb = sb("wfb", (128, 1024), F32)
        gb_sb = kbuf[0].bitcast(F32)
        grow = kbuf[1].bitcast(F32)
        ropec_sb = sb("ropec_sb", (128, T), BF16)
        ropes_sb = sb("ropes_sb", (128, T), BF16)
        p128_sb = sb("p128_sb", (128, 128), BF16)
        identb_sb = sb("identb_sb", (128, 128), BF16)
        identf_sb = sb("identf_sb", (128, 128), F32)
        ones128_sb = sb("ones128_sb", (128, 128), BF16)
        onesrow_sb = sb("onesrow_sb", (1, 128), BF16)
        onesrow_f_sb = sb("onesrow_f_sb", (1, 128), F32)
        allones = sb("allones", (128, 512), BF16)
        m_sb = sb("m_sb", (128, 16 * 128), BF16)
        u128_sb = sb("u128_sb", (128, 128), BF16)
        mbig_sb = sb("mbig_sb", (128, 16), BF16)
        c1024_sb = sb("c1024_sb", (1, 128), BF16)
        bq_sb = sb("bq_sb", (128, NCH), F32)
        bk_sb = sb("bk_sb", (128, NCH), F32)
        bv_sb = sb("bv_sb", (128, NCH), F32)
        bo_sb = sb("bo_sb", (128, NCH), F32)
        b1_sb = sb("b1_sb", (128, 64), F32)
        b2_sb = sb("b2_sb", (128, NCH), F32)
        rw_sb = sb("rw_sb", (128, NCH * 8), BF16)
        wbf = [sb(f"wbf{i}", (128, 1024), BF16) for i in range(5)]
        wk = [sb(f"wkt{i}", (128, 512), F32) for i in range(4)]
        wk.append(wbf[0].bitcast(F32))
        wk.append(sb("wkt5", (128, 512), F32))
        epsT = sb("epsT", (128, 1), F32)
        b3l2T = sb("b3l2T", (128, 1), F32)
        bm3l2T = sb("bm3l2T", (128, 1), F32)
        moe_g = sb("moe_g", (128, 8), F32)
        moe_m1 = sb("moe_m1", (128, 8), F32)
        moe_m2 = sb("moe_m2", (128, 8), F32)
        moe_oh1 = sb("moe_oh1", (128, 8 * 8), BF16)
        moe_oh2 = sb("moe_oh2", (128, 8 * 8), BF16)
        mt8 = [sb(f"mt8_{i}", (128, 8), F32) for i in range(4)]
        cntrow = sb("cntrow", (1, 128), F32)
        cnt_sb1 = sb("cnt_sb1", (128, 8), BF16)
        cnt_sb2 = sb("cnt_sb2", (128, 8), BF16)
        offrow = sb("offrow", (1, 128), F32)
        moe_s1 = sb("moe_s1", (16, 128), F32)
        q_am = sb("q_am", (128, 16), F32)
        q_rc448 = sb("q_rc448", (128, 16), F32)
        q_s = sb("q_s", (128, 16), F32)
        wq_cm = sb("wq_cm", (128, 16), F32)
        wq_rows = sb("wq_rows", (1, 512), F32)
        wq_amrow = wq_rows[0:1, 0:256]
        wq_row2 = wq_rows[0:1, 256:512]
        _wbf1f = wbf[1].bitcast(F32)
        wq_rb448 = _wbf1f[:, 0:256]
        wq_s = _wbf1f[:, 256:512]
        wfb_i = wfb.bitcast(I32)

        pb = [stack.enter_context(nc.psum_tensor(f"pb{i}", [128, 512], F32))
              for i in range(8)]

        assert nc.sbuf_bytes_remaining >= 0, nc.sbuf_bytes_remaining
        assert nc.psum_banks_remaining >= 0

        kb = KB(nc)
        cC = Ch(stack, nc, "cC", 16)
        cX = Ch(stack, nc, "cX", 16)
        cW = [Ch(stack, nc, f"cW{i}", 16) for i in range(4)]
        cSg = [Ch(stack, nc, f"cSg{i}", 16) for i in range(2)]
        cK = [Ch(stack, nc, f"cK{i}", 16) for i in range(2)]
        cVT = Ch(stack, nc, "cVT", 16)
        cM = Ch(stack, nc, "cM", 16)
        cO = Ch(stack, nc, "cO", 16)
        cA = Ch(stack, nc, "cA", 16)
        cT = Ch(stack, nc, "cT", 1)
        cV = Ch(stack, nc, "cV", 1)
        cS = Ch(stack, nc, "cS", 1)
        cCC = Ch(stack, nc, "cCC", 1)

        def ydma(ch, out, in_, **kw):
            kb.tick("y", ch, lambda e, o=out, i=in_, k=kw: e.dma_start(out=o, in_=i, **k))

        def gdma(ch, out, in_, **kw):
            kb.tick("g", ch, lambda e, o=out, i=in_, k=kw: e.dma_start(out=o, in_=i, **k))

        def mm(out, lhsT, rhs, start, stop, tick=False):
            fn = lambda e, o=out, l=lhsT, r=rhs, s1=start, s2=stop: \
                e.matmul(o, l, r, start=s1, stop=s2, skip_group_check=True)
            if tick:
                kb.tick("t", cT, fn)
            else:
                kb.on("t", fn)

        def vop(fn, tick=True):
            kb.wait("v", cV, cV.n)
            kb.tick("v", cV, fn)

        def sop(fn, tick=True):
            kb.wait("s", cS, cS.n)
            kb.tick("s", cS, fn)

        for dst, src in [
            (ropec_sb, ropec), (ropes_sb, ropes), (identb_sb, identb),
            (identf_sb, identf), (ones128_sb, ones128), (onesrow_sb, onesrow),
            (onesrow_f_sb, onesrow_f), (p128_sb, p128),
            (u128_sb, u128), (mbig_sb, mbig),
            (c1024_sb, c1024), (bq_sb, bq_c), (bk_sb, bk_c), (bv_sb, bv_c),
            (bo_sb, bo_c), (b1_sb, b1_c), (b2_sb, b2_c),
        ]:
            ydma(cC, dst[:, :], src[:, :])
        n_const = cC.n
        gdma(cX, rw_sb[:, :], rw[:, :])
        for c in range(NCH):
            gdma(cX, xA[:, 1024 * c:1024 * (c + 1)], x0[128 * c:128 * (c + 1), :])
        vop(lambda e: e.memset(epsT[:, :], 1e-5))
        vop(lambda e: e.memset(b3l2T[:, :], float(3.0 * np.log(2.0))))
        vop(lambda e: e.memset(bm3l2T[:, :], float(-3.0 * np.log(2.0))))
        vop(lambda e: e.memset(allones[:, :], 1.0), tick=True)
        kb.wait("t", cC, n_const)
        kb.wait("v", cC, n_const)
        kb.wait("s", cC, n_const)
        kb.wait("t", cX)

        cur = lambda c, t=None: xA[:, 1024 * c + (0 if t is None else 512 * t):
                                   1024 * c + (1024 if t is None else 512 * (t + 1))]
        aox = lambda c, t=None: xB[:, 1024 * c + (0 if t is None else 512 * t):
                                   1024 * c + (1024 if t is None else 512 * (t + 1))]
        rr = lambda c, t=None: arenaA[:, 1024 * c + (0 if t is None else 512 * t):
                                      1024 * c + (1024 if t is None else 512 * (t + 1))]

        def proj_mms(bank, wslot, tt, ecw=None, fullw=None):
            if fullw is not None:
                kb.wait("t", cV, fullw)
            for ec in range(NCH):
                if ecw is not None:
                    kb.wait("t", cV, ecw[(ec, tt)])
                mm(bank[:, :], wslot[:, 128 * ec:128 * (ec + 1)],
                   cur(ec, tt), start=(ec == 0), stop=(ec == NCH - 1),
                   tick=(ec == NCH - 1))

        def kv_phase(layer, do_vt=True, do_cc=True, lnt=None, bankfree=None):
            wkw = wk0 if layer == 0 else wk1
            wvw = wv0 if layer == 0 else wv1
            grp = [[0, 1], [2, 3], [4, 5], [6, 7]]
            steps = ([("k", h) for h in range(H)] + [("v", h) for h in range(H)])
            cons = {}
            dmad = {}
            entry_t = cT.n
            agk = [0, 0, 0, 0]
            bf0 = bankfree if bankfree is not None else cV.n
            bank_free = {0: (cV, bf0), 1: (cV, bf0)}
            for i, (kind, h) in enumerate(steps):
                kb.wait("g", cT, cons[i - WR] if i >= WR else entry_t)
                gdma(cW[i % WR], wb[i % WR][:, :], (wkw if kind == "k" else wvw)[h][:, :])
                ldn = cW[i % WR].n
                stg = wbf[3 + (i % 2)]
                if i >= 2:
                    kb.wait("s", cSg[i % 2], dmad[i - 2])
                    kb.wait("v", cSg[i % 2], dmad[i - 2])
                for tt in range(2):
                    bank = pb[tt]
                    kb.wait("t", cW[i % WR], ldn)
                    ch_, n_ = bank_free[tt]
                    kb.wait("t", ch_, n_)
                    if lnt is not None and i < 8:
                        proj_mms(bank, wb[i % WR], tt,
                                 ecw=(lnt if i == 0 else None),
                                 fullw=(lnt[(NCH - 1, 1)] if i == 1 else None))
                    else:
                        proj_mms(bank, wb[i % WR], tt)
                    mmn = cT.n
                    if layer == 0 and kind == "k":
                        bcol = bk_sb[:, h:h + 1]
                        kb.wait("s", cT, mmn)
                        kb.wait("s", cV, cV.n)
                        sop(lambda e, b=bank, t=tt, bc=bcol: e.activation(
                            wbf[0][:, 512 * t:512 * (t + 1)], b[:, :], AF.Identity,
                            bias=bc))
                        kb.wait("v", cS, cS.n)
                        vop(lambda e, t=tt: e.tensor_mul(
                            wbf[1][:, 512 * t:512 * (t + 1)],
                            wbf[0][:, 512 * t:512 * (t + 1)],
                            ropec_sb[:, 512 * t:512 * (t + 1)]))
                        vop(lambda e, t=tt: e.tensor_mul(
                            wbf[2][:, 512 * t:512 * (t + 1)],
                            wbf[0][:, 512 * t:512 * (t + 1)],
                            ropes_sb[:, 512 * t:512 * (t + 1)]))
                        kb.wait("t", cV, cV.n)
                        mm(pb[7][:, :], p128_sb[:, :],
                           wbf[2][:, 512 * tt:512 * (tt + 1)], True, True,
                           tick=True)
                        kb.wait("v", cT, cT.n)
                        vop(lambda e, t=tt, o=stg: e.tensor_add(
                            o[:, 512 * t:512 * (t + 1)],
                            wbf[1][:, 512 * t:512 * (t + 1)], pb[7][:, :]))
                        bank_free[tt] = (cS, cS.n)
                    else:
                        kb.wait("s", cT, mmn)
                        if layer == 0:
                            bcol = bv_sb[:, h:h + 1]
                            sop(lambda e, b=bank, o=stg, t=tt, bc=bcol:
                                e.activation(o[:, 512 * t:512 * (t + 1)], b[:, :],
                                             AF.Identity, bias=bc))
                        else:
                            sop(lambda e, b=bank, o=stg, t=tt: e.copy(
                                o[:, 512 * t:512 * (t + 1)], b[:, :]))
                        bank_free[tt] = (cS, cS.n)
                cons[i] = cT.n
                if layer == 0 and kind == "k":
                    kb.wait("y", cV, cV.n)
                else:
                    kb.wait("y", cS, cS.n)
                if kind == "k":
                    qtr = h // 4
                    dst = ccT[(layer, "ki", qtr)]
                    ydma(cSg[i % 2], dst[128 * (h % 4):128 * (h % 4) + 128, :],
                         stg[:, :])
                    dmad[i] = cSg[i % 2].n
                    if h % 4 == 3 and do_cc:
                        kb.wait("g", cSg[0])
                        kb.wait("g", cSg[1])
                        kb.tick("g", cCC, lambda e, ci=dst,
                                co=ccT[(layer, "ko", qtr)]: e.collective_compute(
                                    "AllGather", ALU.bypass, replica_groups=grp,
                                    ins=[ci], outs=[co]))
                        agk[qtr] = cCC.n
                else:
                    for vh in range(4):
                        ydma(cSg[i % 2],
                             ccT[(layer, "vi", vh)][128 * h:128 * (h + 1), :],
                             stg[:, 256 * vh:256 * (vh + 1)])
                    dmad[i] = cSg[i % 2].n
            agv = [0, 0, 0, 0]
            for vh in range(4):
                kb.wait("g", cSg[0])
                kb.wait("g", cSg[1])
                if do_cc:
                    kb.tick("g", cCC, lambda e, ci=ccT[(layer, "vi", vh)],
                            co=ccT[(layer, "vo", vh)]: e.collective_compute(
                                "AllGather", ALU.bypass, replica_groups=grp,
                                ins=[ci], outs=[co]))
                agv[vh] = cCC.n
            ydma(cM, m_sb[:, :], (m0t if layer == 0 else m1t)[:, :])
            vtn = [0, 0, 0, 0]
            if do_vt:
                for vh in range(4):
                    kb.wait("y", cCC, agv[vh])
                    for r in range(2):
                        for m2 in range(2 * vh, 2 * vh + 2):
                            a = r * 8 + m2
                            src = ccT[(layer, "vo", vh)][E * r: E * (r + 1),
                                                         128 * (m2 - 2 * vh):
                                                         128 * (m2 - 2 * vh) + 128]
                            ydma(cVT, arenaA[:, 2048 * a:2048 * (a + 1)], src,
                                 transpose=True)
                    vtn[vh] = cVT.n
            return layer, agk, vtn, cT.n

        def attn_phase(layer, agk, vtn, kvdone_t, side=None):
            wqw = wq0 if layer == 0 else wq1
            msk = m_sb
            AO = xB
            Pb = arenaB
            scale = SC_L0 if layer == 0 else SC_L1
            fillval = 0.0 if layer == 0 else 1.0
            kload = {}
            headdone = {}
            kb.wait("t", cM)
            vt_waited = [False] * 4
            agk_waited = [False] * 4
            for h in range(H):
                kb.wait("g", cT, kvdone_t if h < WR else headdone[h - WR])
                gdma(cW[h % WR], wb[h % WR][:, :], wqw[h][:, :])
                qldn = cW[h % WR].n
                if h >= 2:
                    kb.wait("y", cT, headdone[h - 2])
                if not agk_waited[h // 4]:
                    kb.wait("y", cCC, agk[h // 4])
                    agk_waited[h // 4] = True
                ko = ccT[(layer, "ko", h // 4)]
                for r in range(2):
                    ydma(cK[h % 2], kbuf[h % 2][:, 1024 * r:1024 * (r + 1)],
                         ko[512 * r + 128 * (h % 4): 512 * r + 128 * (h % 4) + 128, :])
                kload[h] = cK[h % 2].n
                for tt in range(2):
                    kb.wait("t", cW[h % WR], qldn)
                    kb.wait("t", cS, cS.n)
                    kb.wait("t", cV, cV.n)
                    proj_mms(pb[6], wb[h % WR], tt)
                    mmn = cT.n
                    kb.wait("s", cT, mmn)
                    if layer == 0:
                        kb.wait("s", cV, cV.n)
                        sop(lambda e, t=tt, h_=h: e.activation(
                            wbf[0][:, 512 * t:512 * (t + 1)], pb[6][:, :],
                            AF.Identity, bias=bq_sb[:, h_:h_ + 1]), tick=True)
                        kb.wait("v", cS, cS.n)
                        vop(lambda e, t=tt: e.tensor_mul(
                            wbf[1][:, 512 * t:512 * (t + 1)],
                            wbf[0][:, 512 * t:512 * (t + 1)],
                            ropec_sb[:, 512 * t:512 * (t + 1)]))
                        vop(lambda e, t=tt: e.tensor_mul(
                            wbf[2][:, 512 * t:512 * (t + 1)],
                            wbf[0][:, 512 * t:512 * (t + 1)],
                            ropes_sb[:, 512 * t:512 * (t + 1)]), tick=True)
                        kb.wait("t", cV, cV.n)
                        mm(pb[7][:, :], p128_sb[:, :],
                           wbf[2][:, 512 * tt:512 * (tt + 1)], True, True,
                           tick=True)
                        kb.wait("v", cT, cT.n)
                        vop(lambda e, t=tt: e.tensor_add(
                            qh[:, 512 * t:512 * (t + 1)],
                            wbf[1][:, 512 * t:512 * (t + 1)], pb[7][:, :]),
                            tick=True)
                    else:
                        sop(lambda e, t=tt: e.copy(
                            qh[:, 512 * t:512 * (t + 1)], pb[6][:, :]), tick=True)
                kb.wait("t", cV, cV.n)
                kb.wait("t", cS, cS.n)
                kb.wait("t", cK[h % 2], kload[h])
                exp_tick = {}
                for j in range(2):
                    kb.wait("v", cT, cT.n)
                    fk = [(k, c0_of(j, k)) for k in range(0, 8 * j + 8)
                          if c0_of(j, k) > 0]
                    for fi, (k, c0) in enumerate(fk):
                        vop(lambda e, k_=k, c_=c0: e.memset(
                            Pb[:, 512 * k_:512 * k_ + c_], fillval),
                            tick=(fi == len(fk) - 1))
                    kb.wait("s", cV, cV.n)
                    kb.wait("t", cV, cV.n)
                    for k in range(0, 8 * j + 8):
                        c0 = c0_of(j, k)
                        bank = pb[k % 2]
                        if k >= 2:
                            kb.wait("t", cS, exp_tick[k - 2])
                        has_mask = (layer == 0 and k >= 8 * j)
                        mm(bank[:, c0:512],
                           kbuf[h % 2][:, kcol(k):kcol(k) + 128],
                           qh[:, 512 * j + c0:512 * (j + 1)],
                           start=True, stop=not has_mask, tick=not has_mask)
                        if has_mask:
                            mm(bank[:, c0:c0 + 128], identb_sb[:, :],
                               msk[:, 128 * k:128 * (k + 1)], start=False,
                               stop=True, tick=True)
                        smn = cT.n
                        if layer == 0 or k < 8 * j:
                            kb.wait("s", cT, smn)
                            sop(lambda e, k_=k, c_=c0, b=bank: e.activation(
                                Pb[:, 512 * k_ + c_:512 * (k_ + 1)], b[:, c_:512],
                                AF.Exp, scale=scale), tick=True)
                        else:
                            kb.wait("v", cT, smn)
                            kb.wait("v", cS, cS.n)
                            vop(lambda e, k_=k, c_=c0, b=bank: e.tensor_mul(
                                wk[5][:, 0:128], b[:, c_:c_ + 128],
                                msk[:, 128 * k_:128 * (k_ + 1)]), tick=True)
                            kb.wait("s", cV, cV.n)
                            sop(lambda e, k_=k, c_=c0: e.activation(
                                Pb[:, 512 * k_ + c_:512 * k_ + c_ + 128],
                                wk[5][:, 0:128], AF.Exp, scale=scale),
                                tick=(c0 + 128 >= 512))
                            if c0 + 128 < 512:
                                sop(lambda e, k_=k, c_=c0, b=bank: e.activation(
                                    Pb[:, 512 * k_ + c_ + 128:512 * (k_ + 1)],
                                    b[:, c_ + 128:512], AF.Exp, scale=scale),
                                    tick=True)
                        exp_tick[k] = cS.n
                    kb.wait("t", cS, cS.n)
                    vtg = 3 if layer == 1 else (1 if j == 0 else 3)
                    if not vt_waited[vtg]:
                        kb.wait("t", cVT, vtn[vtg])
                        vt_waited[vtg] = True
                    obank, zbank = pb[2 + j], pb[4 + j]
                    krange = list(range(0, 8 * j + 8)) if layer == 0 else (
                        list(range(16)) if j == 0 else list(range(16)))
                    zrange = set(range(0, 8 * j + 8))
                    zlast = max(k for k in krange if k in zrange)
                    first = True
                    for k in krange:
                        inwin = k in zrange
                        last = (k == krange[-1])
                        rhs = Pb[:, 512 * k:512 * (k + 1)] if inwin else allones[:, :]
                        vt = arenaA[:, 2048 * vtslot(k) + 128 * h:
                                    2048 * vtslot(k) + 128 * (h + 1)]
                        mm(obank[:, :], vt, rhs, start=first, stop=last,
                           tick=(last and not (inwin and True)))
                        if inwin:
                            mm(zbank[:, :], ones128_sb[:, :], rhs,
                               start=first, stop=(k == zlast), tick=(k == zlast))
                        if layer == 1 and first and j == 0:
                            mm(zbank[:, :], c1024_sb[:, :], allones[0:1, 0:512],
                               start=False, stop=False)
                        first = False
                    pvn = cT.n
                    kb.wait("s", cT, pvn)
                    sop(lambda e, z=zbank: e.activation(wk[0][:, :], z[:, :], AF.Ln),
                        tick=True)
                    kb.wait("s", cS, cS.n)
                    sop(lambda e: e.activation(wk[1][:, :], wk[0][:, :], AF.Exp,
                                               scale=-1.0), tick=True)
                    kb.wait("v", cS, cS.n)
                    vop(lambda e, o=obank, h_=h, j_=j: e.tensor_mul(
                        AO[:, 1024 * h_ + 512 * j_:1024 * h_ + 512 * (j_ + 1)],
                        o[:, :], wk[1][:, :]), tick=True)
                headdone[h] = cT.n
                if side is not None and h < len(side):
                    for fn in side[h]:
                        fn()
            kb.wait("t", cV, cV.n)
            return AO

        def wo_phase(layer, AO):
            wow = wo0 if layer == 0 else wo1
            cons = {}
            entry_t = cT.n
            evac = {0: (cV, cV.n), 1: (cV, cV.n)}
            for fh in range(NCH):
                kb.wait("g", cT, cons[fh - WR] if fh >= WR else entry_t)
                gdma(cW[fh % WR], wb2[fh % WR][:, :], wow[fh][:, :])
                ldn = cW[fh % WR].n
                for tt in range(2):
                    bank = pb[tt]
                    kb.wait("t", cW[fh % WR], ldn)
                    ch_, n_ = evac[tt]
                    kb.wait("t", ch_, n_)
                    for ec in range(NCH):
                        mm(bank[:, :], wb2[fh % WR][:, 128 * ec:128 * (ec + 1)],
                           AO[:, 1024 * ec + 512 * tt:1024 * ec + 512 * (tt + 1)],
                           start=(ec == 0), stop=(ec == NCH - 1),
                           tick=(ec == NCH - 1))
                    kb.wait("v", cT, cT.n)
                    if layer == 0:
                        vop(lambda e, b=bank, f=fh, t=tt: e.scalar_tensor_tensor(
                            rr(f, t), b[:, :], bo_sb[:, f:f + 1], cur(f, t),
                            ALU.add, ALU.add), tick=True)
                    else:
                        vop(lambda e, b=bank, f=fh, t=tt: e.tensor_add(
                            rr(f, t), b[:, :], cur(f, t)), tick=True)
                    evac[tt] = (cV, cV.n)
                cons[fh] = cT.n

        def ln_phase(dst, src=None, desc=False, mrow=None):
            if src is None:
                src = rr
            kb.wait("t", cV, cV.n)
            kb.wait("s", cV, cV.n)
            prev_tick = cT.n
            for c in range(NCH):
                for tt in range(2):
                    kb.wait("s", cT, prev_tick)
                    sop(lambda e, c_=c, t=tt: e.activation(
                        wbf[0][:, 512 * t:512 * (t + 1)], src(c_, t), AF.Square))
                    sqn = cS.n
                    mm(pb[tt][:, :], ones128_sb[:, :], src(c, tt),
                       start=(c == 0), stop=(c == NCH - 1))
                    kb.wait("t", cS, sqn)
                    mm(pb[2 + tt][:, :], ones128_sb[:, :],
                       wbf[0][:, 512 * tt:512 * (tt + 1)],
                       start=(c == 0), stop=(c == NCH - 1), tick=True)
                    prev_tick = cT.n
            stn = cT.n
            for tt in range(2):
                nr = wk[3] if tt == 0 else wk[5]
                kb.wait("v", cT, stn)
                vop(lambda e, t=tt: e.tensor_scalar(
                    wk[0][:, :], pb[2 + t][:, :], RDIV, None, ALU.mult))
                vop(lambda e, t=tt: e.tensor_scalar(
                    wk[2][:, :], pb[t][:, :], RDIV, None, ALU.mult))
                kb.wait("s", cV, cV.n)
                sop(lambda e: e.activation(wk[1][:, :], wk[2][:, :], AF.Square))
                kb.wait("v", cS, cS.n)
                vop(lambda e: e.tensor_sub(wk[0][:, :], wk[0][:, :], wk[1][:, :]))
                kb.wait("s", cV, cV.n)
                sop(lambda e: e.activation(wk[1][:, :], wk[0][:, :], AF.Ln,
                                           bias=epsT[:, :]))
                kb.wait("s", cS, cS.n)
                sop(lambda e: e.activation(wk[1][:, :], wk[1][:, :], AF.Exp,
                                           scale=-0.5))
                kb.wait("v", cS, cS.n)
                vop(lambda e, o=nr: e.tensor_scalar(
                    o[:, :], wk[1][:, :], -1.0, None, ALU.mult))
                if mrow is not None:
                    vop(lambda e, t=tt: e.tensor_scalar(
                        mrow(t), pb[t][:, :], RDIV, None, ALU.mult))
            bankfree = cV.n
            corder = list(range(NCH))
            ttorder = [0, 1]
            if desc:
                corder = corder[::-1]
                ttorder = [1, 0]
            lnt = {}
            for c in corder:
                for tt in ttorder:
                    nr = wk[3] if tt == 0 else wk[5]
                    kb.wait("v", cV, cV.n)
                    if mrow is not None:
                        vop(lambda e, c_=c, t=tt: e.scalar_tensor_tensor(
                            wk[1][:, :], mrow(t), 1.0, src(c_, t),
                            ALU.mult, ALU.subtract))
                    else:
                        vop(lambda e, c_=c, t=tt: e.scalar_tensor_tensor(
                            wk[1][:, :], pb[t][:, :], RDIV, src(c_, t),
                            ALU.mult, ALU.subtract))
                    kb.wait("v", cV, cV.n)
                    vop(lambda e, c_=c, t=tt, o=nr: e.tensor_mul(
                        dst(c_, t), wk[1][:, :], o[:, :]))
                    lnt[(c, tt)] = cV.n
            return lnt, bankfree


        def moe_pre(lnt=None):
            kb.wait("v", cA, cA.n)
            lgcopy = [0, 0]
            for tt in range(2):
                if tt == 1:
                    kb.wait("t", cS, cS.n)
                for ec in range(NCH):
                    if lnt is not None:
                        kb.wait("t", cV, lnt[(ec, tt)])
                    mm(pb[6][0:8, :], rw_sb[:, 8 * ec:8 * (ec + 1)], cur(ec, tt),
                       start=(ec == 0), stop=(ec == NCH - 1), tick=(ec == NCH - 1))
                kb.wait("s", cT, cT.n)
                sop(lambda e, t=tt: e.copy(arenaB_f[0:8, 512 * t:512 * (t + 1)],
                                           pb[6][0:8, :]))
                lgcopy[tt] = cS.n
            trn = [0] * 8
            for lb in range(8):
                tt, co = (0, 128 * lb) if lb < 4 else (1, 128 * (lb - 4))
                if lb in (0, 4):
                    kb.wait("t", cS, lgcopy[tt])
                mm(pb[7][:, 8 * lb:8 * lb + 8],
                   arenaB_f[0:8, 512 * tt + co:512 * tt + co + 128],
                   identf_sb[0:8, 0:8], start=True, stop=True, tick=True)
                trn[lb] = cT.n
            for lb in range(8):
                kb.wait("v", cT, trn[lb])
                vop(lambda e, l=lb: e.tensor_copy(mt8[0][:, :],
                                                  pb[7][:, 8 * l:8 * l + 8]))
                vop(lambda e: e.reduce_max(mt8[1][:, 0:1], mt8[0][:, :], AX.X,
                                           negate=True))
                kb.wait("s", cV, cV.n)
                sop(lambda e: e.activation(mt8[2][:, :], mt8[0][:, :], AF.Exp,
                                           bias=mt8[1][:, 0:1],
                                           accum_out=mt8[1][:, 1:2]))
                kb.wait("v", cS, cS.n)
                vop(lambda e: e.reciprocal(mt8[1][:, 2:3], mt8[1][:, 1:2]))
                vop(lambda e: e.tensor_scalar(mt8[2][:, :], mt8[2][:, :],
                                              mt8[1][:, 2:3], None, ALU.mult))

                def top_one(m_dst, oh_dst_bf):
                    vop(lambda e, m=m_dst: e.reduce_max(m, mt8[2][:, :], AX.X))
                    vop(lambda e, m=m_dst: e.tensor_scalar(
                        mt8[0][:, :], mt8[2][:, :], m, None, ALU.is_equal))
                    vop(lambda e: e.tensor_copy(mt8[3][:, :], mt8[0][:, :]))
                    vop(lambda e: e.tensor_add(mt8[3][:, 1:8], mt8[0][:, 1:8],
                                               mt8[0][:, 0:7]))
                    vop(lambda e: e.tensor_copy(wk[0][:, 0:8], mt8[3][:, :]))
                    vop(lambda e: e.tensor_add(wk[0][:, 2:8], mt8[3][:, 2:8],
                                               mt8[3][:, 0:6]))
                    vop(lambda e: e.tensor_copy(mt8[3][:, :], wk[0][:, 0:8]))
                    vop(lambda e: e.tensor_add(mt8[3][:, 4:8], wk[0][:, 4:8],
                                               wk[0][:, 0:4]))
                    vop(lambda e: e.tensor_scalar(mt8[3][:, :], mt8[3][:, :], 1.0,
                                                  None, ALU.is_equal))
                    vop(lambda e: e.tensor_mul(mt8[0][:, :], mt8[0][:, :],
                                               mt8[3][:, :]))
                    vop(lambda e, o=oh_dst_bf: e.tensor_copy(o, mt8[0][:, :]))

                top_one(moe_m1[:, lb:lb + 1], moe_oh1[:, 8 * lb:8 * lb + 8])
                vop(lambda e: e.tensor_scalar(mt8[3][:, :], mt8[0][:, :], -1.0,
                                              1.0, ALU.mult, ALU.add))
                vop(lambda e: e.tensor_mul(mt8[2][:, :], mt8[2][:, :],
                                           mt8[3][:, :]))
                top_one(moe_m2[:, lb:lb + 1], moe_oh2[:, 8 * lb:8 * lb + 8])

        def moe_counts():
            kb.wait("t", cV, cV.n)
            first = True
            for choice in range(2):
                ohs = moe_oh1 if choice == 0 else moe_oh2
                for lb in range(8):
                    co = 8 * (choice * 8 + lb)
                    mm(pb[7][0:1, co:co + 8], ones128_sb[:, 0:1],
                       ohs[:, 8 * lb:8 * lb + 8], start=first, stop=False,
                       tick=(choice == 1 and lb == 7))
                    first = False
            kb.wait("s", cT, cT.n)
            sop(lambda e: e.copy(cntrow[:, :], pb[7][0:1, 0:128]))
            kb.wait("g", cS, cS.n)
            gdma(cA, cnt_in[:, :], cntrow[:, :])
            kb.wait("g", cA)
            kb.tick("g", cCC, lambda e: e.collective_compute(
                "AllGather", ALU.bypass,
                replica_groups=[[0, 1, 2, 3, 4, 5, 6, 7]],
                ins=[cnt_in], outs=[cnt_out]))
            kb.wait("g", cCC, cCC.n)
            for c_ in range(2):
                srcap = cnt_out[:, 64 * c_:64 * (c_ + 1)].rearrange(
                    "core (blk e) -> core blk e", blk=8)
                gdma(cA, cnt_sb1[64 * c_:64 * (c_ + 1), :], srcap,
                     allow_slow_non_contiguous=True)

        def moe_gates():
            kb.wait("t", cA)
            kb.wait("t", cS, cS.n)
            mm(pb[7][0:16, 0:8], mbig_sb[:, :], cnt_sb1[:, :], start=True,
               stop=True, tick=True)
            kb.wait("s", cT, cT.n)
            sop(lambda e: e.copy(moe_s1[0:16, 0:8], pb[7][0:16, 0:8]))
            kb.wait("g", cS, cS.n)
            gdma(cA, offrow[0:1, 0:128], moe_s1[0:16, 0:8],
                 allow_slow_non_contiguous=True)
            kb.wait("t", cA)
            for lb in range(8):
                for choice in range(2):
                    ohs = moe_oh1 if choice == 0 else moe_oh2
                    ms = moe_m1 if choice == 0 else moe_m2
                    r = choice * 8 + lb
                    kb.wait("t", cV, cV.n)
                    mm(pb[6][:, 0:8], u128_sb[:, :], ohs[:, 8 * lb:8 * lb + 8],
                       start=True, stop=False)
                    mm(pb[6][:, 0:8], onesrow_f_sb[:, :],
                       offrow[0:1, 8 * r:8 * r + 8], start=False, stop=True,
                       tick=True)
                    kb.wait("v", cT, cT.n)
                    vop(lambda e, o=ohs, l=lb: e.tensor_tensor(
                        mt8[0][:, :], pb[6][:, 0:8], o[:, 8 * l:8 * l + 8],
                        ALU.mult))
                    vop(lambda e: e.reduce_sum(mt8[1][:, 0:1], mt8[0][:, :], AX.X))
                    vop(lambda e: e.tensor_scalar(mt8[1][:, 1:2], mt8[1][:, 0:1],
                                                  512.0, None, ALU.is_le))
                    vop(lambda e, m=ms, l=lb, ch_=choice: e.tensor_tensor(
                        mt8[2][:, ch_:ch_ + 1], mt8[1][:, 1:2], m[:, l:l + 1],
                        ALU.mult))
                vop(lambda e, l=lb: e.tensor_add(moe_g[:, l:l + 1],
                                                 mt8[2][:, 0:1], mt8[2][:, 1:2]))
            kb.wait("t", cV, cV.n)
            mm(pb[7][0:8, 0:128], moe_g[:, :], identf_sb[:, :],
               start=True, stop=True, tick=True)
            kb.wait("s", cT, cT.n)
            sop(lambda e: e.copy(moe_s1[0:8, :], pb[7][0:8, 0:128]))
            kb.wait("g", cS, cS.n)
            gdma(cA, grow[0:1, 0:1024], moe_s1[0:8, :],
                 allow_slow_non_contiguous=True)
            kb.wait("t", cA)
            for tt in range(2):
                kb.wait("t", cS, cS.n)
                mm(pb[7][:, :], onesrow_f_sb[:, :],
                   grow[0:1, 512 * tt:512 * (tt + 1)], start=True, stop=True,
                   tick=True)
                kb.wait("s", cT, cT.n)
                sop(lambda e, t=tt: e.copy(gb_sb[:, 512 * t:512 * (t + 1)],
                                           pb[7][:, :]))

        def ffn_phase(lnt=None, bankfree=None, sides=None):
            wcons = {}
            lni = 0
            for tt in range(2):
                bf0 = bankfree if (tt == 0 and bankfree is not None) else cV.n
                gel = {0: (cV, bf0), 1: (cV, bf0)}
                for f in range(64):
                    if tt == 0 and sides and f in sides:
                        for fn in sides[f]:
                            fn()
                    li = lni + f
                    kb.wait("g", cT, wcons[li - WR] if li >= WR else cT.n)
                    gdma(cW[li % WR], wb[li % WR][:, :], w1[f][:, :])
                    ldn = cW[li % WR].n
                    bank = pb[f % 2]
                    kb.wait("t", cW[li % WR], ldn)
                    ch_, n_ = gel[f % 2]
                    kb.wait("t", ch_, n_)
                    if lnt is not None and tt == 1 and f == 0:
                        kb.wait("t", cV, lnt[(NCH - 1, 1)])
                    for ec in range(NCH):
                        if lnt is not None and tt == 0 and f == 0:
                            kb.wait("t", cV, lnt[(ec, 0)])
                        mm(bank[:, :], wb[li % WR][:, 128 * ec:128 * (ec + 1)],
                           cur(ec, tt), start=(ec == 0), stop=(ec == NCH - 1),
                           tick=(ec == NCH - 1))
                    wcons[li] = cT.n
                    kb.wait("s", cT, cT.n)
                    sop(lambda e, b=bank, f_=f: e.activation(
                        arenaA[:, 512 * f_:512 * (f_ + 1)], b[:, :], AF.Gelu,
                        bias=b1_sb[:, f_:f_ + 1]))
                    gel[f % 2] = (cS, cS.n)
                lni += 64
                if tt == 0:
                    moe_gates()
                kb.wait("t", cS, cS.n)
                ev2 = {0: (cV, cV.n), 1: (cV, cV.n)}
                for oh in range(NCH):
                    for qq in range(4):
                        li = lni + oh * 4 + qq
                        kb.wait("g", cT, wcons[li - WR] if li - WR in wcons else cT.n)
                        gdma(cW[li % WR], wb[li % WR][:, :],
                             w2[oh][:, 2048 * qq:2048 * (qq + 1)])
                        ldn = cW[li % WR].n
                        bank = pb[2 + oh % 2]
                        kb.wait("t", cW[li % WR], ldn)
                        if qq == 0:
                            ch_, n_ = ev2[oh % 2]
                            kb.wait("t", ch_, n_)
                        for fi in range(16):
                            fc = 16 * qq + fi
                            mm(bank[:, :], wb[li % WR][:, 128 * fi:128 * (fi + 1)],
                               arenaA[:, 512 * fc:512 * (fc + 1)],
                               start=(fc == 0), stop=(fc == 63),
                               tick=(fi == 15))
                        wcons[li] = cT.n
                    kb.wait("v", cT, cT.n)
                    vop(lambda e, b=pb[2 + oh % 2], o=oh, t=tt:
                        e.scalar_tensor_tensor(
                            wk[4][:, :], b[:, :], b2_sb[:, o:o + 1],
                            gb_sb[:, 512 * t:512 * (t + 1)], ALU.add, ALU.mult))
                    kb.wait("v", cV, cV.n)
                    vop(lambda e, o=oh, t=tt: e.tensor_add(
                        aox(o, t), wk[4][:, :], cur(o, t)))
                    ev2[oh % 2] = (cV, cV.n)
                lni += 64

        def emit_out_stream(srcfn):
            idx = 0
            for c in range(NCH):
                for tt in range(2):
                    buf = wk[idx % 2]
                    if idx >= 2:
                        kb.wait("s", cO, cO.n)
                    kb.wait("s", cV, cV.n)
                    sop(lambda e, c_=c, t=tt, b=buf: e.copy(b[:, :], srcfn(c_, t)))
                    kb.wait("y", cS, cS.n)
                    ydma(cO, y[128 * c:128 * (c + 1), 512 * tt:512 * (tt + 1)],
                         buf[:, :])
                    idx += 1
            kb.wait("y", cO, cO.n)

        def quant_chain(src, src_i, seg448, segS, out_bf, slots=None):
            if slots is None:
                Af, Bf, Cf = (arenaB_f[:, 1024:2048], arenaB_f[:, 2048:3072],
                              arenaB_f[:, 3072:4096])
                Ai, Bi, Ci = (arenaB_i[:, 1024:2048], arenaB_i[:, 2048:3072],
                              arenaB_i[:, 3072:4096])
            else:
                (Af, Ai), (Bf, Bi), (Cf, Ci) = slots
            nseg = src.shape[-1] // 128
            hseg = nseg // 2
            hw = 128 * hseg
            H2 = (slice(0, hw), slice(hw, 2 * hw))
            v1n = [0, 0]; s1n = [0, 0]; v2n = [0, 0]
            s2n = [0, 0]; v3n = [0, 0]; s3n = [0, 0]; v4n = [0, 0]
            for h in (0, 1):
                hs = H2[h]
                vop(lambda e, q=hs: e.tensor_scalar(Bi[:, q], src_i[:, q],
                                                    0x7FFFFFFF, None,
                                                    ALU.bitwise_and))
                vop(lambda e, q=hs: e.tensor_scalar(Af[:, q], Bf[:, q], 1e-30,
                                                    None, ALU.max))
                v1n[h] = cV.n
            for h in (0, 1):
                kb.wait("s", cV, v1n[h])
                for sg in range(h * hseg, (h + 1) * hseg):
                    sop(lambda e, g=sg: e.activation(
                        Bf[:, 128 * g:128 * (g + 1)],
                        Af[:, 128 * g:128 * (g + 1)], AF.Ln, scale=seg448[g]))
                s1n[h] = cS.n
            for h in (0, 1):
                hs = H2[h]
                kb.wait("v", cS, s1n[h])
                vop(lambda e, q=hs: e.tensor_scalar(Af[:, q], Bf[:, q], INV_LN2,
                                                    -0.5, ALU.mult, ALU.add))
                vop(lambda e, q=hs: e.tensor_scalar(Ci[:, q], Af[:, q], -6.5,
                                                    None, ALU.max))
                v2n[h] = cV.n
            for h in (0, 1):
                hs = H2[h]
                kb.wait("s", cV, v2n[h])
                sop(lambda e, q=hs: e.copy(Bf[:, q], Ci[:, q]))
                sop(lambda e, q=hs: e.activation(Cf[:, q], Bf[:, q], AF.Exp,
                                                 scale=-LN2,
                                                 bias=b3l2T[:, :]))
                s2n[h] = cS.n
            for h in (0, 1):
                hs = H2[h]
                for sg in range(h * hseg, (h + 1) * hseg):
                    vop(lambda e, g=sg: e.tensor_scalar(
                        Af[:, 128 * g:128 * (g + 1)],
                        src[:, 128 * g:128 * (g + 1)], seg448[g], None,
                        ALU.mult))
                kb.wait("v", cS, s2n[h])
                vop(lambda e, q=hs: e.tensor_tensor(src_i[:, q], Af[:, q],
                                                    Cf[:, q], ALU.mult))
                v3n[h] = cV.n
            for h in (0, 1):
                hs = H2[h]
                kb.wait("s", cV, v3n[h])
                sop(lambda e, q=hs: e.activation(Cf[:, q], Bf[:, q], AF.Exp,
                                                 scale=LN2,
                                                 bias=bm3l2T[:, :]))
                sop(lambda e, q=hs: e.copy(Af[:, q], src_i[:, q]))
                s3n[h] = cS.n
            for h in (0, 1):
                hs = H2[h]
                kb.wait("v", cS, s3n[h])
                vop(lambda e, q=hs: e.tensor_tensor(Bf[:, q], Af[:, q],
                                                    Cf[:, q], ALU.mult))
                for sg in range(h * hseg, (h + 1) * hseg):
                    vop(lambda e, g=sg: e.tensor_scalar(
                        out_bf[:, 128 * g:128 * (g + 1)],
                        Bf[:, 128 * g:128 * (g + 1)], segS[g], None, ALU.mult))
                v4n[h] = cV.n
            return {"r32": v3n[1], "out": v4n}

        _wbf2f = wbf[2].bitcast(F32)
        _wk2i = wk[2].bitcast(I32)
        _wk3i = wk[3].bitcast(I32)
        wq_slots = ((_wbf2f[:, 0:512], wbf[2].bitcast(I32)[:, 0:512]),
                    (wk[2][:, :], _wk2i[:, :]),
                    (wk[3][:, :], _wk3i[:, :]))

        def wquant_passA(oh):
            for hf in range(2):
                kb.wait("g", cV, cV.n)
                gdma(cA, wfb[:, 0:1024],
                     wf1[oh][:, 1024 * hf:1024 * (hf + 1)])
                kb.wait("v", cA)
                for sg in range(8):
                    vop(lambda e, g=sg, h_=hf: e.reduce_max(
                        wq_cm[:, 8 * h_ + g:8 * h_ + g + 1],
                        wfb[:, 128 * g:128 * (g + 1)], AX.X,
                        apply_absolute_value=True))
            kb.wait("t", cV, cV.n)
            kb.wait("t", cS, cS.n)
            mm(pb[7][0:16, 0:128], wq_cm[:, :], identf_sb[:, :],
               start=True, stop=True, tick=True)
            kb.wait("v", cT, cT.n)
            vop(lambda e: e.tensor_copy(moe_s1[0:16, :], pb[7][0:16, 0:128]))
            vop(lambda e: e.reduce_max(mt8[0][0:16, 0:1], moe_s1[0:16, :],
                                       AX.X))
            vop(lambda e: e.tensor_scalar(mt8[0][0:16, 1:2],
                                          mt8[0][0:16, 0:1], 1e-4, None,
                                          ALU.max))
            kb.wait("g", cV, cV.n)
            gdma(cA, wq_rows[0:1, 16 * oh:16 * (oh + 1)],
                 mt8[0][0:16, 1:2], allow_slow_non_contiguous=True)

        def wquant_prep():
            kb.wait("v", cA)
            vop(lambda e: e.reciprocal(wq_rows[0:1, 256:288], wq_rows[0:1, 0:32]))
            vop(lambda e: e.tensor_scalar(wq_rows[0:1, 256:288], wq_rows[0:1, 256:288], 448.0,
                                          None, ALU.mult))
            vop(lambda e: e.tensor_scalar(wq_rows[0:1, 0:32], wq_rows[0:1, 0:32],
                                          float(1.0 / 448.0), None, ALU.mult))
            kb.wait("t", cV, cV.n)
            mm(pb[7][:, 0:32], onesrow_f_sb[:, :], wq_rows[0:1, 256:288], start=True,
               stop=True, tick=True)
            kb.wait("s", cT, cT.n)
            sop(lambda e: e.copy(wq_rb448[:, 0:32], pb[7][:, 0:32]))
            kb.wait("t", cS, cS.n)
            mm(pb[7][:, 0:32], onesrow_f_sb[:, :], wq_rows[0:1, 0:32], start=True,
               stop=True, tick=True)
            kb.wait("s", cT, cT.n)
            sop(lambda e: e.copy(wq_s[:, 0:32], pb[7][:, 0:32]))
            kb.wait("v", cS, cS.n)

        def wquant_passB(oh, hfs=(0, 1)):
            for hf in hfs:
                kb.wait("g", cV, cV.n)
                gdma(cA, wfb[:, 0:1024],
                     wf1[oh][:, 1024 * hf:1024 * (hf + 1)])
                kb.wait("v", cA)
                out_bf = wbf[3 + hf]
                kb.wait("v", cO, cO.n)
                for q2 in range(2):
                    seg448 = [wq_rb448[:, 16 * oh + 8 * hf + 4 * q2 + g:
                                       16 * oh + 8 * hf + 4 * q2 + g + 1]
                              for g in range(4)]
                    segS = [wq_s[:, 16 * oh + 8 * hf + 4 * q2 + g:
                                 16 * oh + 8 * hf + 4 * q2 + g + 1]
                            for g in range(4)]
                    quant_chain(wfb[:, 512 * q2:512 * (q2 + 1)],
                                wfb_i[:, 512 * q2:512 * (q2 + 1)], seg448, segS,
                                out_bf[:, 512 * q2:512 * (q2 + 1)],
                                slots=wq_slots)
                kb.wait("y", cV, cV.n)
                ydma(cO, wdq_loc[128 * oh:128 * (oh + 1),
                                 1024 * hf:1024 * (hf + 1)], out_bf[:, :])

        xq_bk = [0, 0]
        xq_srcfree = [0]

        def xquant_block(tb, hf, lnw=None):
            tt, co = (tb // 4, 128 * (tb % 4))
            kb.wait("s", cV, xq_srcfree[0])
            for g in range(8):
                c = 8 * hf + g
                bk = pb[5 + g % 2]
                if lnw is not None:
                    kb.wait("t", cV, lnw[(c, tt)])
                kb.wait("t", cS, xq_bk[g % 2])
                mm(bk[:, 0:128],
                   arenaA_f[:, 1024 * c + 512 * tt + co:
                            1024 * c + 512 * tt + co + 128],
                   identf_sb[:, :], start=True, stop=True, tick=True)
                kb.wait("s", cT, cT.n)
                sop(lambda e, g_=g, b=bk: e.copy(
                    arenaB_f[:, 128 * g_:128 * (g_ + 1)], b[:, 0:128]))
                xq_bk[g % 2] = cS.n
            kb.wait("v", cS, cS.n)
            for g in range(8):
                vop(lambda e, g_=g: e.reduce_max(
                    q_am[:, g_:g_ + 1], arenaB_f[:, 128 * g_:128 * (g_ + 1)],
                    AX.X, apply_absolute_value=True))
            vop(lambda e: e.tensor_scalar(q_am[:, 0:8], q_am[:, 0:8], 1e-4,
                                          None, ALU.max))
            vop(lambda e: e.reciprocal(q_rc448[:, 0:8], q_am[:, 0:8]))
            vop(lambda e: e.tensor_scalar(q_rc448[:, 0:8], q_rc448[:, 0:8],
                                          448.0, None, ALU.mult))
            vop(lambda e: e.tensor_scalar(q_s[:, 0:8], q_am[:, 0:8],
                                          float(1.0 / 448.0), None, ALU.mult))
            seg448 = [q_rc448[:, g:g + 1] for g in range(8)]
            segS = [q_s[:, g:g + 1] for g in range(8)]
            qn = quant_chain(arenaB_f[:, 0:1024], arenaB_i[:, 0:1024], seg448,
                             segS, wbf[0][:, :])
            xq_srcfree[0] = qn["r32"]
            for g in range(8):
                c = 8 * hf + g
                bk = pb[5 + g % 2]
                kb.wait("t", cV, qn["out"][g // 4])
                kb.wait("t", cS, xq_bk[g % 2])
                mm(bk[:, 0:128],
                   wbf[0][:, 128 * g:128 * (g + 1)],
                   identb_sb[:, :], start=True, stop=True, tick=True)
                kb.wait("s", cT, cT.n)
                sop(lambda e, c_=c, t=tt, co_=co, b=bk: e.copy(
                    xB[:, 1024 * c_ + 512 * t + co_:
                       1024 * c_ + 512 * t + co_ + 128], b[:, 0:128]))
                xq_bk[g % 2] = cS.n

        agw_n = [0]

        def wquant_gather():
            kb.wait("g", cO, cO.n)
            kb.tick("g", cCC, lambda e: e.collective_compute(
                "AllGather", ALU.bypass,
                replica_groups=[[0, 1, 2, 3, 4, 5, 6, 7]],
                ins=[wdq_loc], outs=[wdqG]))
            agw_n[0] = cCC.n

        def ffn1_phase(agw, lnt=None, bankfree=None):
            wcons = {}
            kb.wait("g", cCC, agw)
            hf_tick = {}
            entry_t = cT.n
            for hf in (1, 0):
                for tb in range(4):
                    xquant_block(tb, hf, lnw=lnt)
                hf_tick[(0, hf)] = cS.n
            for tt in range(2):
                kb.wait("t", cV, cV.n)
                bf0 = bankfree if (tt == 0 and bankfree is not None) else cV.n
                ev = {0: (cV, bf0), 1: (cV, bf0)}
                for oh in range(NCH):
                    li = tt * NCH + oh
                    kb.wait("g", cT,
                            wcons[li - WR] if li - WR in wcons else entry_t)
                    gdma(cW[li % WR], wb[li % WR][:, :],
                         wdqG[128 * oh:128 * (oh + 1), :])
                    ldn = cW[li % WR].n
                    bank = pb[tt]
                    kb.wait("t", cW[li % WR], ldn)
                    ch_, n_ = ev[tt]
                    kb.wait("t", ch_, n_)
                    ecs = list(range(8, 16)) + list(range(8))
                    for pos, ec in enumerate(ecs):
                        if pos % 8 == 0:
                            kb.wait("t", cS, hf_tick[(tt, ec // 8)])
                        mm(bank[:, :], wb[li % WR][:, 128 * ec:128 * (ec + 1)],
                           xB[:, 1024 * ec + 512 * tt:1024 * ec + 512 * (tt + 1)],
                           start=(pos == 0), stop=(pos == NCH - 1),
                           tick=(pos == NCH - 1))
                    wcons[li] = cT.n
                    kb.wait("s", cT, cT.n)
                    sop(lambda e, b=bank: e.copy(wbf[1][:, 0:512], b[:, :]))
                    kb.wait("v", cS, cS.n)
                    vop(lambda e, o=oh, t=tt: e.tensor_add(
                        cur(o, t), arenaA_f[:, 1024 * o + 512 * t:
                                            1024 * o + 512 * (t + 1)],
                        wbf[1][:, 0:512]))
                    ev[tt] = (cV, cV.n)
                    if tt == 0 and oh in (1, 3):
                        hf = 1 if oh == 1 else 0
                        for tb in range(4, 8):
                            xquant_block(tb, hf)
                        hf_tick[(1, hf)] = cS.n

        def final_ln():
            kb.wait("t", cV, cV.n)
            kb.wait("s", cV, cV.n)
            prev_tick = cT.n
            for c in range(NCH):
                for tt in range(2):
                    kb.wait("s", cT, prev_tick)
                    sop(lambda e, c_=c, t=tt: e.activation(
                        wbf[0][:, 512 * t:512 * (t + 1)], cur(c_, t), AF.Square))
                    sqn = cS.n
                    mm(pb[tt][:, :], ones128_sb[:, :], cur(c, tt),
                       start=(c == 0), stop=(c == NCH - 1))
                    kb.wait("t", cS, sqn)
                    mm(pb[2 + tt][:, :], ones128_sb[:, :],
                       wbf[0][:, 512 * tt:512 * (tt + 1)],
                       start=(c == 0), stop=(c == NCH - 1), tick=True)
                    prev_tick = cT.n
            stn = cT.n
            for tt in range(2):
                nr = wk[3] if tt == 0 else wk[5]
                kb.wait("v", cT, stn)
                vop(lambda e, t=tt: e.tensor_scalar(
                    wk[0][:, :], pb[2 + t][:, :], RDIV, None, ALU.mult))
                vop(lambda e, t=tt: e.tensor_scalar(
                    wk[2][:, :], pb[t][:, :], RDIV, None, ALU.mult))
                kb.wait("s", cV, cV.n)
                sop(lambda e: e.activation(wk[1][:, :], wk[2][:, :], AF.Square))
                kb.wait("v", cS, cS.n)
                vop(lambda e: e.tensor_sub(wk[0][:, :], wk[0][:, :], wk[1][:, :]))
                kb.wait("s", cV, cV.n)
                sop(lambda e: e.activation(wk[1][:, :], wk[0][:, :], AF.Ln,
                                           bias=epsT[:, :]))
                sop(lambda e: e.activation(wk[1][:, :], wk[1][:, :], AF.Exp,
                                           scale=-0.5))
                kb.wait("v", cS, cS.n)
                vop(lambda e, o=nr: e.tensor_scalar(
                    o[:, :], wk[1][:, :], -1.0, None, ALU.mult))
                for c in range(NCH):
                    buf = wk[2] if c % 2 == 0 else wk[4]
                    vop(lambda e, c_=c, t=tt: e.scalar_tensor_tensor(
                        wk[1][:, :], pb[t][:, :], RDIV, cur(c_, t),
                        ALU.mult, ALU.subtract))
                    kb.wait("v", cO, cO.n)
                    vop(lambda e, b=buf, o=nr: e.tensor_mul(
                        b[:, :], wk[1][:, :], o[:, :]))
                    kb.wait("y", cV, cV.n)
                    ydma(cO, y[128 * c:128 * (c + 1), 512 * tt:512 * (tt + 1)],
                         buf[:, :])
            kb.wait("y", cO, cO.n)

        if stage == "ld":
            emit_out_stream(cur)
        elif stage in ("kv", "kvt", "kvnc"):
            kv_phase(0, do_vt=(stage == "kvt"), do_cc=(stage != "kvnc"))
            emit_out_stream(cur)
        else:
            _, agk0, vtn0, kvt0 = kv_phase(0)
            AO0 = attn_phase(0, agk0, vtn0, kvt0)
            if stage == "attn":
                emit_out_stream(lambda c, t: AO0[:, 1024 * c + 512 * t:
                                                 1024 * c + 512 * (t + 1)])
            else:
                wo_phase(0, AO0)
                lnt1, bf1 = ln_phase(
                    cur, mrow=lambda t: arenaB_f[:, 1024 + 512 * t:
                                                 1536 + 512 * t])
                if stage == "l0a":
                    emit_out_stream(cur)
                else:
                    wquant_passA(0)
                    wquant_passA(1)
                    wquant_prep()
                    moe_pre(lnt=lnt1)
                    ffn_phase(lnt=lnt1, bankfree=bf1, sides={
                        32: [moe_counts],
                        36: [lambda: wquant_passB(0, (0,))],
                        42: [lambda: wquant_passB(0, (1,))],
                        48: [lambda: wquant_passB(1, (0,))],
                        54: [lambda: wquant_passB(1, (1,))],
                        60: [wquant_gather],
                    })
                    lnt2, bf2 = ln_phase(
                        cur, src=aox,
                        mrow=lambda t: gb_sb[:, 512 * t:512 * (t + 1)])
                    if stage == "l0f":
                        emit_out_stream(cur)
                    else:
                        _, agk1, vtn1, kvt1 = kv_phase(1, lnt=lnt2,
                                                       bankfree=bf2)
                        AO1 = attn_phase(1, agk1, vtn1, kvt1)
                        agw = agw_n[0]
                        wo_phase(1, AO1)
                        x3f = lambda c, t: arenaA_f[:, 1024 * c + 512 * t:
                                                    1024 * c + 512 * (t + 1)]
                        lnt3, bf3 = ln_phase(
                            x3f, src=rr, desc=True,
                            mrow=lambda t: gb_sb[:, 512 * t:512 * (t + 1)])
                        if stage == "l1a":
                            emit_out_stream(x3f)
                        elif stage == "xd":
                            xq_bk[0] = cS.n
                            xq_bk[1] = cS.n
                            xq_srcfree[0] = bf3
                            for tb in range(8):
                                for hf in (1, 0):
                                    xquant_block(tb, hf, lnw=lnt3)
                            emit_out_stream(lambda c, t: xB[:, 1024 * c + 512 * t:
                                                            1024 * c + 512 * (t + 1)])
                        else:
                            xq_bk[0] = cS.n
                            xq_bk[1] = cS.n
                            xq_srcfree[0] = bf3
                            ffn1_phase(agw, lnt=lnt3, bankfree=bf3)
                            final_ln()

        with nc.Block() as block:
            block.tensor(lambda e: [f(e) for f in kb.q["t"]])
            block.vector(lambda e: [f(e) for f in kb.q["v"]])
            block.scalar(lambda e: [f(e) for f in kb.q["s"]])
            block.gpsimd(lambda e: [f(e) for f in kb.q["g"]])
            block.sync(lambda e: [f(e) for f in kb.q["y"]])

    return nc



def _tile_lhsT(Wct, fh_chunks):
    C, F = Wct.shape
    A = Wct.reshape(C // 128, 128, F // 128, 128).transpose(2, 1, 0, 3)
    return np.ascontiguousarray(A.reshape(F // 128, 128, C)).astype(np.float32)


def _bias_cols(b):
    return np.ascontiguousarray(b.reshape(-1, 128).T).astype(np.float32)


def make_core_inputs(inp, c):
    b, p = c // 2, c % 2
    cols = np.concatenate([np.arange(128 * (2 * lb + p), 128 * (2 * lb + p) + 128)
                           for lb in range(8)])
    x = np.asarray(inp["L0_attn_x"], np.float32)[b][cols]
    d = {}
    d["x0"] = np.ascontiguousarray(x.T)
    d["wq0"] = _tile_lhsT(np.asarray(inp["L0_attn_Wq"], np.float32).T, H)
    d["wk0"] = _tile_lhsT(np.asarray(inp["L0_attn_Wk"], np.float32).T, H)
    d["wv0"] = _tile_lhsT(np.asarray(inp["L0_attn_Wv"], np.float32).T, H)
    d["wo0"] = _tile_lhsT(np.asarray(inp["L0_attn_Wo"], np.float32).T, NCH)
    d["bq_c"] = _bias_cols(np.asarray(inp["L0_attn_bq"], np.float32))
    d["bk_c"] = _bias_cols(np.asarray(inp["L0_attn_bk"], np.float32))
    d["bv_c"] = _bias_cols(np.asarray(inp["L0_attn_bv"], np.float32))
    d["bo_c"] = _bias_cols(np.asarray(inp["L0_attn_bo"], np.float32))
    d["b1_c"] = _bias_cols(np.asarray(inp["L0_ffn_b1"], np.float32))
    d["b2_c"] = _bias_cols(np.asarray(inp["L0_ffn_b2"], np.float32))
    rwt = np.asarray(inp["L0_ffn_router_w"], np.float32).T
    d["rw"] = np.ascontiguousarray(
        rwt.reshape(NCH, 128, 8).transpose(1, 0, 2).reshape(128, NCH * 8)).astype(np.float32)
    d["w1"] = _tile_lhsT(np.asarray(inp["L0_ffn_w1"], np.float32).T, 64)
    d["w2"] = _tile_lhsT(np.asarray(inp["L0_ffn_w2"], np.float32).T, NCH)
    d["wq1"] = _tile_lhsT(np.asarray(inp["L1_attn_Wq"], np.float32), H)
    d["wk1"] = _tile_lhsT(np.asarray(inp["L1_attn_Wk"], np.float32), H)
    d["wv1"] = _tile_lhsT(np.asarray(inp["L1_attn_Wv"], np.float32), H)
    d["wo1"] = _tile_lhsT(np.asarray(inp["L1_attn_Wo"], np.float32), H)
    _wf1_all = _tile_lhsT(np.asarray(inp["L1_ffn_weight"], np.float32).T, NCH)
    d["wf1"] = np.ascontiguousarray(_wf1_all[[2 * c, 2 * c + 1]])

    pos = (cols).astype(np.float64)
    theta = np.repeat(-np.arange(0, 128, 2, dtype=np.float32) / 128.0, 2)
    theta = np.power(np.float32(10000.0), theta.astype(np.float32)).astype(np.float64)
    ang = pos[None, :] * 0 + 0.0
    ang = theta[:, None] * pos[None, :]
    d["ropec"] = np.cos(ang).astype(ml_dtypes.bfloat16)
    d["ropes"] = np.sin(ang).astype(ml_dtypes.bfloat16)
    P = np.zeros((128, 128), np.float32)
    for i2 in range(0, 128, 2):
        P[i2, i2 + 1] = -1.0
        P[i2 + 1, i2] = 1.0
    d["p128"] = np.ascontiguousarray(P.T).astype(ml_dtypes.bfloat16)
    d["identb"] = np.eye(128, dtype=np.float32).astype(ml_dtypes.bfloat16)
    d["identf"] = np.eye(128, dtype=np.float32)
    d["ones128"] = np.ones((128, 128), ml_dtypes.bfloat16)
    d["onesrow"] = np.ones((1, 128), ml_dtypes.bfloat16)
    d["onesrow_f"] = np.ones((1, 128), np.float32)
    d["u128"] = np.triu(np.ones((128, 128), np.float32)).astype(ml_dtypes.bfloat16)
    d["c1024"] = np.full((1, 128), 1024.0, ml_dtypes.bfloat16)

    tri_strict = np.tril(np.ones((128, 128), np.float32), -1)
    m0 = np.zeros((16, 128, 128), np.float32)
    m1 = np.zeros((16, 128, 128), np.float32)
    for k in range(16):
        j = 0 if k < 8 else 1
        i_min = max(0, int(math.ceil((k - 8 * j - 1) / 2.0)))
        qa = 8 * j + 2 * i_min + p
        if qa == k:
            m0[k] = tri_strict * -30000.0
            m1[k] = np.triu(np.ones((128, 128), np.float32), 0)
        elif qa < k:
            m0[k] = -30000.0
            m1[k] = 0.0
        else:
            m0[k] = 0.0
            m1[k] = 1.0
    d["m0t"] = np.ascontiguousarray(
        m0.transpose(1, 0, 2).reshape(128, 16 * 128)).astype(ml_dtypes.bfloat16)
    d["m1t"] = np.ascontiguousarray(
        m1.transpose(1, 0, 2).reshape(128, 16 * 128)).astype(ml_dtypes.bfloat16)

    myabs = [(b * 16 + 2 * lb + p) for lb in range(8)]
    Mb = np.zeros((128, 16), np.float32)
    for r in range(8):
        a_r = myabs[r]
        for core in range(8):
            for blk in range(8):
                a = (core // 2) * 16 + 2 * blk + (core % 2)
                if a < a_r:
                    Mb[core * 8 + blk, r] = 1.0
                    Mb[64 + core * 8 + blk, 8 + r] = 1.0
        Mb[0:64, 8 + r] = 1.0
    d["mbig"] = Mb.astype(ml_dtypes.bfloat16)
    return d


def kernel(**inputs):
    nc = build(stage="full")
    in_maps = [make_core_inputs(inputs, c) for c in range(8)]
    res = run_bass_kernel_spmd(nc, in_maps, core_ids=list(range(8)))
    out = np.zeros((B, S, E), np.float32)
    for c in range(8):
        b, p = c // 2, c % 2
        yt = res.results[c]["y"]
        xt = np.ascontiguousarray(yt.T)
        for lb in range(8):
            out[b, 128 * (2 * lb + p):128 * (2 * lb + p) + 128, :] = \
                xt[128 * lb:128 * (lb + 1)]
    return out


# version 17
# speedup vs baseline: 1.2757x; 1.0110x over previous
import math
import contextlib
import numpy as np
import ml_dtypes
import concourse.bass as bass
import concourse.mybir as mybir
from concourse.bass_utils import run_bass_kernel_spmd

F32 = mybir.dt.float32
BF16 = mybir.dt.bfloat16
I32 = mybir.dt.int32
AF = mybir.ActivationFunctionType
ALU = mybir.AluOpType
AX = mybir.AxisListType

B, S, E, H, D = 4, 2048, 2048, 16, 128
T = 1024
NCH = 16
LN2C = float(np.log(2.0))
INV_LN2 = float(1.0 / np.log(2.0))
LN2 = float(np.log(2.0))
SC_L0 = float(1.0 / np.sqrt(128.0))
SC_L1 = float(1.0 / np.sqrt(2048.0))
RDIV = float(1.0 / 2048.0)


def c0_of(j, k):
    if k < 8 * j:
        return 0
    return 128 * max(0, int(math.ceil((k - 8 * j - 1) / 2.0)))


def kcol(k):
    return (k % 2) * 1024 + (k // 2) * 128


def vtslot(k):
    return (k % 2) * 8 + (k // 2)


class Ch:
    def __init__(self, stack, nc, name, step=1):
        self.h = stack.enter_context(nc.semaphore(name))
        self.n = 0
        self.step = step


class KB:
    def __init__(self, nc):
        self.q = {k: [] for k in "tvsgy"}

    def on(self, ek, fn):
        self.q[ek].append(fn)

    def wait(self, ek, ch, n=None):
        n = ch.n if n is None else n
        if n > 0:
            self.on(ek, lambda e, h=ch.h, m=n: e.wait_ge(h, m))

    def tick(self, ek, ch, fn):
        ch.n += ch.step
        self.on(ek, lambda e, f=fn, h=ch.h, s=ch.step: f(e).then_inc(h, s))


def build(stage="full"):
    nc = bass.Bass("TRN2", target_bir_lowering=False, debug=False)

    def din(name, shape, dtype=F32):
        return nc.dram_tensor(name, list(shape), dtype, kind="ExternalInput").ap()

    x0 = din("x0", (E, T))
    wq0 = din("wq0", (H, 128, E)); wk0 = din("wk0", (H, 128, E))
    wv0 = din("wv0", (H, 128, E)); wo0 = din("wo0", (NCH, 128, E))
    bq_c = din("bq_c", (128, NCH)); bk_c = din("bk_c", (128, NCH))
    bv_c = din("bv_c", (128, NCH)); bo_c = din("bo_c", (128, NCH))
    b1_c = din("b1_c", (128, 64)); b2_c = din("b2_c", (128, NCH))
    rw = din("rw", (128, NCH * 8))
    w1 = din("w1", (64, 128, E))
    w2 = din("w2", (NCH, 128, 8192))
    wq1 = din("wq1", (H, 128, E)); wk1 = din("wk1", (H, 128, E))
    wv1 = din("wv1", (H, 128, E)); wo1 = din("wo1", (NCH, 128, E))
    wf1 = din("wf1", (2, 128, E))
    ropec = din("ropec", (128, T), BF16); ropes = din("ropes", (128, T), BF16)
    p128 = din("p128", (128, 128), BF16)
    identb = din("identb", (128, 128), BF16); identf = din("identf", (128, 128))
    ones128 = din("ones128", (128, 128), BF16)
    onesrow = din("onesrow", (1, 128), BF16); onesrow_f = din("onesrow_f", (1, 128))
    m0t = din("m0t", (128, 16 * 128), BF16); m1t = din("m1t", (128, 16 * 128), BF16)
    u128 = din("u128", (128, 128), BF16)
    mbig = din("mbig", (128, 16), BF16)
    c1024 = din("c1024", (1, 128), BF16)

    y = nc.dram_tensor("y", [E, T], F32, kind="ExternalOutput").ap()

    ccT = {}
    for L in (0, 1):
        for qtr in range(4):
            ccT[(L, "ki", qtr)] = nc.dram_tensor(
                f"cc{L}K{qtr}_in", [512, T], BF16, kind="Internal").ap()
            ccT[(L, "ko", qtr)] = nc.dram_tensor(
                f"cc{L}K{qtr}_out", [1024, T], BF16, kind="Internal").ap()
            ccT[(L, "vi", qtr)] = nc.dram_tensor(
                f"cc{L}V{qtr}_in", [E, 256], BF16, kind="Internal").ap()
            ccT[(L, "vo", qtr)] = nc.dram_tensor(
                f"cc{L}V{qtr}_out", [2 * E, 256], BF16, kind="Internal").ap()
    cnt_in = nc.dram_tensor("cnt_in", [1, 128], F32, kind="Internal").ap()
    cnt_out = nc.dram_tensor("cnt_out", [8, 128], F32, kind="Internal").ap()
    wdq_loc = nc.dram_tensor("wdq_loc", [256, E], BF16, kind="Internal").ap()
    wdqG = nc.dram_tensor("wdqG", [NCH * 128, E], BF16, kind="Internal").ap()

    stack = contextlib.ExitStack()
    with stack:
        sb = lambda name, shape, dtype: stack.enter_context(
            nc.sbuf_tensor(name, list(shape), dtype))

        xA = sb("xA", (128, NCH * 1024), BF16)
        xB = sb("xB", (128, NCH * 1024), BF16)
        arenaA = sb("arenaA", (128, 32768), BF16)
        arenaB = sb("arenaB", (128, 8192), BF16)
        arenaA_f = arenaA.bitcast(F32)
        arenaB_f = arenaB.bitcast(F32)
        arenaB_i = arenaB.bitcast(I32)

        kbuf = [sb(f"kbuf{i}", (128, 2048), BF16) for i in range(2)]
        qh = sb("qh", (128, 1024), BF16)
        WR = 3
        wb = [sb(f"wb{i}", (128, 2048), BF16) for i in range(WR)]
        wb2 = wb
        wfb = sb("wfb", (128, 1024), F32)
        gb_sb = kbuf[0].bitcast(F32)
        grow = kbuf[1].bitcast(F32)
        ropec_sb = sb("ropec_sb", (128, T), BF16)
        ropes_sb = sb("ropes_sb", (128, T), BF16)
        p128_sb = sb("p128_sb", (128, 128), BF16)
        identb_sb = sb("identb_sb", (128, 128), BF16)
        identf_sb = sb("identf_sb", (128, 128), F32)
        ones128_sb = sb("ones128_sb", (128, 128), BF16)
        onesrow_sb = sb("onesrow_sb", (1, 128), BF16)
        onesrow_f_sb = sb("onesrow_f_sb", (1, 128), F32)
        allones = sb("allones", (128, 512), BF16)
        m_sb = sb("m_sb", (128, 16 * 128), BF16)
        u128_sb = sb("u128_sb", (128, 128), BF16)
        mbig_sb = sb("mbig_sb", (128, 16), BF16)
        c1024_sb = sb("c1024_sb", (1, 128), BF16)
        bq_sb = sb("bq_sb", (128, NCH), F32)
        bk_sb = sb("bk_sb", (128, NCH), F32)
        bv_sb = sb("bv_sb", (128, NCH), F32)
        bo_sb = sb("bo_sb", (128, NCH), F32)
        b1_sb = sb("b1_sb", (128, 64), F32)
        b2_sb = sb("b2_sb", (128, NCH), F32)
        rw_sb = sb("rw_sb", (128, NCH * 8), BF16)
        wbf = [sb(f"wbf{i}", (128, 1024), BF16) for i in range(5)]
        wk = [sb(f"wkt{i}", (128, 512), F32) for i in range(4)]
        wk.append(wbf[0].bitcast(F32))
        wk.append(sb("wkt5", (128, 512), F32))
        epsT = sb("epsT", (128, 1), F32)
        b3l2T = sb("b3l2T", (128, 1), F32)
        bm3l2T = sb("bm3l2T", (128, 1), F32)
        moe_g = sb("moe_g", (128, 8), F32)
        moe_m1 = sb("moe_m1", (128, 8), F32)
        moe_m2 = sb("moe_m2", (128, 8), F32)
        moe_oh1 = sb("moe_oh1", (128, 8 * 8), BF16)
        moe_oh2 = sb("moe_oh2", (128, 8 * 8), BF16)
        mt8 = [sb(f"mt8_{i}", (128, 8), F32) for i in range(4)]
        cntrow = sb("cntrow", (1, 128), F32)
        cnt_sb1 = sb("cnt_sb1", (128, 8), BF16)
        cnt_sb2 = sb("cnt_sb2", (128, 8), BF16)
        offrow = sb("offrow", (1, 128), F32)
        moe_s1 = sb("moe_s1", (16, 128), F32)
        q_am = sb("q_am", (128, 16), F32)
        q_rc448 = sb("q_rc448", (128, 16), F32)
        q_s = sb("q_s", (128, 16), F32)
        wq_cm = sb("wq_cm", (128, 16), F32)
        wq_rows = sb("wq_rows", (1, 512), F32)
        wq_amrow = wq_rows[0:1, 0:256]
        wq_row2 = wq_rows[0:1, 256:512]
        _wbf1f = wbf[1].bitcast(F32)
        wq_rb448 = _wbf1f[:, 0:256]
        wq_s = _wbf1f[:, 256:512]
        wfb_i = wfb.bitcast(I32)

        pb = [stack.enter_context(nc.psum_tensor(f"pb{i}", [128, 512], F32))
              for i in range(8)]

        assert nc.sbuf_bytes_remaining >= 0, nc.sbuf_bytes_remaining
        assert nc.psum_banks_remaining >= 0

        kb = KB(nc)
        cC = Ch(stack, nc, "cC", 16)
        cX = Ch(stack, nc, "cX", 16)
        cW = [Ch(stack, nc, f"cW{i}", 16) for i in range(4)]
        cSg = [Ch(stack, nc, f"cSg{i}", 16) for i in range(2)]
        cK = [Ch(stack, nc, f"cK{i}", 16) for i in range(2)]
        cVT = Ch(stack, nc, "cVT", 16)
        cM = Ch(stack, nc, "cM", 16)
        cO = Ch(stack, nc, "cO", 16)
        cA = Ch(stack, nc, "cA", 16)
        cT = Ch(stack, nc, "cT", 1)
        cV = Ch(stack, nc, "cV", 1)
        cS = Ch(stack, nc, "cS", 1)
        cCC = Ch(stack, nc, "cCC", 1)

        def ydma(ch, out, in_, **kw):
            kb.tick("y", ch, lambda e, o=out, i=in_, k=kw: e.dma_start(out=o, in_=i, **k))

        def gdma(ch, out, in_, **kw):
            kb.tick("g", ch, lambda e, o=out, i=in_, k=kw: e.dma_start(out=o, in_=i, **k))

        def mm(out, lhsT, rhs, start, stop, tick=False):
            fn = lambda e, o=out, l=lhsT, r=rhs, s1=start, s2=stop: \
                e.matmul(o, l, r, start=s1, stop=s2, skip_group_check=True)
            if tick:
                kb.tick("t", cT, fn)
            else:
                kb.on("t", fn)

        def vop(fn, tick=True):
            kb.wait("v", cV, cV.n)
            kb.tick("v", cV, fn)

        def sop(fn, tick=True):
            kb.wait("s", cS, cS.n)
            kb.tick("s", cS, fn)

        for dst, src in [
            (ropec_sb, ropec), (ropes_sb, ropes), (identb_sb, identb),
            (identf_sb, identf), (ones128_sb, ones128), (onesrow_sb, onesrow),
            (onesrow_f_sb, onesrow_f), (p128_sb, p128),
            (u128_sb, u128), (mbig_sb, mbig),
            (c1024_sb, c1024), (bq_sb, bq_c), (bk_sb, bk_c), (bv_sb, bv_c),
            (bo_sb, bo_c), (b1_sb, b1_c), (b2_sb, b2_c),
        ]:
            ydma(cC, dst[:, :], src[:, :])
        n_const = cC.n
        gdma(cX, rw_sb[:, :], rw[:, :])
        for c in range(NCH):
            gdma(cX, xA[:, 1024 * c:1024 * (c + 1)], x0[128 * c:128 * (c + 1), :])
        vop(lambda e: e.memset(epsT[:, :], 1e-5))
        vop(lambda e: e.memset(b3l2T[:, :], float(3.0 * np.log(2.0))))
        vop(lambda e: e.memset(bm3l2T[:, :], float(-3.0 * np.log(2.0))))
        vop(lambda e: e.memset(allones[:, :], 1.0), tick=True)
        kb.wait("t", cC, n_const)
        kb.wait("v", cC, n_const)
        kb.wait("s", cC, n_const)
        kb.wait("t", cX)

        cur = lambda c, t=None: xA[:, 1024 * c + (0 if t is None else 512 * t):
                                   1024 * c + (1024 if t is None else 512 * (t + 1))]
        aox = lambda c, t=None: xB[:, 1024 * c + (0 if t is None else 512 * t):
                                   1024 * c + (1024 if t is None else 512 * (t + 1))]
        rr = lambda c, t=None: arenaA[:, 1024 * c + (0 if t is None else 512 * t):
                                      1024 * c + (1024 if t is None else 512 * (t + 1))]

        def proj_mms(bank, wslot, tt, ecw=None, fullw=None):
            if fullw is not None:
                kb.wait("t", cV, fullw)
            for ec in range(NCH):
                if ecw is not None:
                    kb.wait("t", cV, ecw[(ec, tt)])
                mm(bank[:, :], wslot[:, 128 * ec:128 * (ec + 1)],
                   cur(ec, tt), start=(ec == 0), stop=(ec == NCH - 1),
                   tick=(ec == NCH - 1))

        def kv_phase(layer, do_vt=True, do_cc=True, lnt=None, bankfree=None):
            wkw = wk0 if layer == 0 else wk1
            wvw = wv0 if layer == 0 else wv1
            grp = [[0, 1], [2, 3], [4, 5], [6, 7]]
            steps = ([("k", h) for h in range(H)] + [("v", h) for h in range(H)])
            cons = {}
            dmad = {}
            entry_t = cT.n
            agk = [0, 0, 0, 0]
            bf0 = bankfree if bankfree is not None else cV.n
            bank_free = {0: (cV, bf0), 1: (cV, bf0)}
            for i, (kind, h) in enumerate(steps):
                kb.wait("g", cT, cons[i - WR] if i >= WR else entry_t)
                gdma(cW[i % WR], wb[i % WR][:, :], (wkw if kind == "k" else wvw)[h][:, :])
                ldn = cW[i % WR].n
                stg = wbf[3 + (i % 2)]
                if i >= 2:
                    kb.wait("s", cSg[i % 2], dmad[i - 2])
                    kb.wait("v", cSg[i % 2], dmad[i - 2])
                for tt in range(2):
                    bank = pb[tt]
                    kb.wait("t", cW[i % WR], ldn)
                    ch_, n_ = bank_free[tt]
                    kb.wait("t", ch_, n_)
                    if lnt is not None and i < 8:
                        proj_mms(bank, wb[i % WR], tt,
                                 ecw=(lnt if i == 0 else None),
                                 fullw=(lnt[(NCH - 1, 1)] if i == 1 else None))
                    else:
                        proj_mms(bank, wb[i % WR], tt)
                    mmn = cT.n
                    if layer == 0 and kind == "k":
                        bcol = bk_sb[:, h:h + 1]
                        kb.wait("s", cT, mmn)
                        kb.wait("s", cV, cV.n)
                        sop(lambda e, b=bank, t=tt, bc=bcol: e.activation(
                            wbf[0][:, 512 * t:512 * (t + 1)], b[:, :], AF.Identity,
                            bias=bc))
                        kb.wait("v", cS, cS.n)
                        vop(lambda e, t=tt: e.tensor_mul(
                            wbf[1][:, 512 * t:512 * (t + 1)],
                            wbf[0][:, 512 * t:512 * (t + 1)],
                            ropec_sb[:, 512 * t:512 * (t + 1)]))
                        vop(lambda e, t=tt: e.tensor_mul(
                            wbf[2][:, 512 * t:512 * (t + 1)],
                            wbf[0][:, 512 * t:512 * (t + 1)],
                            ropes_sb[:, 512 * t:512 * (t + 1)]))
                        kb.wait("t", cV, cV.n)
                        mm(pb[7][:, :], p128_sb[:, :],
                           wbf[2][:, 512 * tt:512 * (tt + 1)], True, True,
                           tick=True)
                        kb.wait("v", cT, cT.n)
                        vop(lambda e, t=tt, o=stg: e.tensor_add(
                            o[:, 512 * t:512 * (t + 1)],
                            wbf[1][:, 512 * t:512 * (t + 1)], pb[7][:, :]))
                        bank_free[tt] = (cS, cS.n)
                    else:
                        kb.wait("s", cT, mmn)
                        if layer == 0:
                            bcol = bv_sb[:, h:h + 1]
                            sop(lambda e, b=bank, o=stg, t=tt, bc=bcol:
                                e.activation(o[:, 512 * t:512 * (t + 1)], b[:, :],
                                             AF.Identity, bias=bc))
                        else:
                            sop(lambda e, b=bank, o=stg, t=tt: e.copy(
                                o[:, 512 * t:512 * (t + 1)], b[:, :]))
                        bank_free[tt] = (cS, cS.n)
                cons[i] = cT.n
                if layer == 0 and kind == "k":
                    kb.wait("y", cV, cV.n)
                else:
                    kb.wait("y", cS, cS.n)
                if kind == "k":
                    qtr = h // 4
                    dst = ccT[(layer, "ki", qtr)]
                    ydma(cSg[i % 2], dst[128 * (h % 4):128 * (h % 4) + 128, :],
                         stg[:, :])
                    dmad[i] = cSg[i % 2].n
                    if h % 4 == 3 and do_cc:
                        kb.wait("g", cSg[0])
                        kb.wait("g", cSg[1])
                        kb.tick("g", cCC, lambda e, ci=dst,
                                co=ccT[(layer, "ko", qtr)]: e.collective_compute(
                                    "AllGather", ALU.bypass, replica_groups=grp,
                                    ins=[ci], outs=[co]))
                        agk[qtr] = cCC.n
                else:
                    for vh in range(4):
                        ydma(cSg[i % 2],
                             ccT[(layer, "vi", vh)][128 * h:128 * (h + 1), :],
                             stg[:, 256 * vh:256 * (vh + 1)])
                    dmad[i] = cSg[i % 2].n
            agv = [0, 0, 0, 0]
            for vh in range(4):
                kb.wait("g", cSg[0])
                kb.wait("g", cSg[1])
                if do_cc:
                    kb.tick("g", cCC, lambda e, ci=ccT[(layer, "vi", vh)],
                            co=ccT[(layer, "vo", vh)]: e.collective_compute(
                                "AllGather", ALU.bypass, replica_groups=grp,
                                ins=[ci], outs=[co]))
                agv[vh] = cCC.n
            ydma(cM, m_sb[:, :], (m0t if layer == 0 else m1t)[:, :])
            vtn = [0, 0, 0, 0]
            if do_vt:
                for vh in range(4):
                    kb.wait("y", cCC, agv[vh])
                    for r in range(2):
                        for m2 in range(2 * vh, 2 * vh + 2):
                            a = r * 8 + m2
                            src = ccT[(layer, "vo", vh)][E * r: E * (r + 1),
                                                         128 * (m2 - 2 * vh):
                                                         128 * (m2 - 2 * vh) + 128]
                            ydma(cVT, arenaA[:, 2048 * a:2048 * (a + 1)], src,
                                 transpose=True)
                    vtn[vh] = cVT.n
            return layer, agk, vtn, cT.n

        def attn_phase(layer, agk, vtn, kvdone_t, side=None):
            wqw = wq0 if layer == 0 else wq1
            msk = m_sb
            AO = xB
            Pb = arenaB
            scale = SC_L0 if layer == 0 else SC_L1
            fillval = 0.0 if layer == 0 else 1.0
            kload = {}
            headdone = {}
            kb.wait("t", cM)
            vt_waited = [False] * 4
            agk_waited = [False] * 4
            for h in range(H):
                kb.wait("g", cT, kvdone_t if h < WR else headdone[h - WR])
                gdma(cW[h % WR], wb[h % WR][:, :], wqw[h][:, :])
                qldn = cW[h % WR].n
                if h >= 2:
                    kb.wait("y", cT, headdone[h - 2])
                if not agk_waited[h // 4]:
                    kb.wait("y", cCC, agk[h // 4])
                    agk_waited[h // 4] = True
                ko = ccT[(layer, "ko", h // 4)]
                for r in range(2):
                    ydma(cK[h % 2], kbuf[h % 2][:, 1024 * r:1024 * (r + 1)],
                         ko[512 * r + 128 * (h % 4): 512 * r + 128 * (h % 4) + 128, :])
                kload[h] = cK[h % 2].n
                for tt in range(2):
                    kb.wait("t", cW[h % WR], qldn)
                    kb.wait("t", cS, cS.n)
                    kb.wait("t", cV, cV.n)
                    proj_mms(pb[6], wb[h % WR], tt)
                    mmn = cT.n
                    kb.wait("s", cT, mmn)
                    if layer == 0:
                        kb.wait("s", cV, cV.n)
                        sop(lambda e, t=tt, h_=h: e.activation(
                            wbf[0][:, 512 * t:512 * (t + 1)], pb[6][:, :],
                            AF.Identity, bias=bq_sb[:, h_:h_ + 1]), tick=True)
                        kb.wait("v", cS, cS.n)
                        vop(lambda e, t=tt: e.tensor_mul(
                            wbf[1][:, 512 * t:512 * (t + 1)],
                            wbf[0][:, 512 * t:512 * (t + 1)],
                            ropec_sb[:, 512 * t:512 * (t + 1)]))
                        vop(lambda e, t=tt: e.tensor_mul(
                            wbf[2][:, 512 * t:512 * (t + 1)],
                            wbf[0][:, 512 * t:512 * (t + 1)],
                            ropes_sb[:, 512 * t:512 * (t + 1)]), tick=True)
                        kb.wait("t", cV, cV.n)
                        mm(pb[7][:, :], p128_sb[:, :],
                           wbf[2][:, 512 * tt:512 * (tt + 1)], True, True,
                           tick=True)
                        kb.wait("v", cT, cT.n)
                        vop(lambda e, t=tt: e.tensor_add(
                            qh[:, 512 * t:512 * (t + 1)],
                            wbf[1][:, 512 * t:512 * (t + 1)], pb[7][:, :]),
                            tick=True)
                    else:
                        sop(lambda e, t=tt: e.copy(
                            qh[:, 512 * t:512 * (t + 1)], pb[6][:, :]), tick=True)
                kb.wait("t", cV, cV.n)
                kb.wait("t", cS, cS.n)
                kb.wait("t", cK[h % 2], kload[h])
                exp_tick = {}
                for j in range(2):
                    kb.wait("v", cT, cT.n)
                    fk = [(k, c0_of(j, k)) for k in range(0, 8 * j + 8)
                          if c0_of(j, k) > 0]
                    for fi, (k, c0) in enumerate(fk):
                        vop(lambda e, k_=k, c_=c0: e.memset(
                            Pb[:, 512 * k_:512 * k_ + c_], fillval),
                            tick=(fi == len(fk) - 1))
                    kb.wait("s", cV, cV.n)
                    kb.wait("t", cV, cV.n)
                    for k in range(0, 8 * j + 8):
                        c0 = c0_of(j, k)
                        bank = pb[k % 2]
                        if k >= 2:
                            kb.wait("t", cS, exp_tick[k - 2])
                        has_mask = (layer == 0 and k >= 8 * j)
                        mm(bank[:, c0:512],
                           kbuf[h % 2][:, kcol(k):kcol(k) + 128],
                           qh[:, 512 * j + c0:512 * (j + 1)],
                           start=True, stop=not has_mask, tick=not has_mask)
                        if has_mask:
                            mm(bank[:, c0:c0 + 128], identb_sb[:, :],
                               msk[:, 128 * k:128 * (k + 1)], start=False,
                               stop=True, tick=True)
                        smn = cT.n
                        if layer == 0 or k < 8 * j:
                            kb.wait("s", cT, smn)
                            sop(lambda e, k_=k, c_=c0, b=bank: e.activation(
                                Pb[:, 512 * k_ + c_:512 * (k_ + 1)], b[:, c_:512],
                                AF.Exp, scale=scale), tick=True)
                        else:
                            kb.wait("v", cT, smn)
                            kb.wait("v", cS, cS.n)
                            vop(lambda e, k_=k, c_=c0, b=bank: e.tensor_mul(
                                wk[5][:, 0:128], b[:, c_:c_ + 128],
                                msk[:, 128 * k_:128 * (k_ + 1)]), tick=True)
                            kb.wait("s", cV, cV.n)
                            sop(lambda e, k_=k, c_=c0: e.activation(
                                Pb[:, 512 * k_ + c_:512 * k_ + c_ + 128],
                                wk[5][:, 0:128], AF.Exp, scale=scale),
                                tick=(c0 + 128 >= 512))
                            if c0 + 128 < 512:
                                sop(lambda e, k_=k, c_=c0, b=bank: e.activation(
                                    Pb[:, 512 * k_ + c_ + 128:512 * (k_ + 1)],
                                    b[:, c_ + 128:512], AF.Exp, scale=scale),
                                    tick=True)
                        exp_tick[k] = cS.n
                    kb.wait("t", cS, cS.n)
                    vtg = 3 if layer == 1 else (1 if j == 0 else 3)
                    if not vt_waited[vtg]:
                        kb.wait("t", cVT, vtn[vtg])
                        vt_waited[vtg] = True
                    obank, zbank = pb[2 + j], pb[4 + j]
                    krange = list(range(0, 8 * j + 8)) if layer == 0 else (
                        list(range(16)) if j == 0 else list(range(16)))
                    zrange = set(range(0, 8 * j + 8))
                    zlast = max(k for k in krange if k in zrange)
                    first = True
                    for k in krange:
                        inwin = k in zrange
                        last = (k == krange[-1])
                        rhs = Pb[:, 512 * k:512 * (k + 1)] if inwin else allones[:, :]
                        vt = arenaA[:, 2048 * vtslot(k) + 128 * h:
                                    2048 * vtslot(k) + 128 * (h + 1)]
                        mm(obank[:, :], vt, rhs, start=first, stop=last,
                           tick=(last and not (inwin and True)))
                        if inwin:
                            mm(zbank[:, :], ones128_sb[:, :], rhs,
                               start=first, stop=(k == zlast), tick=(k == zlast))
                        if layer == 1 and first and j == 0:
                            mm(zbank[:, :], c1024_sb[:, :], allones[0:1, 0:512],
                               start=False, stop=False)
                        first = False
                    pvn = cT.n
                    kb.wait("s", cT, pvn)
                    sop(lambda e, z=zbank: e.activation(wk[0][:, :], z[:, :], AF.Ln),
                        tick=True)
                    kb.wait("s", cS, cS.n)
                    sop(lambda e: e.activation(wk[1][:, :], wk[0][:, :], AF.Exp,
                                               scale=-1.0), tick=True)
                    kb.wait("v", cS, cS.n)
                    vop(lambda e, o=obank, h_=h, j_=j: e.tensor_mul(
                        AO[:, 1024 * h_ + 512 * j_:1024 * h_ + 512 * (j_ + 1)],
                        o[:, :], wk[1][:, :]), tick=True)
                headdone[h] = cT.n
                if side is not None and h < len(side):
                    for fn in side[h]:
                        fn()
            kb.wait("t", cV, cV.n)
            return AO

        def wo_phase(layer, AO):
            wow = wo0 if layer == 0 else wo1
            cons = {}
            entry_t = cT.n
            evac = {0: (cV, cV.n), 1: (cV, cV.n)}
            for fh in range(NCH):
                kb.wait("g", cT, cons[fh - WR] if fh >= WR else entry_t)
                gdma(cW[fh % WR], wb2[fh % WR][:, :], wow[fh][:, :])
                ldn = cW[fh % WR].n
                for tt in range(2):
                    bank = pb[tt]
                    kb.wait("t", cW[fh % WR], ldn)
                    ch_, n_ = evac[tt]
                    kb.wait("t", ch_, n_)
                    for ec in range(NCH):
                        mm(bank[:, :], wb2[fh % WR][:, 128 * ec:128 * (ec + 1)],
                           AO[:, 1024 * ec + 512 * tt:1024 * ec + 512 * (tt + 1)],
                           start=(ec == 0), stop=(ec == NCH - 1),
                           tick=(ec == NCH - 1))
                    kb.wait("v", cT, cT.n)
                    if layer == 0:
                        vop(lambda e, b=bank, f=fh, t=tt: e.scalar_tensor_tensor(
                            rr(f, t), b[:, :], bo_sb[:, f:f + 1], cur(f, t),
                            ALU.add, ALU.add), tick=True)
                    else:
                        vop(lambda e, b=bank, f=fh, t=tt: e.tensor_add(
                            rr(f, t), b[:, :], cur(f, t)), tick=True)
                    evac[tt] = (cV, cV.n)
                cons[fh] = cT.n

        def ln_phase(dst, src=None, desc=False, mrow=None):
            if src is None:
                src = rr
            kb.wait("t", cV, cV.n)
            kb.wait("s", cV, cV.n)
            prev_tick = cT.n
            for c in range(NCH):
                for tt in range(2):
                    kb.wait("s", cT, prev_tick)
                    sop(lambda e, c_=c, t=tt: e.activation(
                        wbf[0][:, 512 * t:512 * (t + 1)], src(c_, t), AF.Square))
                    sqn = cS.n
                    mm(pb[tt][:, :], ones128_sb[:, :], src(c, tt),
                       start=(c == 0), stop=(c == NCH - 1))
                    kb.wait("t", cS, sqn)
                    mm(pb[2 + tt][:, :], ones128_sb[:, :],
                       wbf[0][:, 512 * tt:512 * (tt + 1)],
                       start=(c == 0), stop=(c == NCH - 1), tick=True)
                    prev_tick = cT.n
            stn = cT.n
            for tt in range(2):
                nr = wk[3] if tt == 0 else wk[5]
                kb.wait("v", cT, stn)
                vop(lambda e, t=tt: e.tensor_scalar(
                    wk[0][:, :], pb[2 + t][:, :], RDIV, None, ALU.mult))
                vop(lambda e, t=tt: e.tensor_scalar(
                    wk[2][:, :], pb[t][:, :], RDIV, None, ALU.mult))
                kb.wait("s", cV, cV.n)
                sop(lambda e: e.activation(wk[1][:, :], wk[2][:, :], AF.Square))
                kb.wait("v", cS, cS.n)
                vop(lambda e: e.tensor_sub(wk[0][:, :], wk[0][:, :], wk[1][:, :]))
                kb.wait("s", cV, cV.n)
                sop(lambda e: e.activation(wk[1][:, :], wk[0][:, :], AF.Ln,
                                           bias=epsT[:, :]))
                kb.wait("s", cS, cS.n)
                sop(lambda e: e.activation(wk[1][:, :], wk[1][:, :], AF.Exp,
                                           scale=-0.5))
                kb.wait("v", cS, cS.n)
                vop(lambda e, o=nr: e.tensor_scalar(
                    o[:, :], wk[1][:, :], -1.0, None, ALU.mult))
                if mrow is not None:
                    vop(lambda e, t=tt: e.tensor_scalar(
                        mrow(t), pb[t][:, :], RDIV, None, ALU.mult))
            bankfree = cV.n
            corder = list(range(NCH))
            ttorder = [0, 1]
            if desc:
                corder = corder[::-1]
                ttorder = [1, 0]
            lnt = {}
            for c in corder:
                for tt in ttorder:
                    nr = wk[3] if tt == 0 else wk[5]
                    kb.wait("v", cV, cV.n)
                    if mrow is not None:
                        vop(lambda e, c_=c, t=tt: e.scalar_tensor_tensor(
                            wk[1][:, :], mrow(t), 1.0, src(c_, t),
                            ALU.mult, ALU.subtract))
                    else:
                        vop(lambda e, c_=c, t=tt: e.scalar_tensor_tensor(
                            wk[1][:, :], pb[t][:, :], RDIV, src(c_, t),
                            ALU.mult, ALU.subtract))
                    kb.wait("v", cV, cV.n)
                    vop(lambda e, c_=c, t=tt, o=nr: e.tensor_mul(
                        dst(c_, t), wk[1][:, :], o[:, :]))
                    lnt[(c, tt)] = cV.n
            return lnt, bankfree


        def moe_pre(lnt=None):
            kb.wait("v", cA, cA.n)
            lgcopy = [0, 0]
            for tt in range(2):
                if tt == 1:
                    kb.wait("t", cS, cS.n)
                for ec in range(NCH):
                    if lnt is not None:
                        kb.wait("t", cV, lnt[(ec, tt)])
                    mm(pb[6][0:8, :], rw_sb[:, 8 * ec:8 * (ec + 1)], cur(ec, tt),
                       start=(ec == 0), stop=(ec == NCH - 1), tick=(ec == NCH - 1))
                kb.wait("s", cT, cT.n)
                sop(lambda e, t=tt: e.copy(arenaB_f[0:8, 512 * t:512 * (t + 1)],
                                           pb[6][0:8, :]))
                lgcopy[tt] = cS.n
            trn = [0] * 8
            for lb in range(8):
                tt, co = (0, 128 * lb) if lb < 4 else (1, 128 * (lb - 4))
                if lb in (0, 4):
                    kb.wait("t", cS, lgcopy[tt])
                mm(pb[7][:, 8 * lb:8 * lb + 8],
                   arenaB_f[0:8, 512 * tt + co:512 * tt + co + 128],
                   identf_sb[0:8, 0:8], start=True, stop=True, tick=True)
                trn[lb] = cT.n
            for lb in range(8):
                kb.wait("v", cT, trn[lb])
                vop(lambda e, l=lb: e.tensor_copy(mt8[0][:, :],
                                                  pb[7][:, 8 * l:8 * l + 8]))
                vop(lambda e: e.reduce_max(mt8[1][:, 0:1], mt8[0][:, :], AX.X,
                                           negate=True))
                kb.wait("s", cV, cV.n)
                sop(lambda e: e.activation(mt8[2][:, :], mt8[0][:, :], AF.Exp,
                                           bias=mt8[1][:, 0:1],
                                           accum_out=mt8[1][:, 1:2]))
                kb.wait("v", cS, cS.n)
                vop(lambda e: e.reciprocal(mt8[1][:, 2:3], mt8[1][:, 1:2]))
                vop(lambda e: e.tensor_scalar(mt8[2][:, :], mt8[2][:, :],
                                              mt8[1][:, 2:3], None, ALU.mult))

                def top_one(m_dst, oh_dst_bf):
                    vop(lambda e, m=m_dst: e.reduce_max(m, mt8[2][:, :], AX.X))
                    vop(lambda e, m=m_dst: e.tensor_scalar(
                        mt8[0][:, :], mt8[2][:, :], m, None, ALU.is_equal))
                    vop(lambda e: e.tensor_copy(mt8[3][:, :], mt8[0][:, :]))
                    vop(lambda e: e.tensor_add(mt8[3][:, 1:8], mt8[0][:, 1:8],
                                               mt8[0][:, 0:7]))
                    vop(lambda e: e.tensor_copy(wk[0][:, 0:8], mt8[3][:, :]))
                    vop(lambda e: e.tensor_add(wk[0][:, 2:8], mt8[3][:, 2:8],
                                               mt8[3][:, 0:6]))
                    vop(lambda e: e.tensor_copy(mt8[3][:, :], wk[0][:, 0:8]))
                    vop(lambda e: e.tensor_add(mt8[3][:, 4:8], wk[0][:, 4:8],
                                               wk[0][:, 0:4]))
                    vop(lambda e: e.tensor_scalar(mt8[3][:, :], mt8[3][:, :], 1.0,
                                                  None, ALU.is_equal))
                    vop(lambda e: e.tensor_mul(mt8[0][:, :], mt8[0][:, :],
                                               mt8[3][:, :]))
                    vop(lambda e, o=oh_dst_bf: e.tensor_copy(o, mt8[0][:, :]))

                top_one(moe_m1[:, lb:lb + 1], moe_oh1[:, 8 * lb:8 * lb + 8])
                vop(lambda e: e.tensor_scalar(mt8[3][:, :], mt8[0][:, :], -1.0,
                                              1.0, ALU.mult, ALU.add))
                vop(lambda e: e.tensor_mul(mt8[2][:, :], mt8[2][:, :],
                                           mt8[3][:, :]))
                top_one(moe_m2[:, lb:lb + 1], moe_oh2[:, 8 * lb:8 * lb + 8])

        def moe_counts():
            kb.wait("t", cV, cV.n)
            first = True
            for choice in range(2):
                ohs = moe_oh1 if choice == 0 else moe_oh2
                for lb in range(8):
                    co = 8 * (choice * 8 + lb)
                    mm(pb[7][0:1, co:co + 8], ones128_sb[:, 0:1],
                       ohs[:, 8 * lb:8 * lb + 8], start=first, stop=False,
                       tick=(choice == 1 and lb == 7))
                    first = False
            kb.wait("s", cT, cT.n)
            sop(lambda e: e.copy(cntrow[:, :], pb[7][0:1, 0:128]))
            kb.wait("g", cS, cS.n)
            gdma(cA, cnt_in[:, :], cntrow[:, :])
            kb.wait("g", cA)
            kb.tick("g", cCC, lambda e: e.collective_compute(
                "AllGather", ALU.bypass,
                replica_groups=[[0, 1, 2, 3, 4, 5, 6, 7]],
                ins=[cnt_in], outs=[cnt_out]))
            kb.wait("g", cCC, cCC.n)
            for c_ in range(2):
                srcap = cnt_out[:, 64 * c_:64 * (c_ + 1)].rearrange(
                    "core (blk e) -> core blk e", blk=8)
                gdma(cA, cnt_sb1[64 * c_:64 * (c_ + 1), :], srcap,
                     allow_slow_non_contiguous=True)

        def moe_gates():
            kb.wait("v", cA, cA.n)
            kb.wait("t", cA)
            kb.wait("t", cS, cS.n)
            mm(pb[7][0:16, 0:8], mbig_sb[:, :], cnt_sb1[:, :], start=True,
               stop=True, tick=True)
            kb.wait("s", cT, cT.n)
            sop(lambda e: e.copy(moe_s1[0:16, 0:8], pb[7][0:16, 0:8]))
            kb.wait("g", cS, cS.n)
            gdma(cA, offrow[0:1, 0:128], moe_s1[0:16, 0:8],
                 allow_slow_non_contiguous=True)
            kb.wait("t", cA)
            for lb in range(8):
                for choice in range(2):
                    ohs = moe_oh1 if choice == 0 else moe_oh2
                    ms = moe_m1 if choice == 0 else moe_m2
                    r = choice * 8 + lb
                    kb.wait("t", cV, cV.n)
                    mm(pb[6][:, 0:8], u128_sb[:, :], ohs[:, 8 * lb:8 * lb + 8],
                       start=True, stop=False)
                    mm(pb[6][:, 0:8], onesrow_f_sb[:, :],
                       offrow[0:1, 8 * r:8 * r + 8], start=False, stop=True,
                       tick=True)
                    kb.wait("v", cT, cT.n)
                    vop(lambda e, o=ohs, l=lb: e.tensor_tensor(
                        mt8[0][:, :], pb[6][:, 0:8], o[:, 8 * l:8 * l + 8],
                        ALU.mult))
                    vop(lambda e: e.reduce_sum(mt8[1][:, 0:1], mt8[0][:, :], AX.X))
                    vop(lambda e: e.tensor_scalar(mt8[1][:, 1:2], mt8[1][:, 0:1],
                                                  512.0, None, ALU.is_le))
                    vop(lambda e, m=ms, l=lb, ch_=choice: e.tensor_tensor(
                        mt8[2][:, ch_:ch_ + 1], mt8[1][:, 1:2], m[:, l:l + 1],
                        ALU.mult))
                vop(lambda e, l=lb: e.tensor_add(moe_g[:, l:l + 1],
                                                 mt8[2][:, 0:1], mt8[2][:, 1:2]))
            kb.wait("t", cV, cV.n)
            mm(pb[7][0:8, 0:128], moe_g[:, :], identf_sb[:, :],
               start=True, stop=True, tick=True)
            kb.wait("s", cT, cT.n)
            sop(lambda e: e.copy(moe_s1[0:8, :], pb[7][0:8, 0:128]))
            kb.wait("g", cS, cS.n)
            gdma(cA, grow[0:1, 0:1024], moe_s1[0:8, :],
                 allow_slow_non_contiguous=True)
            kb.wait("t", cA)
            for tt in range(2):
                kb.wait("t", cS, cS.n)
                mm(pb[7][:, :], onesrow_f_sb[:, :],
                   grow[0:1, 512 * tt:512 * (tt + 1)], start=True, stop=True,
                   tick=True)
                kb.wait("s", cT, cT.n)
                sop(lambda e, t=tt: e.copy(gb_sb[:, 512 * t:512 * (t + 1)],
                                           pb[7][:, :]))

        def ffn_phase(lnt=None, bankfree=None, sides=None):
            wcons = {}
            lni = 0
            for tt in range(2):
                bf0 = bankfree if (tt == 0 and bankfree is not None) else cV.n
                gel = {0: (cV, bf0), 1: (cV, bf0)}
                for f in range(64):
                    if tt == 0 and sides and f in sides:
                        for fn in sides[f]:
                            fn()
                    li = lni + f
                    kb.wait("g", cT, wcons[li - WR] if li >= WR else cT.n)
                    gdma(cW[li % WR], wb[li % WR][:, :], w1[f][:, :])
                    ldn = cW[li % WR].n
                    bank = pb[f % 2]
                    kb.wait("t", cW[li % WR], ldn)
                    ch_, n_ = gel[f % 2]
                    kb.wait("t", ch_, n_)
                    if lnt is not None and tt == 1 and f == 0:
                        kb.wait("t", cV, lnt[(NCH - 1, 1)])
                    for ec in range(NCH):
                        if lnt is not None and tt == 0 and f == 0:
                            kb.wait("t", cV, lnt[(ec, 0)])
                        mm(bank[:, :], wb[li % WR][:, 128 * ec:128 * (ec + 1)],
                           cur(ec, tt), start=(ec == 0), stop=(ec == NCH - 1),
                           tick=(ec == NCH - 1))
                    wcons[li] = cT.n
                    kb.wait("s", cT, cT.n)
                    sop(lambda e, b=bank, f_=f: e.activation(
                        arenaA[:, 512 * f_:512 * (f_ + 1)], b[:, :], AF.Gelu,
                        bias=b1_sb[:, f_:f_ + 1]))
                    gel[f % 2] = (cS, cS.n)
                lni += 64
                if tt == 0:
                    moe_gates()
                kb.wait("t", cS, cS.n)
                ev2 = {0: (cV, cV.n), 1: (cV, cV.n)}
                for oh in range(NCH):
                    for qq in range(4):
                        li = lni + oh * 4 + qq
                        kb.wait("g", cT, wcons[li - WR] if li - WR in wcons else cT.n)
                        gdma(cW[li % WR], wb[li % WR][:, :],
                             w2[oh][:, 2048 * qq:2048 * (qq + 1)])
                        ldn = cW[li % WR].n
                        bank = pb[2 + oh % 2]
                        kb.wait("t", cW[li % WR], ldn)
                        if qq == 0:
                            ch_, n_ = ev2[oh % 2]
                            kb.wait("t", ch_, n_)
                        for fi in range(16):
                            fc = 16 * qq + fi
                            mm(bank[:, :], wb[li % WR][:, 128 * fi:128 * (fi + 1)],
                               arenaA[:, 512 * fc:512 * (fc + 1)],
                               start=(fc == 0), stop=(fc == 63),
                               tick=(fi == 15))
                        wcons[li] = cT.n
                    kb.wait("v", cT, cT.n)
                    vop(lambda e, b=pb[2 + oh % 2], o=oh, t=tt:
                        e.scalar_tensor_tensor(
                            wk[4][:, :], b[:, :], b2_sb[:, o:o + 1],
                            gb_sb[:, 512 * t:512 * (t + 1)], ALU.add, ALU.mult))
                    kb.wait("v", cV, cV.n)
                    vop(lambda e, o=oh, t=tt: e.tensor_add(
                        aox(o, t), wk[4][:, :], cur(o, t)))
                    ev2[oh % 2] = (cV, cV.n)
                lni += 64

        def emit_out_stream(srcfn):
            idx = 0
            for c in range(NCH):
                for tt in range(2):
                    buf = wk[idx % 2]
                    if idx >= 2:
                        kb.wait("s", cO, cO.n)
                    kb.wait("s", cV, cV.n)
                    sop(lambda e, c_=c, t=tt, b=buf: e.copy(b[:, :], srcfn(c_, t)))
                    kb.wait("y", cS, cS.n)
                    ydma(cO, y[128 * c:128 * (c + 1), 512 * tt:512 * (tt + 1)],
                         buf[:, :])
                    idx += 1
            kb.wait("y", cO, cO.n)

        def quant_chain(src, src_i, seg448, segS, out_bf, slots=None):
            if slots is None:
                Af, Bf, Cf = (arenaB_f[:, 1024:2048], arenaB_f[:, 2048:3072],
                              arenaB_f[:, 3072:4096])
                Ai, Bi, Ci = (arenaB_i[:, 1024:2048], arenaB_i[:, 2048:3072],
                              arenaB_i[:, 3072:4096])
            else:
                (Af, Ai), (Bf, Bi), (Cf, Ci) = slots
            nseg = src.shape[-1] // 128
            hseg = nseg // 2
            hw = 128 * hseg
            H2 = (slice(0, hw), slice(hw, 2 * hw))
            v1n = [0, 0]; s1n = [0, 0]; v2n = [0, 0]
            s2n = [0, 0]; v3n = [0, 0]; s3n = [0, 0]; v4n = [0, 0]
            for h in (0, 1):
                hs = H2[h]
                vop(lambda e, q=hs: e.tensor_scalar(Bi[:, q], src_i[:, q],
                                                    0x7FFFFFFF, None,
                                                    ALU.bitwise_and))
                vop(lambda e, q=hs: e.tensor_scalar(Af[:, q], Bf[:, q], 1e-30,
                                                    None, ALU.max))
                v1n[h] = cV.n
            for h in (0, 1):
                kb.wait("s", cV, v1n[h])
                for sg in range(h * hseg, (h + 1) * hseg):
                    sop(lambda e, g=sg: e.activation(
                        Bf[:, 128 * g:128 * (g + 1)],
                        Af[:, 128 * g:128 * (g + 1)], AF.Ln, scale=seg448[g]))
                s1n[h] = cS.n
            for h in (0, 1):
                hs = H2[h]
                kb.wait("v", cS, s1n[h])
                vop(lambda e, q=hs: e.tensor_scalar(Af[:, q], Bf[:, q], INV_LN2,
                                                    -0.5, ALU.mult, ALU.add))
                vop(lambda e, q=hs: e.tensor_scalar(Ci[:, q], Af[:, q], -6.5,
                                                    None, ALU.max))
                v2n[h] = cV.n
            for h in (0, 1):
                hs = H2[h]
                kb.wait("s", cV, v2n[h])
                sop(lambda e, q=hs: e.copy(Bf[:, q], Ci[:, q]))
                sop(lambda e, q=hs: e.activation(Cf[:, q], Bf[:, q], AF.Exp,
                                                 scale=-LN2,
                                                 bias=b3l2T[:, :]))
                s2n[h] = cS.n
            for h in (0, 1):
                hs = H2[h]
                for sg in range(h * hseg, (h + 1) * hseg):
                    vop(lambda e, g=sg: e.tensor_scalar(
                        Af[:, 128 * g:128 * (g + 1)],
                        src[:, 128 * g:128 * (g + 1)], seg448[g], None,
                        ALU.mult))
                kb.wait("v", cS, s2n[h])
                vop(lambda e, q=hs: e.tensor_tensor(src_i[:, q], Af[:, q],
                                                    Cf[:, q], ALU.mult))
                v3n[h] = cV.n
            for h in (0, 1):
                hs = H2[h]
                kb.wait("s", cV, v3n[h])
                sop(lambda e, q=hs: e.activation(Cf[:, q], Bf[:, q], AF.Exp,
                                                 scale=LN2,
                                                 bias=bm3l2T[:, :]))
                sop(lambda e, q=hs: e.copy(Af[:, q], src_i[:, q]))
                s3n[h] = cS.n
            for h in (0, 1):
                hs = H2[h]
                kb.wait("v", cS, s3n[h])
                vop(lambda e, q=hs: e.tensor_tensor(Bf[:, q], Af[:, q],
                                                    Cf[:, q], ALU.mult))
                for sg in range(h * hseg, (h + 1) * hseg):
                    vop(lambda e, g=sg: e.tensor_scalar(
                        out_bf[:, 128 * g:128 * (g + 1)],
                        Bf[:, 128 * g:128 * (g + 1)], segS[g], None, ALU.mult))
                v4n[h] = cV.n
            return {"r32": v3n[1], "out": v4n}

        _wbf2f = wbf[2].bitcast(F32)
        _wk2i = wk[2].bitcast(I32)
        _wk3i = wk[3].bitcast(I32)
        wq_slots = ((_wbf2f[:, 0:512], wbf[2].bitcast(I32)[:, 0:512]),
                    (wk[2][:, :], _wk2i[:, :]),
                    (wk[3][:, :], _wk3i[:, :]))

        def wquant_passA(oh):
            for hf in range(2):
                kb.wait("g", cV, cV.n)
                gdma(cA, wfb[:, 0:1024],
                     wf1[oh][:, 1024 * hf:1024 * (hf + 1)])
                kb.wait("v", cA)
                for sg in range(8):
                    vop(lambda e, g=sg, h_=hf: e.reduce_max(
                        wq_cm[:, 8 * h_ + g:8 * h_ + g + 1],
                        wfb[:, 128 * g:128 * (g + 1)], AX.X,
                        apply_absolute_value=True))
            kb.wait("t", cV, cV.n)
            kb.wait("t", cS, cS.n)
            mm(pb[7][0:16, 0:128], wq_cm[:, :], identf_sb[:, :],
               start=True, stop=True, tick=True)
            kb.wait("v", cT, cT.n)
            vop(lambda e: e.tensor_copy(moe_s1[0:16, :], pb[7][0:16, 0:128]))
            vop(lambda e: e.reduce_max(mt8[0][0:16, 0:1], moe_s1[0:16, :],
                                       AX.X))
            vop(lambda e: e.tensor_scalar(mt8[0][0:16, 1:2],
                                          mt8[0][0:16, 0:1], 1e-4, None,
                                          ALU.max))
            kb.wait("g", cV, cV.n)
            gdma(cA, wq_rows[0:1, 16 * oh:16 * (oh + 1)],
                 mt8[0][0:16, 1:2], allow_slow_non_contiguous=True)

        def wquant_prep():
            kb.wait("v", cA)
            vop(lambda e: e.reciprocal(wq_rows[0:1, 256:288], wq_rows[0:1, 0:32]))
            vop(lambda e: e.tensor_scalar(wq_rows[0:1, 256:288], wq_rows[0:1, 256:288], 448.0,
                                          None, ALU.mult))
            vop(lambda e: e.tensor_scalar(wq_rows[0:1, 0:32], wq_rows[0:1, 0:32],
                                          float(1.0 / 448.0), None, ALU.mult))
            kb.wait("t", cV, cV.n)
            mm(pb[7][:, 0:32], onesrow_f_sb[:, :], wq_rows[0:1, 256:288], start=True,
               stop=True, tick=True)
            kb.wait("s", cT, cT.n)
            sop(lambda e: e.copy(wq_rb448[:, 0:32], pb[7][:, 0:32]))
            kb.wait("t", cS, cS.n)
            mm(pb[7][:, 0:32], onesrow_f_sb[:, :], wq_rows[0:1, 0:32], start=True,
               stop=True, tick=True)
            kb.wait("s", cT, cT.n)
            sop(lambda e: e.copy(wq_s[:, 0:32], pb[7][:, 0:32]))
            kb.wait("v", cS, cS.n)

        def wquant_passB(oh, hfs=(0, 1)):
            for hf in hfs:
                kb.wait("g", cV, cV.n)
                gdma(cA, wfb[:, 0:1024],
                     wf1[oh][:, 1024 * hf:1024 * (hf + 1)])
                kb.wait("v", cA)
                out_bf = wbf[3 + hf]
                kb.wait("v", cO, cO.n)
                for q2 in range(2):
                    seg448 = [wq_rb448[:, 16 * oh + 8 * hf + 4 * q2 + g:
                                       16 * oh + 8 * hf + 4 * q2 + g + 1]
                              for g in range(4)]
                    segS = [wq_s[:, 16 * oh + 8 * hf + 4 * q2 + g:
                                 16 * oh + 8 * hf + 4 * q2 + g + 1]
                            for g in range(4)]
                    quant_chain(wfb[:, 512 * q2:512 * (q2 + 1)],
                                wfb_i[:, 512 * q2:512 * (q2 + 1)], seg448, segS,
                                out_bf[:, 512 * q2:512 * (q2 + 1)],
                                slots=wq_slots)
                kb.wait("y", cV, cV.n)
                ydma(cO, wdq_loc[128 * oh:128 * (oh + 1),
                                 1024 * hf:1024 * (hf + 1)], out_bf[:, :])

        xq_bk = [0, 0]
        xq_srcfree = [0]

        def xquant_block(tb, hf, lnw=None):
            tt, co = (tb // 4, 128 * (tb % 4))
            kb.wait("s", cV, xq_srcfree[0])
            for g in range(8):
                c = 8 * hf + g
                bk = pb[5 + g % 2]
                if lnw is not None:
                    kb.wait("t", cV, lnw[(c, tt)])
                kb.wait("t", cS, xq_bk[g % 2])
                mm(bk[:, 0:128],
                   arenaA_f[:, 1024 * c + 512 * tt + co:
                            1024 * c + 512 * tt + co + 128],
                   identf_sb[:, :], start=True, stop=True, tick=True)
                kb.wait("s", cT, cT.n)
                sop(lambda e, g_=g, b=bk: e.copy(
                    arenaB_f[:, 128 * g_:128 * (g_ + 1)], b[:, 0:128]))
                xq_bk[g % 2] = cS.n
            kb.wait("v", cS, cS.n)
            for g in range(8):
                vop(lambda e, g_=g: e.reduce_max(
                    q_am[:, g_:g_ + 1], arenaB_f[:, 128 * g_:128 * (g_ + 1)],
                    AX.X, apply_absolute_value=True))
            vop(lambda e: e.tensor_scalar(q_am[:, 0:8], q_am[:, 0:8], 1e-4,
                                          None, ALU.max))
            vop(lambda e: e.reciprocal(q_rc448[:, 0:8], q_am[:, 0:8]))
            vop(lambda e: e.tensor_scalar(q_rc448[:, 0:8], q_rc448[:, 0:8],
                                          448.0, None, ALU.mult))
            vop(lambda e: e.tensor_scalar(q_s[:, 0:8], q_am[:, 0:8],
                                          float(1.0 / 448.0), None, ALU.mult))
            seg448 = [q_rc448[:, g:g + 1] for g in range(8)]
            segS = [q_s[:, g:g + 1] for g in range(8)]
            qn = quant_chain(arenaB_f[:, 0:1024], arenaB_i[:, 0:1024], seg448,
                             segS, wbf[0][:, :])
            xq_srcfree[0] = qn["r32"]
            for g in range(8):
                c = 8 * hf + g
                bk = pb[5 + g % 2]
                kb.wait("t", cV, qn["out"][g // 4])
                kb.wait("t", cS, xq_bk[g % 2])
                mm(bk[:, 0:128],
                   wbf[0][:, 128 * g:128 * (g + 1)],
                   identb_sb[:, :], start=True, stop=True, tick=True)
                kb.wait("s", cT, cT.n)
                sop(lambda e, c_=c, t=tt, co_=co, b=bk: e.copy(
                    xB[:, 1024 * c_ + 512 * t + co_:
                       1024 * c_ + 512 * t + co_ + 128], b[:, 0:128]))
                xq_bk[g % 2] = cS.n

        agw_n = [0]

        def wquant_gather():
            kb.wait("g", cO, cO.n)
            kb.tick("g", cCC, lambda e: e.collective_compute(
                "AllGather", ALU.bypass,
                replica_groups=[[0, 1, 2, 3, 4, 5, 6, 7]],
                ins=[wdq_loc], outs=[wdqG]))
            agw_n[0] = cCC.n

        def ffn1_phase(agw, lnt=None, bankfree=None):
            wcons = {}
            kb.wait("g", cCC, agw)
            hf_tick = {}
            entry_t = cT.n
            for hf in (1, 0):
                for tb in range(4):
                    xquant_block(tb, hf, lnw=lnt)
                hf_tick[(0, hf)] = cS.n
            for tt in range(2):
                kb.wait("t", cV, cV.n)
                bf0 = bankfree if (tt == 0 and bankfree is not None) else cV.n
                ev = {0: (cV, bf0), 1: (cV, bf0)}
                for oh in range(NCH):
                    li = tt * NCH + oh
                    kb.wait("g", cT,
                            wcons[li - WR] if li - WR in wcons else entry_t)
                    gdma(cW[li % WR], wb[li % WR][:, :],
                         wdqG[128 * oh:128 * (oh + 1), :])
                    ldn = cW[li % WR].n
                    bank = pb[tt]
                    kb.wait("t", cW[li % WR], ldn)
                    ch_, n_ = ev[tt]
                    kb.wait("t", ch_, n_)
                    ecs = list(range(8, 16)) + list(range(8))
                    for pos, ec in enumerate(ecs):
                        if pos % 8 == 0:
                            kb.wait("t", cS, hf_tick[(tt, ec // 8)])
                        mm(bank[:, :], wb[li % WR][:, 128 * ec:128 * (ec + 1)],
                           xB[:, 1024 * ec + 512 * tt:1024 * ec + 512 * (tt + 1)],
                           start=(pos == 0), stop=(pos == NCH - 1),
                           tick=(pos == NCH - 1))
                    wcons[li] = cT.n
                    kb.wait("s", cT, cT.n)
                    sop(lambda e, b=bank: e.copy(wbf[1][:, 0:512], b[:, :]))
                    kb.wait("v", cS, cS.n)
                    vop(lambda e, o=oh, t=tt: e.tensor_add(
                        cur(o, t), arenaA_f[:, 1024 * o + 512 * t:
                                            1024 * o + 512 * (t + 1)],
                        wbf[1][:, 0:512]))
                    ev[tt] = (cV, cV.n)
                    if tt == 0 and oh in (1, 3):
                        hf = 1 if oh == 1 else 0
                        for tb in range(4, 8):
                            xquant_block(tb, hf)
                        hf_tick[(1, hf)] = cS.n

        def final_ln():
            kb.wait("t", cV, cV.n)
            kb.wait("s", cV, cV.n)
            prev_tick = cT.n
            for c in range(NCH):
                for tt in range(2):
                    kb.wait("s", cT, prev_tick)
                    sop(lambda e, c_=c, t=tt: e.activation(
                        wbf[0][:, 512 * t:512 * (t + 1)], cur(c_, t), AF.Square))
                    sqn = cS.n
                    mm(pb[tt][:, :], ones128_sb[:, :], cur(c, tt),
                       start=(c == 0), stop=(c == NCH - 1))
                    kb.wait("t", cS, sqn)
                    mm(pb[2 + tt][:, :], ones128_sb[:, :],
                       wbf[0][:, 512 * tt:512 * (tt + 1)],
                       start=(c == 0), stop=(c == NCH - 1), tick=True)
                    prev_tick = cT.n
            stn = cT.n
            for tt in range(2):
                nr = wk[3] if tt == 0 else wk[5]
                kb.wait("v", cT, stn)
                vop(lambda e, t=tt: e.tensor_scalar(
                    wk[0][:, :], pb[2 + t][:, :], RDIV, None, ALU.mult))
                vop(lambda e, t=tt: e.tensor_scalar(
                    wk[2][:, :], pb[t][:, :], RDIV, None, ALU.mult))
                kb.wait("s", cV, cV.n)
                sop(lambda e: e.activation(wk[1][:, :], wk[2][:, :], AF.Square))
                kb.wait("v", cS, cS.n)
                vop(lambda e: e.tensor_sub(wk[0][:, :], wk[0][:, :], wk[1][:, :]))
                kb.wait("s", cV, cV.n)
                sop(lambda e: e.activation(wk[1][:, :], wk[0][:, :], AF.Ln,
                                           bias=epsT[:, :]))
                sop(lambda e: e.activation(wk[1][:, :], wk[1][:, :], AF.Exp,
                                           scale=-0.5))
                kb.wait("v", cS, cS.n)
                vop(lambda e, o=nr: e.tensor_scalar(
                    o[:, :], wk[1][:, :], -1.0, None, ALU.mult))
                for c in range(NCH):
                    buf = wk[2] if c % 2 == 0 else wk[4]
                    vop(lambda e, c_=c, t=tt: e.scalar_tensor_tensor(
                        wk[1][:, :], pb[t][:, :], RDIV, cur(c_, t),
                        ALU.mult, ALU.subtract))
                    kb.wait("v", cO, cO.n)
                    vop(lambda e, b=buf, o=nr: e.tensor_mul(
                        b[:, :], wk[1][:, :], o[:, :]))
                    kb.wait("y", cV, cV.n)
                    ydma(cO, y[128 * c:128 * (c + 1), 512 * tt:512 * (tt + 1)],
                         buf[:, :])
            kb.wait("y", cO, cO.n)

        if stage == "ld":
            emit_out_stream(cur)
        elif stage in ("kv", "kvt", "kvnc"):
            kv_phase(0, do_vt=(stage == "kvt"), do_cc=(stage != "kvnc"))
            emit_out_stream(cur)
        else:
            _, agk0, vtn0, kvt0 = kv_phase(0)
            AO0 = attn_phase(0, agk0, vtn0, kvt0)
            if stage == "attn":
                emit_out_stream(lambda c, t: AO0[:, 1024 * c + 512 * t:
                                                 1024 * c + 512 * (t + 1)])
            else:
                wo_phase(0, AO0)
                lnt1, bf1 = ln_phase(
                    cur, mrow=lambda t: arenaB_f[:, 1024 + 512 * t:
                                                 1536 + 512 * t])
                if stage == "l0a":
                    emit_out_stream(cur)
                else:
                    moe_pre(lnt=lnt1)
                    ffn_phase(lnt=lnt1, bankfree=bf1, sides={
                        32: [moe_counts],
                        36: [lambda: wquant_passA(0)],
                        40: [lambda: wquant_passA(1)],
                        44: [wquant_prep],
                        46: [lambda: wquant_passB(0, (0,))],
                        50: [lambda: wquant_passB(0, (1,))],
                        54: [lambda: wquant_passB(1, (0,))],
                        58: [lambda: wquant_passB(1, (1,))],
                        62: [wquant_gather],
                    })
                    lnt2, bf2 = ln_phase(
                        cur, src=aox,
                        mrow=lambda t: gb_sb[:, 512 * t:512 * (t + 1)])
                    if stage == "l0f":
                        emit_out_stream(cur)
                    else:
                        _, agk1, vtn1, kvt1 = kv_phase(1, lnt=lnt2,
                                                       bankfree=bf2)
                        AO1 = attn_phase(1, agk1, vtn1, kvt1)
                        agw = agw_n[0]
                        wo_phase(1, AO1)
                        x3f = lambda c, t: arenaA_f[:, 1024 * c + 512 * t:
                                                    1024 * c + 512 * (t + 1)]
                        lnt3, bf3 = ln_phase(
                            x3f, src=rr, desc=True,
                            mrow=lambda t: gb_sb[:, 512 * t:512 * (t + 1)])
                        if stage == "l1a":
                            emit_out_stream(x3f)
                        elif stage == "xd":
                            xq_bk[0] = cS.n
                            xq_bk[1] = cS.n
                            xq_srcfree[0] = bf3
                            for tb in range(8):
                                for hf in (1, 0):
                                    xquant_block(tb, hf, lnw=lnt3)
                            emit_out_stream(lambda c, t: xB[:, 1024 * c + 512 * t:
                                                            1024 * c + 512 * (t + 1)])
                        else:
                            xq_bk[0] = cS.n
                            xq_bk[1] = cS.n
                            xq_srcfree[0] = bf3
                            ffn1_phase(agw, lnt=lnt3, bankfree=bf3)
                            final_ln()

        with nc.Block() as block:
            block.tensor(lambda e: [f(e) for f in kb.q["t"]])
            block.vector(lambda e: [f(e) for f in kb.q["v"]])
            block.scalar(lambda e: [f(e) for f in kb.q["s"]])
            block.gpsimd(lambda e: [f(e) for f in kb.q["g"]])
            block.sync(lambda e: [f(e) for f in kb.q["y"]])

    return nc



def _tile_lhsT(Wct, fh_chunks):
    C, F = Wct.shape
    A = Wct.reshape(C // 128, 128, F // 128, 128).transpose(2, 1, 0, 3)
    return np.ascontiguousarray(A.reshape(F // 128, 128, C)).astype(np.float32)


def _bias_cols(b):
    return np.ascontiguousarray(b.reshape(-1, 128).T).astype(np.float32)


def make_core_inputs(inp, c):
    b, p = c // 2, c % 2
    cols = np.concatenate([np.arange(128 * (2 * lb + p), 128 * (2 * lb + p) + 128)
                           for lb in range(8)])
    x = np.asarray(inp["L0_attn_x"], np.float32)[b][cols]
    d = {}
    d["x0"] = np.ascontiguousarray(x.T)
    d["wq0"] = _tile_lhsT(np.asarray(inp["L0_attn_Wq"], np.float32).T, H)
    d["wk0"] = _tile_lhsT(np.asarray(inp["L0_attn_Wk"], np.float32).T, H)
    d["wv0"] = _tile_lhsT(np.asarray(inp["L0_attn_Wv"], np.float32).T, H)
    d["wo0"] = _tile_lhsT(np.asarray(inp["L0_attn_Wo"], np.float32).T, NCH)
    d["bq_c"] = _bias_cols(np.asarray(inp["L0_attn_bq"], np.float32))
    d["bk_c"] = _bias_cols(np.asarray(inp["L0_attn_bk"], np.float32))
    d["bv_c"] = _bias_cols(np.asarray(inp["L0_attn_bv"], np.float32))
    d["bo_c"] = _bias_cols(np.asarray(inp["L0_attn_bo"], np.float32))
    d["b1_c"] = _bias_cols(np.asarray(inp["L0_ffn_b1"], np.float32))
    d["b2_c"] = _bias_cols(np.asarray(inp["L0_ffn_b2"], np.float32))
    rwt = np.asarray(inp["L0_ffn_router_w"], np.float32).T
    d["rw"] = np.ascontiguousarray(
        rwt.reshape(NCH, 128, 8).transpose(1, 0, 2).reshape(128, NCH * 8)).astype(np.float32)
    d["w1"] = _tile_lhsT(np.asarray(inp["L0_ffn_w1"], np.float32).T, 64)
    d["w2"] = _tile_lhsT(np.asarray(inp["L0_ffn_w2"], np.float32).T, NCH)
    d["wq1"] = _tile_lhsT(np.asarray(inp["L1_attn_Wq"], np.float32), H)
    d["wk1"] = _tile_lhsT(np.asarray(inp["L1_attn_Wk"], np.float32), H)
    d["wv1"] = _tile_lhsT(np.asarray(inp["L1_attn_Wv"], np.float32), H)
    d["wo1"] = _tile_lhsT(np.asarray(inp["L1_attn_Wo"], np.float32), H)
    _wf1_all = _tile_lhsT(np.asarray(inp["L1_ffn_weight"], np.float32).T, NCH)
    d["wf1"] = np.ascontiguousarray(_wf1_all[[2 * c, 2 * c + 1]])

    pos = (cols).astype(np.float64)
    theta = np.repeat(-np.arange(0, 128, 2, dtype=np.float32) / 128.0, 2)
    theta = np.power(np.float32(10000.0), theta.astype(np.float32)).astype(np.float64)
    ang = pos[None, :] * 0 + 0.0
    ang = theta[:, None] * pos[None, :]
    d["ropec"] = np.cos(ang).astype(ml_dtypes.bfloat16)
    d["ropes"] = np.sin(ang).astype(ml_dtypes.bfloat16)
    P = np.zeros((128, 128), np.float32)
    for i2 in range(0, 128, 2):
        P[i2, i2 + 1] = -1.0
        P[i2 + 1, i2] = 1.0
    d["p128"] = np.ascontiguousarray(P.T).astype(ml_dtypes.bfloat16)
    d["identb"] = np.eye(128, dtype=np.float32).astype(ml_dtypes.bfloat16)
    d["identf"] = np.eye(128, dtype=np.float32)
    d["ones128"] = np.ones((128, 128), ml_dtypes.bfloat16)
    d["onesrow"] = np.ones((1, 128), ml_dtypes.bfloat16)
    d["onesrow_f"] = np.ones((1, 128), np.float32)
    d["u128"] = np.triu(np.ones((128, 128), np.float32)).astype(ml_dtypes.bfloat16)
    d["c1024"] = np.full((1, 128), 1024.0, ml_dtypes.bfloat16)

    tri_strict = np.tril(np.ones((128, 128), np.float32), -1)
    m0 = np.zeros((16, 128, 128), np.float32)
    m1 = np.zeros((16, 128, 128), np.float32)
    for k in range(16):
        j = 0 if k < 8 else 1
        i_min = max(0, int(math.ceil((k - 8 * j - 1) / 2.0)))
        qa = 8 * j + 2 * i_min + p
        if qa == k:
            m0[k] = tri_strict * -30000.0
            m1[k] = np.triu(np.ones((128, 128), np.float32), 0)
        elif qa < k:
            m0[k] = -30000.0
            m1[k] = 0.0
        else:
            m0[k] = 0.0
            m1[k] = 1.0
    d["m0t"] = np.ascontiguousarray(
        m0.transpose(1, 0, 2).reshape(128, 16 * 128)).astype(ml_dtypes.bfloat16)
    d["m1t"] = np.ascontiguousarray(
        m1.transpose(1, 0, 2).reshape(128, 16 * 128)).astype(ml_dtypes.bfloat16)

    myabs = [(b * 16 + 2 * lb + p) for lb in range(8)]
    Mb = np.zeros((128, 16), np.float32)
    for r in range(8):
        a_r = myabs[r]
        for core in range(8):
            for blk in range(8):
                a = (core // 2) * 16 + 2 * blk + (core % 2)
                if a < a_r:
                    Mb[core * 8 + blk, r] = 1.0
                    Mb[64 + core * 8 + blk, 8 + r] = 1.0
        Mb[0:64, 8 + r] = 1.0
    d["mbig"] = Mb.astype(ml_dtypes.bfloat16)
    return d


def kernel(**inputs):
    nc = build(stage="full")
    in_maps = [make_core_inputs(inputs, c) for c in range(8)]
    res = run_bass_kernel_spmd(nc, in_maps, core_ids=list(range(8)))
    out = np.zeros((B, S, E), np.float32)
    for c in range(8):
        b, p = c // 2, c % 2
        yt = res.results[c]["y"]
        xt = np.ascontiguousarray(yt.T)
        for lb in range(8):
            out[b, 128 * (2 * lb + p):128 * (2 * lb + p) + 128, :] = \
                xt[128 * lb:128 * (lb + 1)]
    return out
